# version 11
# baseline (speedup 1.0000x reference)
import numpy as np
import ml_dtypes
import concourse.bass as bass
import concourse.mybir as mybir
from concourse.bass_utils import run_bass_kernel_spmd

F32 = mybir.dt.float32
BF16 = mybir.dt.bfloat16
AF = mybir.ActivationFunctionType
ALU = mybir.AluOpType

D = 1024
DC = 8
H = 4096
HC = 32
T = 512
NT = 8
TPS = 4
NH = 16
DR = 256
RC = 2
NCORE = 8

EPS_LN = 1e-5
EPS_RMS = float(np.finfo(np.float32).eps)
NSLOT = 4
SLOTE = 4096

ENGINES = ("sync", "gpsimd", "vector", "scalar", "tensor")

WSPEC = {
    "w1": (8, 1, 4096), "w2": (8, 1, 4096),
    "h1": (8, 4, 4096), "h2": (8, 4, 4096),
    "w3": (2, 4, 4096),
    "qw1": (1, 1, 2048), "qw2": (24, 1, 512),
    "cw1": (1, 1, 4096), "cw2": (4, 1, 1024),
}

C_HALF, C_NEG1, C_ZERO, C_FRQ, C_I2PI, C_MAGIC, C_N2PI, C_PIH, \
    C_EPSLN, C_NHALF, C_I1024, C_I128, C_EPSRMS, C_PI, C_NPI = range(15)
CONSTV = {C_HALF: 0.5, C_NEG1: -1.0, C_ZERO: 0.0,
          C_FRQ: -float(np.log(1024.0)) / 31.0,
          C_I2PI: float(1.0 / (2 * np.pi)), C_MAGIC: 12582912.0,
          C_N2PI: float(-2 * np.pi), C_PIH: float(np.pi / 2),
          C_EPSLN: EPS_LN, C_NHALF: -0.5, C_I1024: 1.0 / 1024.0,
          C_I128: 1.0 / 128.0, C_EPSRMS: EPS_RMS,
          C_PI: float(np.pi), C_NPI: float(-np.pi)}


class Sched:
    def __init__(self, nc):
        self.nc = nc
        self.q = {e: [] for e in ENGINES}
        self.sems = {}
        self.cnt = {}

    def sem(self, name):
        self.sems[name] = self.nc.alloc_semaphore(name)
        self.cnt[name] = 0
        return name

    def op(self, eng, fn, waits=(), inc=None, inc_by=1):
        waits = [(s, v) for (s, v) in waits if v and v > 0]
        mark = None
        if inc is not None:
            self.cnt[inc] += inc_by
            mark = self.cnt[inc]
        self.q[eng].append((waits, fn, inc, inc_by))
        return mark

    def drain(self, eng):
        self.op(eng, lambda e: e.drain())

    def emit(self):
        nc = self.nc
        with nc.Block() as block:
            for ename in ENGINES:
                def make_body(name):
                    def body(eng):
                        for waits, fn, inc, n in self.q[name]:
                            for (s, v) in waits:
                                eng.wait_ge(self.sems[s], v)
                            ins = fn(eng)
                            if inc is not None:
                                ins.then_inc(self.sems[inc], n)
                    return body
                getattr(block, ename)(make_body(ename))


def build_program(nt=NT):
    nc = bass.Bass("TRN2", target_bir_lowering=False)

    x_in = nc.dram_tensor("x_in", [nt, 128, DC * T], F32, kind="ExternalInput")
    o_out = nc.dram_tensor("o_out", [nt, 128, DC * T], F32,
                           kind="ExternalOutput")
    w_f32 = {k: nc.dram_tensor(f"{k}_f32", [ng, kq, 128, ch], F32,
                               kind="ExternalInput")
             for k, (ng, kq, ch) in WSPEC.items()}
    w_scr = {k: nc.dram_tensor(f"{k}_scr", [ng, kq, 128, ch], BF16)
             for k, (ng, kq, ch) in WSPEC.items()}
    sc_in = {k: nc.dram_tensor(k, [128, n], F32, kind="ExternalInput")
             for k, n in (("s1c", 32), ("s2c", 32), ("sh1c", 32),
                          ("sh2c", 32), ("s3c", 8))}
    sel16_in = nc.dram_tensor("sel16", [16, 2048], BF16, kind="ExternalInput")
    colsel_in = nc.dram_tensor("colsel", [128, 256], BF16, kind="ExternalInput")

    SB = nc.alloc_sbuf_tensor
    x_res = [SB(f"x{i}", [128, DC * T], F32) for i in range(2)]
    ring = SB("ring", [128, NSLOT * SLOTE], BF16)
    slot = [ring[:, i * SLOTE:(i + 1) * SLOTE] for i in range(NSLOT)]
    A = SB("arnA", [128, HC * T], BF16)
    B = SB("arnB", [128, HC * T], BF16)
    C = SB("arnC", [128, NH * T], BF16)
    lnb = SB("lnb", [128, DC * T], BF16)
    tmp1 = SB("tmp1", [128, 2 * 4 * T], BF16)
    hq = SB("hq", [128, RC * T], BF16)
    rtile = SB("rtile", [128, RC * T], BF16)
    sqs = SB("sqs", [128, 2 * T], BF16)
    fscr = SB("fscr", [128, 2 * T], F32)
    rowscr = SB("rowscr", [128, 3 * T], F32)
    abrow = SB("abrow", [128, 2 * T], BF16)
    bcb = SB("bcb", [128, 2 * T], BF16)
    rstd16 = SB("rstd16", [16, T], BF16)
    rstd16k = SB("rstd16k", [16, T], BF16)
    cos32 = SB("cos32", [32, 2048], F32)
    sin32 = SB("sin32", [32, 2048], F32)
    Ct = SB("Ct", [128, T], BF16)
    St = SB("St", [128, T], BF16)
    onesT = SB("onesT", [128, T], BF16)
    ones_col = SB("ones_col", [128, 1], BF16)
    ones_row = SB("ones_row", [1, 128], BF16)
    sel16 = SB("sel16t", [16, 2048], BF16)
    colsel = SB("colselt", [128, 256], BF16)
    scl = {k: SB(f"scl_{k}", [128, 32], F32) for k in ("s1", "s2", "sh1", "sh2")}
    scl3 = SB("scl_s3", [128, 8], F32)
    s3h = SB("s3h", [128, 8], F32)
    carry = SB("carry", [128, NH], F32)
    consts = SB("consts", [128, 16], F32)

    qb = A[:, 0:NH * T]
    kb = A[:, NH * T:2 * NH * T]
    swp = A[:, 0:NH * T]
    vb = B[:, 0:NH * T]
    qh = B[:, NH * T:2 * NH * T]
    kh = C[:, 0:NH * T]
    cums = B[:, 0:NH * T]
    xb8 = C[:, 0:DC * T]
    hin = C[:, DC * T:2 * DC * T]
    tabf = B[0:32, 0:4096].bitcast(F32)

    psH = [nc.alloc_psum_tensor("psA", [128, 4 * T], F32),
           nc.alloc_psum_tensor("psB", [128, 4 * T], F32)]

    S = Sched(nc)
    cF = [S.sem(f"cF{i}") for i in range(NSLOT)]
    cX = [S.sem(f"cX{i}") for i in range(2)]
    cL = S.sem("cL")
    cO = S.sem("cO")
    cP = S.sem("cP")
    cSW = S.sem("cSW")
    cPE = S.sem("cPE")
    cSC = S.sem("cSC")
    cVE = S.sem("cVE")
    cPO = S.sem("cPO")

    bk = {
        "slot_free": [0] * NSLOT,
        "slot_fill": [0] * NSLOT,
        "slot_uses": [0] * NSLOT,
        "half_free": [(None, 0), (None, 0)],
        "x_out": [0, 0],
        "x_fill": [0, 0],
        "prep": {},
        "ci": 0,
        "rope_done": 0,
    }

    def cc(idx, p=128):
        return consts[0:p, idx:idx + 1]

    def fill(layer, ng, kq, elems, tile0):
        s = bk["ci"] % NSLOT
        bk["ci"] += 1
        waits = []
        if bk["slot_free"][s] > 0:
            waits.append((cPE, bk["slot_free"][s]))
        if tile0:
            waits.append((cP, bk["prep"][layer]))
        src = w_scr[layer][ng, kq]
        dst = slot[s] if elems == SLOTE else slot[s][:, 0:elems]
        m = S.op("sync", lambda e, d=dst, sr=src: e.dma_start(out=d, in_=sr),
                 waits=waits, inc=cF[s], inc_by=16)
        bk["slot_fill"][s] = m
        return s

    def half_wait(h):
        sem, m = bk["half_free"][h]
        return [(sem, m)] if (sem is not None and m > 0) else []

    def mm(out_ap, lhsT_ap, rhs_ap, start, stop, waits=(), inc=None):
        return S.op("tensor",
                    lambda e, o=out_ap, l=lhsT_ap, r=rhs_ap, st=start, sp=stop:
                    e.matmul(o, l, r, start=st, stop=sp),
                    waits=waits, inc=inc)

    def emit_xload(t):
        xi = t % 2
        waits = [(cO, bk["x_out"][xi])] if bk["x_out"][xi] > 0 else []
        m = S.op("gpsimd", lambda e, ti=t, b=xi: e.dma_start(
            out=x_res[b][:, :], in_=x_in[ti]),
            waits=waits, inc=cX[xi], inc_by=16)
        bk["x_fill"][xi] = m

    def emit_ln(t, mode, xready=None):
        x = x_res[t % 2]
        xw = t // 2 + 1
        hf = bk.get("hin_free", 0)
        if xready is None:
            xready = [(cX[t % 2], 16 * xw)] * 2
        m_cast = None
        m_sq = None
        for hhalf in range(2):
            r0 = hhalf * 4 * T
            S.drain("vector")
            m_cast = S.op("vector", lambda e, r_=r0: e.tensor_copy(
                xb8[:, r_:r_ + 4 * T], x[:, r_:r_ + 4 * T]),
                waits=[xready[hhalf]], inc=cVE)
            S.drain("vector")
            m_sq = S.op("vector", lambda e, r_=r0: e.tensor_tensor(
                hin[:, r_:r_ + 4 * T], xb8[:, r_:r_ + 4 * T],
                xb8[:, r_:r_ + 4 * T], ALU.mult),
                waits=([(cPE, hf)] if (hhalf == 0 and hf > 0) else []),
                inc=cVE)
        w0 = half_wait(0)
        sq_marks = []
        for k in range(DC):
            mm(psH[0][0:1, 0:T], ones_col[:, 0:1], xb8[:, k * T:(k + 1) * T],
               start=(k == 0), stop=(k == DC - 1),
               waits=(w0 + [(cVE, m_cast - 2)]) if k == 0 else
                     ([(cVE, m_cast)] if k == 4 else []))
            msq = mm(psH[0][0:1, T:2 * T], ones_col[:, 0:1],
                     hin[:, k * T:(k + 1) * T],
                     start=(k == 0), stop=(k == DC - 1),
                     waits=[(cVE, m_sq - 2)] if k == 0 else
                           ([(cVE, m_sq)] if k == 4 else []),
                     inc=(cPE if k == DC - 1 else None))
            sq_marks.append(msq)
        bk["hin_free"] = sq_marks[-1]
        r_m = rowscr[0:1, 0:T]
        r_v = rowscr[0:1, T:2 * T]
        r_r = rowscr[0:1, 2 * T:3 * T]
        S.drain("vector")
        S.op("vector", lambda e: e.tensor_scalar(
            r_m, psH[0][0:1, 0:T], 1.0 / D, None, ALU.mult),
            waits=[(cPE, bk["hin_free"])])
        S.drain("vector")
        S.op("vector", lambda e: e.tensor_tensor(r_v, r_m, r_m, ALU.mult))
        S.drain("vector")
        m_var = S.op("vector", lambda e: e.scalar_tensor_tensor(
            r_v, psH[0][0:1, T:2 * T], 1.0 / D, r_v, ALU.mult, ALU.subtract),
            inc=cVE)
        S.op("scalar", lambda e: e.activation(r_r, r_v, AF.Ln,
                                              bias=cc(C_EPSLN, 1)),
             waits=[(cVE, m_var)])
        S.drain("scalar")
        m_ex = S.op("scalar", lambda e: e.activation(
            r_r, r_r, AF.Exp, scale=cc(C_NHALF, 1)), inc=cSC)
        S.drain("vector")
        S.op("vector", lambda e: e.scalar_tensor_tensor(
            rowscr[0:1, T:2 * T], r_m, -1.0, r_r, ALU.mult, ALU.mult),
            waits=[(cSC, m_ex)])
        S.drain("vector")
        S.op("vector", lambda e: e.tensor_copy(abrow[0:1, 0:T], r_r))
        S.drain("vector")
        m_ab = S.op("vector", lambda e: e.tensor_copy(
            abrow[0:1, T:2 * T], rowscr[0:1, T:2 * T]), inc=cVE)
        w1h = half_wait(1)
        mm(psH[1][:, 0:T], ones_row[0:1, :], abrow[0:1, 0:T], True, True,
           waits=w1h + [(cVE, m_ab)])
        m_bc = mm(psH[1][:, T:2 * T], ones_row[0:1, :], abrow[0:1, T:2 * T],
                  True, True, inc=cPE)
        m_ev = S.op("scalar", lambda e: e.activation(
            bcb[:, 0:2 * T], psH[1][:, 0:2 * T], AF.Copy),
            waits=[(cPE, m_bc)], inc=cSC)
        bk["half_free"][1] = (cSC, m_ev)
        hmarks = []
        for hh2 in range(2):
            r0 = hh2 * 4
            xv = xb8.rearrange("p (c tt) -> p c tt", c=DC)[:, r0:r0 + 4, :]
            lv = lnb.rearrange("p (c tt) -> p c tt", c=DC)[:, r0:r0 + 4, :]
            av = bcb[:, 0:T][:, None, :].broadcast_to([128, 4, T])
            bv = bcb[:, T:2 * T][:, None, :].broadcast_to([128, 4, T])
            S.drain("vector")
            S.op("vector", lambda e, l_=lv, x_=xv, a_=av: e.tensor_tensor(
                l_, x_, a_, ALU.mult),
                waits=[(cSC, m_ev)] if hh2 == 0 else [])
            S.drain("vector")
            m = S.op("vector", lambda e, l_=lv, b_=bv: e.tensor_tensor(
                l_, l_, b_, ALU.add), inc=cVE)
            hmarks.append(m)
        last_m = hmarks[-1]
        bk["half_free"][0] = (cVE, m_var)
        bk["lnb_hmarks"] = hmarks
        return last_m

    def mm_group(lay, g, h, kq_n, ch, rhs_buf, rhs_sem, rhs_marks, tile0,
                 extra_waits=()):
        nq = 8
        mbw = ch // nq // 128
        grp_mark = None
        for kq in range(kq_n):
            s = fill(lay, g, kq, ch, tile0)
            waits = [(cF[s], bk["slot_fill"][s])] + list(extra_waits if kq == 0
                                                         else [])
            if kq == 0:
                waits += half_wait(h)
            waited = 0
            for j in range(nq):
                k = kq * nq + j
                if rhs_marks is not None:
                    mkk = rhs_marks[min(k, len(rhs_marks) - 1)]
                    if mkk > waited:
                        waits.append((rhs_sem, mkk))
                        waited = mkk
                rhs = rhs_buf[:, k * T:(k + 1) * T]
                for ml in range(mbw):
                    is_chunk_last = (j == nq - 1 and ml == mbw - 1)
                    lhsT = slot[s][:, j * mbw * 128 + ml * 128:
                                   j * mbw * 128 + (ml + 1) * 128]
                    mk = mm(psH[h][:, ml * T:(ml + 1) * T], lhsT, rhs,
                            start=(kq == 0 and j == 0),
                            stop=(kq == kq_n - 1 and j == nq - 1),
                            waits=waits,
                            inc=(cPE if is_chunk_last else None))
                    waits = []
            bk["slot_free"][s] = mk
            grp_mark = mk
        return grp_mark

    def ffn_w1w2(t, tile0, hm):
        lnmarks = [hm[0]] * 4 + [hm[1]] * 4
        out_marks = []
        silu_mark = {}
        for q0 in range(0, 8, 2):
            for g in (q0, q0 + 1):
                h = g % 2
                gm = mm_group("w1", g, h, 1, 4096, lnb, cVE, lnmarks,
                              tile0)
                lastc = None
                for ml in range(4):
                    mch = g * 4 + ml
                    w_t = []
                    if ml == 0:
                        w_t = [(cPE, gm)]
                        tf = bk.get(f"tmp1_free{h}", 0)
                        if tf > 0:
                            w_t.append((cVE, tf))
                    lastc = S.op("scalar", lambda e, h_=h, ml_=ml, m_=mch:
                                 e.activation(
                        tmp1[:, (h_ * 4 + ml_) * T:(h_ * 4 + ml_ + 1) * T],
                        psH[h_][:, ml_ * T:(ml_ + 1) * T], AF.Silu,
                        scale=scl["s1"][:, m_:m_ + 1]),
                        waits=w_t, inc=(cSC if ml == 3 else None))
                silu_mark[g] = lastc
                bk["half_free"][h] = (cSC, lastc)
            for g in (q0, q0 + 1):
                h = g % 2
                gm = mm_group("w2", g, h, 1, 4096, lnb, cVE, lnmarks,
                              tile0)
                lastc = None
                for ml in range(4):
                    mch = g * 4 + ml
                    if ml == 0:
                        S.drain("vector")
                    lastc = S.op("vector", lambda e, h_=h, ml_=ml, m_=mch:
                                 e.scalar_tensor_tensor(
                        A[:, m_ * T:(m_ + 1) * T],
                        psH[h_][:, ml_ * T:(ml_ + 1) * T],
                        scl["s2"][:, m_:m_ + 1],
                        tmp1[:, (h_ * 4 + ml_) * T:(h_ * 4 + ml_ + 1) * T],
                        ALU.mult, ALU.mult),
                        waits=([(cPE, gm), (cSC, silu_mark[g])]
                               if ml == 0 else []),
                        inc=(cVE if ml == 3 else None))
                out_marks += [lastc] * 4
                bk["half_free"][h] = (cVE, lastc)
                bk[f"tmp1_free{h}"] = lastc
        return out_marks

    def ffn_layer(t, layer, rhs_buf, out_kind, out_buf, scale_sb,
                  rhs_sem, rhs_marks, act, tile0):
        ng, kq_n, ch = WSPEC[layer]
        mbw = ch // 8 // 128
        out_marks = []
        for g in range(ng):
            h = g % 2
            grp_mark = mm_group(layer, g, h, kq_n, ch, rhs_buf, rhs_sem,
                                rhs_marks, tile0)
            if out_kind == "gelu":
                lastc = None
                for ml in range(mbw):
                    mch = g * mbw + ml
                    lastc = S.op("scalar", lambda e, h_=h, ml_=ml, m_=mch:
                                 e.activation(
                        out_buf[:, m_ * T:(m_ + 1) * T],
                        psH[h_][:, ml_ * T:(ml_ + 1) * T], act,
                        scale=scale_sb[:, m_:m_ + 1]),
                        waits=[(cPE, grp_mark)] if ml == 0 else [],
                        inc=(cSC if ml == mbw - 1 else None))
                out_marks += [lastc] * mbw
                bk["half_free"][h] = (cSC, lastc)
            elif out_kind == "resid":
                lastc = None
                for ml in range(mbw):
                    mch = g * mbw + ml
                    if ml == 0:
                        S.drain("vector")
                    lastc = S.op("vector", lambda e, h_=h, ml_=ml, m_=mch:
                                 e.scalar_tensor_tensor(
                        x_res[t % 2][:, m_ * T:(m_ + 1) * T],
                        psH[h_][:, ml_ * T:(ml_ + 1) * T],
                        s3h[:, m_:m_ + 1],
                        x_res[t % 2][:, m_ * T:(m_ + 1) * T],
                        ALU.mult, ALU.add),
                        waits=[(cPE, grp_mark)] if ml == 0 else [],
                        inc=(cVE if ml == mbw - 1 else None))
                    out_marks.append(lastc)
                bk["half_free"][h] = (cVE, lastc)
        return out_marks

    def emit_ffn(t, tile0, lnb_mark):
        hm = bk.get("lnb_hmarks", [lnb_mark, lnb_mark])
        a_marks = ffn_w1w2(t, tile0, hm)
        b_marks = ffn_layer(t, "h1", A, "gelu", B, scl["sh1"],
                            cVE, a_marks, AF.Gelu, tile0)
        a2_marks = ffn_layer(t, "h2", B, "gelu", A, scl["sh2"],
                             cSC, b_marks, AF.Gelu, tile0)
        r_marks = ffn_layer(t, "w3", A, "resid", None, None,
                            cSC, a2_marks, None, tile0)
        return r_marks

    def emit_rms_qkv1(t, tile0, xlnb_mark):
        w0 = half_wait(0)
        hf = bk.get("hin_free", 0)
        S.drain("vector")
        m_sq = S.op("vector", lambda e: e.tensor_tensor(
            hin[:, :], lnb[:, :], lnb[:, :], ALU.mult),
            waits=([(cVE, xlnb_mark)] + ([(cPE, hf)] if hf > 0 else [])),
            inc=cVE)
        marks = []
        for k in range(DC):
            msq = mm(psH[0][0:1, 0:T], ones_col[:, 0:1],
                     hin[:, k * T:(k + 1) * T],
                     start=(k == 0), stop=(k == DC - 1),
                     waits=(w0 + [(cVE, m_sq)]) if k == 0 else [],
                     inc=(cPE if k == DC - 1 else None))
            marks.append(msq)
        bk["hin_free"] = marks[-1]
        r_r = rowscr[0:1, 0:T]
        S.op("scalar", lambda e: e.activation(
            r_r, psH[0][0:1, 0:T], AF.Ln, scale=cc(C_I1024, 1),
            bias=cc(C_EPSRMS, 1)), waits=[(cPE, marks[-1])])
        S.drain("scalar")
        m_ex = S.op("scalar", lambda e: e.activation(
            r_r, r_r, AF.Exp, scale=cc(C_NHALF, 1)), inc=cSC)
        S.drain("vector")
        m_ab = S.op("vector", lambda e: e.tensor_copy(abrow[0:1, 0:T], r_r),
                    waits=[(cSC, m_ex)], inc=cVE)
        w1h = half_wait(1)
        m_bc = mm(psH[1][:, 0:T], ones_row[0:1, :], abrow[0:1, 0:T], True,
                  True, waits=w1h + [(cVE, m_ab)], inc=cPE)
        bk["half_free"][0] = (cSC, m_ex)
        m_evr = S.op("scalar", lambda e: e.activation(
            bcb[:, 0:T], psH[1][:, 0:T], AF.Copy),
            waits=[(cPE, m_bc)], inc=cSC)
        bk["half_free"][1] = (cSC, m_evr)
        rv_ = bcb[:, 0:T][:, None, :].broadcast_to([128, DC, T])
        S.drain("vector")
        m_hin = S.op("vector", lambda e: e.tensor_tensor(
            hin.rearrange("p (c tt) -> p c tt", c=DC),
            lnb.rearrange("p (c tt) -> p c tt", c=DC), rv_, ALU.mult),
            waits=[(cSC, m_evr), (cPE, bk["hin_free"])], inc=cVE)
        s = fill("qw1", 0, 0, 2048, tile0)
        waits = [(cF[s], bk["slot_fill"][s]), (cVE, m_hin)] + half_wait(1)
        m_q1 = None
        for j in range(DC):
            for ml in range(2):
                m_q1 = mm(psH[1][:, ml * T:(ml + 1) * T],
                          slot[s][:, j * 256 + ml * 128:
                                  j * 256 + (ml + 1) * 128],
                          hin[:, j * T:(j + 1) * T],
                          start=(j == 0), stop=(j == DC - 1),
                          waits=waits,
                          inc=(cPE if (j == DC - 1 and ml == 1) else None))
                waits = []
        bk["slot_free"][s] = m_q1
        bk["hin_free"] = m_q1
        m_r = S.op("scalar", lambda e: e.activation(
            rtile[:, 0:2 * T], psH[1][:, 0:2 * T], AF.Relu),
            waits=[(cPE, m_q1)], inc=cSC)
        bk["half_free"][1] = (cSC, m_r)
        S.drain("vector")
        m_hq = S.op("vector", lambda e: e.tensor_tensor(
            hq[:, :], rtile[:, 0:2 * T], rtile[:, 0:2 * T], ALU.mult),
            waits=[(cSC, m_r)], inc=cVE)
        return m_hq

    def emit_attn_front(t, tile0, hq_mark):
        fsb = fscr[:, :].bitcast(BF16)
        arena = {0: qb, 1: kb, 2: vb}
        ev_marks = []
        st_busy = {"q": 0, "k": 0}
        res = {}

        def stats_batch(which, j, src_, ev_need):
            sth = 0 if which == "q" else 1
            S.drain("vector")
            mq = S.op("vector", lambda e, j_=j, sb=src_: e.tensor_tensor(
                fsb, sb[:, j_ * 4 * T:(j_ + 1) * 4 * T],
                sb[:, j_ * 4 * T:(j_ + 1) * 4 * T], ALU.mult),
                waits=[(cSC, ev_need)] +
                      ([(cPE, st_busy[which])] if st_busy[which] else []),
                inc=cVE)
            m_st = None
            for i in range(4):
                hg = 4 * j + i
                m_st = mm(psH[sth][0:16, 2 * T:3 * T],
                          colsel[:, hg * 16:(hg + 1) * 16],
                          fsb[:, i * T:(i + 1) * T],
                          start=(hg == 0), stop=(hg == NH - 1),
                          waits=([(cVE, mq)] if i == 0 else []) +
                                (half_wait(sth) if hg == 0 else []),
                          inc=(cPE if i == 3 else None))
            st_busy[which] = m_st
            return m_st

        def rows_emit(which):
            sth = 0 if which == "q" else 1
            m_st = st_busy[which]
            S.op("scalar", lambda e, h_=sth: e.activation(
                rowscr[0:16, 0:T], psH[h_][0:16, 2 * T:3 * T], AF.Ln,
                scale=cc(C_I128, 16), bias=cc(C_EPSRMS, 16)),
                waits=[(cPE, m_st)])
            S.drain("scalar")
            m_ex = S.op("scalar", lambda e: e.activation(
                rowscr[0:16, 0:T], rowscr[0:16, 0:T], AF.Exp,
                scale=cc(C_NHALF, 16)), inc=cSC)
            S.drain("vector")
            rdst = rstd16[0:16, 0:T] if which == "q" else rstd16k[0:16, 0:T]
            m_r16 = S.op("vector", lambda e, r_=rdst: e.tensor_copy(
                r_, rowscr[0:16, 0:T]), waits=[(cSC, m_ex)], inc=cVE)
            return m_r16

        mult_state = {"bank": 0, "mark": {}, "last": {}}

        def mult_pair(which, hg, src_, dst, m_r16):
            sth = 0 if which == "q" else 1
            bnk = 2 + mult_state["bank"] % 2
            mult_state["bank"] += 1
            wts = [(cVE, m_r16)]
            pm = mult_state["mark"].get((sth, bnk))
            if pm:
                wts.append((cVE, pm))
            rsrc = rstd16[0:16, 0:T] if which == "q" else rstd16k[0:16, 0:T]
            m_bc = mm(psH[sth][:, bnk * T:(bnk + 1) * T],
                      sel16[0:16, hg * 128:(hg + 1) * 128],
                      rsrc, True, True, waits=wts, inc=cPE)
            S.drain("vector")
            m_mu = S.op("vector", lambda e, g_=hg, sb=src_, db=dst, h_=sth,
                        b_=bnk: e.tensor_tensor(
                db[:, g_ * T:(g_ + 1) * T], sb[:, g_ * T:(g_ + 1) * T],
                psH[h_][:, b_ * T:(b_ + 1) * T], ALU.mult),
                waits=[(cPE, m_bc)], inc=cVE)
            mult_state["mark"][(sth, bnk)] = m_mu
            mult_state["last"][which] = m_mu
            return m_mu

        gm = [0, 0]
        pending = []
        for g in range(24):
            hh = g % 2
            s = fill("qw2", g, 0, 512, tile0)
            waits = [(cF[s], bk["slot_fill"][s])]
            if g == 0:
                waits.append((cVE, hq_mark))
            if g in (0, 1):
                waits += half_wait(hh)
            else:
                waits += [(cSC, ev_marks[g - 2])]
            m_g = None
            for j in range(RC):
                for ml in range(2):
                    m_g = mm(psH[hh][:, ml * T:(ml + 1) * T],
                             slot[s][:, j * 256 + ml * 128:
                                     j * 256 + (ml + 1) * 128],
                             hq[:, j * T:(j + 1) * T],
                             start=(j == 0), stop=(j == RC - 1),
                             waits=waits,
                             inc=(cPE if (j == RC - 1 and ml == 1) else None))
                    waits = []
            bk["slot_free"][s] = m_g
            gm[hh] = m_g
            part = arena[g // 8]
            off = (g % 8) * 2 * T
            m_ev = S.op("scalar", lambda e, h_=hh, p_=part, o_=off:
                        e.activation(p_[:, o_:o_ + 2 * T],
                                     psH[h_][:, 0:2 * T], AF.Copy),
                        waits=[(cPE, m_g)], inc=cSC)
            ev_marks.append(m_ev)
            if g in (1, 3, 5, 7):
                stats_batch("q", g // 2, qb, m_ev)
            if g in (9, 11, 13, 15):
                stats_batch("k", (g - 8) // 2, kb, m_ev)
            if g == 9:
                m_r16q = rows_emit("q")
                pending += [("q", h_, qb, qh, m_r16q) for h_ in range(NH)]
            if g == 17:
                m_r16k = rows_emit("k")
                pending += [("k", h_, kb, kh, m_r16k) for h_ in range(NH)]
            for _ in range(2):
                if pending:
                    mult_pair(*pending.pop(0))
        while pending:
            mult_pair(*pending.pop(0))
        res["mq_hat"] = mult_state["last"]["q"]
        res["mk_hat"] = mult_state["last"]["k"]
        bk["half_free"][0] = (cSC, ev_marks[-2] if len(ev_marks) >= 2 else 0)
        bk["half_free"][1] = (cSC, ev_marks[-1])
        bk["bank23_free"] = (cVE, res["mk_hat"])
        return res["mq_hat"], res["mk_hat"]

    def emit_cs_tiles(t):
        cosv, sinv = sin32, cos32
        p0 = (t % TPS) * T
        S.drain("vector")
        S.op("vector", lambda e: e.tensor_copy(Ct[0:32, :],
                                               cosv[0:32, p0:p0 + T]),
             waits=[(cPO, bk["tabs_po"]), (cSC, bk["tabs_sc"])])
        S.op("vector", lambda e: e.tensor_copy(Ct[64:96, :],
                                               cosv[0:32, p0:p0 + T]))
        S.op("vector", lambda e: e.tensor_copy(St[0:32, :],
                                               sinv[0:32, p0:p0 + T]))
        m3 = S.op("vector", lambda e: e.tensor_scalar(
            St[64:96, :], sinv[0:32, p0:p0 + T], -1.0, None, ALU.mult),
            inc=cVE)
        return m3

    def hv(buf, o, L):
        return buf.rearrange("p (h tt) -> p h tt", h=NH)[:, :, o:o + L]

    def emit_attn(t, tile0, mq_hat, mk_hat, cs_mark):
        HL = T // 2

        def rope(buf, o, mark_in, sw_wait):
            bv3 = buf.rearrange("p (h tt) -> p h tt", h=NH)
            sv3 = swp.rearrange("p (h tt) -> p h tt", h=NH)
            S.op("scalar", lambda e: e.dma_start(
                out=sv3[0:32, :, o:o + HL], in_=bv3[64:96, :, o:o + HL]),
                waits=[(cVE, mark_in), (cSW, S.cnt["cSW"])] + sw_wait,
                inc=cSW, inc_by=16)
            m_sw = S.op("scalar", lambda e: e.dma_start(
                out=sv3[64:96, :, o:o + HL], in_=bv3[0:32, :, o:o + HL]),
                inc=cSW, inc_by=16)
            bv = bv3[:, :, o:o + HL]
            sv = sv3[:, :, o:o + HL]
            cvw = Ct[:, o:o + HL][:, None, :].broadcast_to([128, NH, HL])
            svw = St[:, o:o + HL][:, None, :].broadcast_to([128, NH, HL])
            S.drain("vector")
            S.op("vector", lambda e: e.tensor_tensor(bv, bv, cvw, ALU.mult),
                 waits=[(cVE, cs_mark), (cSW, m_sw)])
            S.drain("vector")
            S.op("vector", lambda e: e.tensor_tensor(sv, sv, svw, ALU.mult))
            S.drain("vector")
            m = S.op("vector", lambda e: e.tensor_tensor(bv, bv, sv, ALU.add),
                     inc=cVE)
            return m

        m_y = [None, None]
        cp_state = {"hh": 0}
        sv3f = swp.rearrange("p (h tt) -> p h tt", h=NH)
        bv3kf = kh.rearrange("p (h tt) -> p h tt", h=NH)
        swl = bk.get("swp_last", 0)
        S.op("scalar", lambda e: e.dma_start(
            out=swp[0:32, :], in_=kh[64:96, :]),
            waits=[(cVE, mk_hat), (cSW, S.cnt["cSW"])] +
                  ([(cVE, swl)] if swl else []),
            inc=cSW, inc_by=16)
        m_kswap = S.op("scalar", lambda e: e.dma_start(
            out=swp[64:96, :], in_=kh[0:32, :]), inc=cSW, inc_by=16)
        for o, i in ((0, 0), (HL, 1)):
            bv3q = qh.rearrange("p (h tt) -> p h tt", h=NH)
            bv3k = kh.rearrange("p (h tt) -> p h tt", h=NH)
            sv3 = swp.rearrange("p (h tt) -> p h tt", h=NH)
            cvw = Ct[:, o:o + HL][:, None, :].broadcast_to([128, NH, HL])
            svw = St[:, o:o + HL][:, None, :].broadcast_to([128, NH, HL])

            def swap_dma(bv3, dep):
                d1 = sv3[0:32, :, o:o + HL]
                s1 = bv3[64:96, :, o:o + HL]
                d2 = sv3[64:96, :, o:o + HL]
                s2 = bv3[0:32, :, o:o + HL]
                S.op("scalar", lambda e, d_=d1, s_=s1: e.dma_start(
                    out=d_, in_=s_),
                    waits=[(cVE, dep), (cSW, S.cnt["cSW"])],
                    inc=cSW, inc_by=16)
                return S.op("scalar", lambda e, d_=d2, s_=s2: e.dma_start(
                    out=d_, in_=s_), inc=cSW, inc_by=16)

            def rope3(bv3, m_sw, extra=()):
                bv = bv3[:, :, o:o + HL]
                sv = sv3[:, :, o:o + HL]
                S.drain("vector")
                S.op("vector", lambda e, b=bv, c_=cvw: e.tensor_tensor(
                    b, b, c_, ALU.mult),
                    waits=[(cVE, cs_mark), (cSW, m_sw)] + list(extra))
                S.drain("vector")
                S.op("vector", lambda e, s_=sv, w_=svw: e.tensor_tensor(
                    s_, s_, w_, ALU.mult))
                S.drain("vector")
                return S.op("vector", lambda e, b=bv, s_=sv: e.tensor_tensor(
                    b, b, s_, ALU.add), inc=cVE)

            m_k = rope3(bv3k, m_kswap)
            m_sg = S.op("scalar", lambda e, o_=o: e.activation(
                hv(kh, o_, HL), hv(kh, o_, HL), AF.Sigmoid),
                waits=[(cVE, m_k)], inc=cSC)
            m_swq = swap_dma(bv3q, max(m_k, mq_hat))
            m_rq = rope3(bv3q, m_swq)
            S.drain("vector")
            m_ropeq = S.op("vector", lambda e, o_=o: e.tensor_scalar(
                hv(qh, o_, HL), hv(qh, o_, HL), cc(C_ZERO), None, ALU.max),
                inc=cVE)
            if i == 1:
                bk["swp_last"] = m_rq
            if i == 1 and cp_state.get("pending") is not None:
                emit_cproj_uv(t, cp_state.pop("pending"), cp_state)
            S.drain("vector")
            m_s = S.op("vector", lambda e, o_=o: e.tensor_tensor(
                hv(kh, o_, HL), hv(kh, o_, HL), hv(vb, o_, HL), ALU.mult),
                waits=[(cSC, m_sg)], inc=cVE)
            seq_start = (t % TPS) == 0
            S.drain("vector")
            for hg in range(NH):
                if i == 0:
                    init = 0.0 if seq_start else carry[:, hg:hg + 1]
                else:
                    init = cums[:, hg * T + HL - 1:hg * T + HL]
                S.op("vector", lambda e, g_=hg, ini=init, o_=o:
                     e.tensor_tensor_scan(
                    cums[:, g_ * T + o_:g_ * T + o_ + HL], onesT[:, 0:HL],
                    kh[:, g_ * T + o_:g_ * T + o_ + HL], ini,
                    ALU.mult, ALU.add),
                    waits=[(cVE, m_s)] if hg == 0 else [])
            S.drain("vector")
            if i == 1:
                cv = cums.rearrange("p (h tt) -> p h tt", h=NH)[:, :, T - 1:T]
                S.op("vector", lambda e: e.tensor_copy(
                    carry[:, 0:NH][:, :, None], cv))
                S.drain("vector")
            m_y[i] = S.op("vector", lambda e, o_=o: e.tensor_tensor(
                hv(qh, o_, HL), hv(qh, o_, HL), hv(cums, o_, HL), ALU.mult),
                inc=cVE)
            info = emit_cproj_mms(t, tile0, o, m_y[i], cp_state)
            if i == 0:
                cp_state["pending"] = info
            else:
                emit_cproj_uv(t, info, cp_state)
        return cp_state

    def emit_cproj_mms(t, tile0, o, y_mark, st):
        HL = T // 2
        s = fill("cw1", 0, 0, 4096, tile0)
        hA = st["hh"]
        st["hh"] ^= 1
        waits = [(cF[s], bk["slot_fill"][s]), (cVE, y_mark)] + half_wait(hA)
        b23 = bk.get("bank23_free")
        if b23:
            waits.append(b23)
            bk["bank23_free"] = None
        m_c1 = None
        for j in range(NH):
            for ml in range(2):
                m_c1 = mm(psH[hA][:, ml * T:ml * T + HL],
                          slot[s][:, j * 256 + ml * 128:
                                  j * 256 + (ml + 1) * 128],
                          qh[:, j * T + o:j * T + o + HL],
                          start=(j == 0), stop=(j == NH - 1), waits=waits,
                          inc=(cPE if (j == NH - 1 and ml == 1) else None))
                waits = []
        bk["slot_free"][s] = m_c1
        m_rt = S.op("scalar", lambda e, h_=hA, o_=o: e.activation(
            rtile.rearrange("p (c tt) -> p c tt", c=2)[:, :, o_:o_ + HL],
            psH[h_][:, 0:2 * T].rearrange(
                "p (c tt) -> p c tt", c=2)[:, :, 0:HL],
            AF.Copy), waits=[(cPE, m_c1)], inc=cSC)
        bk["half_free"][hA] = (cSC, m_rt)
        hB = st["hh"]
        st["hh"] ^= 1
        info = {"o": o, "grp": [], "halves": (hA, hB)}
        for ci in range(4):
            s2 = fill("cw2", ci, 0, 1024, tile0)
            hh = (hB, hB, hA, hA)[ci]
            col = 256 * (ci % 2)
            waits = [(cF[s2], bk["slot_fill"][s2]), (cSC, m_rt)]
            if ci % 2 == 0:
                waits += half_wait(hh)
            m_g = None
            for j in range(RC):
                for ml in range(4):
                    m_g = mm(psH[hh][:, ml * T + col:ml * T + col + HL],
                             slot[s2][:, j * 512 + ml * 128:
                                     j * 512 + (ml + 1) * 128],
                             rtile[:, j * T + o:j * T + o + HL],
                             start=(j == 0), stop=(j == RC - 1), waits=waits,
                             inc=(cPE if (j == RC - 1 and ml == 3) else None))
                    waits = []
            bk["slot_free"][s2] = m_g
            info["grp"].append((hh, col, m_g))
        info["hB_done"] = info["grp"][1][2]
        info["all_done"] = info["grp"][-1][2]
        return info

    def emit_cproj_uv(t, info, st):
        HL = T // 2
        o = info["o"]
        x = x_res[t % 2]
        hA, hB = info["halves"]
        lastc = None
        uv_marks = []
        for ci in range(4):
            hh, col, m_g = info["grp"][ci]
            gview = psH[hh].rearrange("p (c tt) -> p c tt", c=4)[
                :, 1::2, col:col + HL]
            uview = psH[hh].rearrange("p (c tt) -> p c tt", c=4)[
                :, 0::2, col:col + HL]
            sqv = sqs.rearrange("p (c tt) -> p c tt", c=2)[:, :, 0:HL]
            fsv = fscr.rearrange("p (c tt) -> p c tt", c=2)[:, :, 0:HL]
            w_sl = []
            if ci == 0:
                w_sl.append((cPE, info["hB_done"]))
            elif ci == 2:
                w_sl.append((cPE, info["all_done"]))
            uvf = bk.get("uv_free", 0)
            if uvf > 0:
                w_sl.append((cVE, uvf))
            m_sl = S.op("scalar", lambda e, g_=gview: e.activation(
                sqv, g_, AF.Silu), waits=w_sl, inc=cSC)
            S.drain("vector")
            S.op("vector", lambda e, u_=uview: e.tensor_tensor(
                fsv, u_, sqv, ALU.mult),
                waits=[(cSC, m_sl)] + ([(cVE, uvf)] if uvf > 0 else []))
            S.drain("vector")
            xv = x.rearrange("p (c tt) -> p c tt", c=DC)[
                :, 2 * ci:2 * ci + 2, o:o + HL]
            lastc = S.op("vector", lambda e, x_=xv: e.tensor_tensor(
                x_, x_, fsv, ALU.add), inc=cVE)
            bk["uv_free"] = lastc
            uv_marks.append(lastc)
            if ci == 1:
                bk["half_free"][hB] = (cVE, lastc)
        bk["half_free"][hA] = (cVE, lastc)
        st["last"] = lastc
        st["uv_marks"] = uv_marks

    def emit_out(t, done_mark):
        xi = t % 2
        m = S.op("scalar", lambda e, ti=t, b=xi: e.dma_start(
            out=o_out[ti], in_=x_res[b][:, :]),
            waits=[(cVE, done_mark), (cO, S.cnt["cO"])], inc=cO, inc_by=16)
        bk["x_out"][xi] = m

    def emit_startup():
        S.op("gpsimd", lambda e: e.memset(ones_col[:, :], 1.0))
        S.op("gpsimd", lambda e: e.memset(ones_row[:, :], 1.0))
        S.op("gpsimd", lambda e: e.memset(onesT[:, :], 1.0))
        S.op("gpsimd", lambda e: e.memset(carry[:, :], 0.0))
        for idx, v in CONSTV.items():
            S.op("gpsimd", lambda e, i=idx, vv=float(v): e.memset(
                consts[:, i:i + 1], vv))
        S.op("gpsimd", lambda e: e.memset(Ct[32:64, :], 1.0))
        S.op("gpsimd", lambda e: e.memset(Ct[64:128, :], 1.0))
        S.op("gpsimd", lambda e: e.memset(St[32:64, :], 0.0))
        S.op("gpsimd", lambda e: e.memset(St[64:128, :], 0.0))
        S.op("gpsimd", lambda e: e.iota(cos32[0:32, :], [[1, 2048]],
                                        channel_multiplier=0,
                                        allow_small_or_imprecise_dtypes=True))
        m_po = S.op("gpsimd", lambda e: e.iota(
            sin32[0:32, 0:1], [[1, 1]], channel_multiplier=1,
            allow_small_or_imprecise_dtypes=True), inc=cPO)
        bk["tabs_po"] = m_po
        nload = 0
        for dst, src in [(scl["s1"], sc_in["s1c"]), (scl["s2"], sc_in["s2c"]),
                         (scl["sh1"], sc_in["sh1c"]),
                         (scl["sh2"], sc_in["sh2c"]), (scl3, sc_in["s3c"]),
                         (sel16, sel16_in), (colsel, colsel_in)]:
            S.op("sync", lambda e, d=dst, s_=src: e.dma_start(out=d[:, :],
                                                              in_=s_[:, :]),
                 inc=cL, inc_by=16)
            nload += 1
        m_ld = 16 * nload
        S.drain("vector")
        S.op("vector", lambda e: e.tensor_scalar(
            s3h[:, :], scl3[:, :], cc(C_HALF), None, ALU.mult),
            waits=[(cL, m_ld), (cPO, m_po)])
        m_fr = S.op("scalar", lambda e: e.activation(
            rowscr[0:32, 0:1], sin32[0:32, 0:1], AF.Exp,
            scale=cc(C_FRQ, 32)), waits=[(cPO, m_po)], inc=cSC)
        S.drain("vector")
        S.op("vector", lambda e: e.tensor_scalar(
            cos32[0:32, :], cos32[0:32, :], rowscr[0:32, 0:1], None, ALU.mult),
            waits=[(cSC, m_fr)])
        S.drain("vector")
        S.op("vector", lambda e: e.tensor_scalar(
            sin32[0:32, :], cos32[0:32, :], cc(C_PIH, 32), None, ALU.add))
        for buf in (sin32, cos32):
            S.drain("vector")
            S.op("vector", lambda e, b=buf: e.tensor_scalar(
                tabf, b[0:32, :], cc(C_I2PI, 32), cc(C_MAGIC, 32),
                ALU.mult, ALU.add))
            S.drain("vector")
            S.op("vector", lambda e: e.tensor_scalar(
                tabf, tabf, cc(C_MAGIC, 32), None, ALU.subtract))
            S.drain("vector")
            S.op("vector", lambda e, b=buf: e.scalar_tensor_tensor(
                b[0:32, :], tabf, cc(C_N2PI, 32), b[0:32, :],
                ALU.mult, ALU.add))
            S.drain("vector")
            S.op("vector", lambda e, b=buf: e.tensor_scalar(
                b[0:32, :], b[0:32, :], cc(C_PI, 32), None, ALU.min))
            S.drain("vector")
            S.op("vector", lambda e, b=buf: e.tensor_scalar(
                b[0:32, :], b[0:32, :], cc(C_NPI, 32), None, ALU.max))
        S.drain("vector")
        m_v = S.op("vector", lambda e: e.engine_nop(), inc=cVE)
        S.op("scalar", lambda e: e.activation(sin32[0:32, :], sin32[0:32, :],
                                              AF.Sin), waits=[(cVE, m_v)])
        S.drain("scalar")
        m_tab = S.op("scalar", lambda e: e.activation(
            cos32[0:32, :], cos32[0:32, :], AF.Sin), inc=cSC)
        bk["tabs_sc"] = m_tab
        order = ["w1", "w2", "h1", "h2", "w3", "qw1", "qw2", "cw1", "cw2"]
        for i, lay in enumerate(order):
            waits = [(cP, 16 * i)] if i > 0 else []
            m = S.op("gpsimd", lambda e, l=lay: e.dma_start(
                out=w_scr[l][:], in_=w_f32[l][:]),
                waits=waits, inc=cP, inc_by=16)
            bk["prep"][lay] = m

    emit_startup()
    for _e in ("vector", "scalar", "tensor"):
        S.op(_e, lambda e: e.nop(),
             waits=[(cPO, bk["tabs_po"]), (cL, 16 * 7)])
    emit_xload(0)
    for t in range(nt):
        tile0 = (t == 0)
        if t + 1 < nt:
            emit_xload(t + 1)
        m_ln1 = emit_ln(t, "ln")
        f1_marks = emit_ffn(t, tile0, m_ln1)
        m_ln2 = emit_ln(t, "ln2", xready=[(cVE, f1_marks[3]),
                                          (cVE, f1_marks[7])])
        m_cs = emit_cs_tiles(t)
        m_hq = emit_rms_qkv1(t, tile0, m_ln2)
        S.drain("vector")
        S.op("vector", lambda e, b=t % 2: e.tensor_tensor(
            x_res[b][:, :], x_res[b][:, :], lnb[:, :], ALU.add))
        m_qhat, m_khat = emit_attn_front(t, tile0, m_hq)
        cp_state_holder = emit_attn(t, tile0, m_qhat, m_khat, m_cs)
        uvm = cp_state_holder.get("uv_marks")
        m_ln3 = emit_ln(t, "ln", xready=[(cVE, uvm[1]), (cVE, uvm[3])])
        m_f2 = emit_ffn(t, tile0, m_ln3)
        emit_out(t, m_f2[-1])
    S.op("scalar", lambda e: e.nop(), waits=[(cO, 16 * nt)])
    S.emit()
    return nc


def _pack_x(x_full):
    cores = []
    for c in range(NCORE):
        toks = np.ascontiguousarray(x_full[2 * c:2 * c + 2]).reshape(
            2 * 2048, D)
        tiles = []
        for t in range(NT):
            blk = toks[t * T:(t + 1) * T]
            fm = np.ascontiguousarray(blk.T).reshape(DC, 128, T)
            fm = np.ascontiguousarray(fm.transpose(1, 0, 2)).reshape(
                128, DC * T)
            tiles.append(fm)
        cores.append(np.ascontiguousarray(np.stack(tiles)))
    return cores


def _unpack_out(tiles_by_core):
    out = np.empty((16, 2048, D), dtype=np.float32)
    for c in range(NCORE):
        toks = np.empty((2 * 2048, D), dtype=np.float32)
        for t in range(NT):
            fm = np.asarray(tiles_by_core[c][t]).reshape(
                128, DC, T).transpose(1, 0, 2)
            toks[t * T:(t + 1) * T] = fm.reshape(D, T).T
        out[2 * c:2 * c + 2] = toks.reshape(2, 2048, D)
    return out


def _block_w(w, MB, NQ):
    O, I = w.shape
    KC_, MC_ = I // 128, O // 128
    NG, KQ = MC_ // MB, KC_ // NQ
    lhsT = np.ascontiguousarray(w.T)
    a = lhsT.reshape(KQ, NQ, 128, NG, MB * 128)
    a = a.transpose(3, 0, 2, 1, 4)
    return np.ascontiguousarray(a.reshape(NG, KQ, 128, NQ * MB * 128))


def _scale_cols(s):
    n = s.shape[0] // 128
    return np.ascontiguousarray(s.reshape(n, 128).T)


def prep_inputs(L0_attn_qkv_w1, L0_attn_qkv_w2, L0_attn_cproj_w1,
                L0_attn_cproj_w2, L0_ffn_w1_w, L0_ffn_w1_s, L0_ffn_w2_w,
                L0_ffn_w2_s, L0_ffn_h1_w, L0_ffn_h1_s, L0_ffn_h2_w,
                L0_ffn_h2_s, L0_ffn_w3_w, L0_ffn_w3_s):
    qw2p = np.asarray(L0_attn_qkv_w2, dtype=np.float32)
    cw2 = np.asarray(L0_attn_cproj_w2, dtype=np.float32)
    perm2 = []
    for i in range(8):
        perm2 += [i, 8 + i]
    cw2p = cw2.reshape(16, 128, DR)[perm2].reshape(16 * 128, DR)

    base = {
        "w1_f32": _block_w(np.asarray(L0_ffn_w1_w, np.float32), 4, 8),
        "w2_f32": _block_w(np.asarray(L0_ffn_w2_w, np.float32), 4, 8),
        "h1_f32": _block_w(np.asarray(L0_ffn_h1_w, np.float32), 4, 8),
        "h2_f32": _block_w(np.asarray(L0_ffn_h2_w, np.float32), 4, 8),
        "w3_f32": _block_w(np.asarray(L0_ffn_w3_w, np.float32), 4, 8),
        "qw1_f32": _block_w(np.asarray(L0_attn_qkv_w1, np.float32), 2, 8),
        "qw2_f32": _block_w(qw2p, 2, 2),
        "cw1_f32": _block_w(np.asarray(L0_attn_cproj_w1, np.float32), 2, 16),
        "cw2_f32": _block_w(cw2p, 4, 2),
        "s1c": _scale_cols(np.asarray(L0_ffn_w1_s, np.float32)),
        "s2c": _scale_cols(np.asarray(L0_ffn_w2_s, np.float32)),
        "sh1c": _scale_cols(np.asarray(L0_ffn_h1_s, np.float32)),
        "sh2c": _scale_cols(np.asarray(L0_ffn_h2_s, np.float32)),
        "s3c": _scale_cols(np.asarray(L0_ffn_w3_s, np.float32)),
    }
    sel16 = np.repeat(np.eye(16, dtype=np.float32), 128, axis=1)
    colsel = np.zeros((128, 256), np.float32)
    for h in range(16):
        colsel[:, h * 16 + h] = 1.0
    base["sel16"] = sel16.astype(ml_dtypes.bfloat16)
    base["colsel"] = colsel.astype(ml_dtypes.bfloat16)
    return base


def kernel(L0_attn_x, L0_attn_qkv_w1, L0_attn_qkv_w2, L0_attn_cproj_w1,
           L0_attn_cproj_w2, L0_ffn_w1_w, L0_ffn_w1_s, L0_ffn_w2_w,
           L0_ffn_w2_s, L0_ffn_h1_w, L0_ffn_h1_s, L0_ffn_h2_w, L0_ffn_h2_s,
           L0_ffn_w3_w, L0_ffn_w3_s):
    x_cores = _pack_x(np.asarray(L0_attn_x, dtype=np.float32))
    base = prep_inputs(L0_attn_qkv_w1, L0_attn_qkv_w2, L0_attn_cproj_w1,
                       L0_attn_cproj_w2, L0_ffn_w1_w, L0_ffn_w1_s,
                       L0_ffn_w2_w, L0_ffn_w2_s, L0_ffn_h1_w, L0_ffn_h1_s,
                       L0_ffn_h2_w, L0_ffn_h2_s, L0_ffn_w3_w, L0_ffn_w3_s)
    nc = build_program(NT)
    in_maps = [dict(base, x_in=x_cores[c]) for c in range(NCORE)]
    res = run_bass_kernel_spmd(nc, in_maps, core_ids=list(range(NCORE)))
    return _unpack_out([res.results[c]["o_out"] for c in range(NCORE)])


# version 12
# speedup vs baseline: 1.0879x; 1.0879x over previous
import numpy as np
import ml_dtypes
import concourse.bass as bass
import concourse.mybir as mybir
from concourse.bass_utils import run_bass_kernel_spmd

F32 = mybir.dt.float32
BF16 = mybir.dt.bfloat16
AF = mybir.ActivationFunctionType
ALU = mybir.AluOpType

D = 1024
DC = 8
H = 4096
HC = 32
T = 512
NT = 8
TPS = 4
NH = 16
DR = 256
RC = 2
NCORE = 8

EPS_LN = 1e-5
EPS_RMS = float(np.finfo(np.float32).eps)
NSLOT = 4
SLOTE = 4096

ENGINES = ("sync", "gpsimd", "vector", "scalar", "tensor")

WSPEC = {
    "w1": (8, 1, 4096), "w2": (8, 1, 4096),
    "h1": (8, 4, 4096), "h2": (8, 4, 4096),
    "w3": (2, 4, 4096),
    "qw1": (1, 1, 2048), "qw2": (24, 1, 512),
    "cw1": (1, 1, 4096), "cw2": (4, 1, 1024),
}

C_HALF, C_NEG1, C_ZERO, C_FRQ, C_I2PI, C_MAGIC, C_N2PI, C_PIH, \
    C_EPSLN, C_NHALF, C_I1024, C_I128, C_EPSRMS, C_PI, C_NPI = range(15)
CONSTV = {C_HALF: 0.5, C_NEG1: -1.0, C_ZERO: 0.0,
          C_FRQ: -float(np.log(1024.0)) / 31.0,
          C_I2PI: float(1.0 / (2 * np.pi)), C_MAGIC: 12582912.0,
          C_N2PI: float(-2 * np.pi), C_PIH: float(np.pi / 2),
          C_EPSLN: EPS_LN, C_NHALF: -0.5, C_I1024: 1.0 / 1024.0,
          C_I128: 1.0 / 128.0, C_EPSRMS: EPS_RMS,
          C_PI: float(np.pi), C_NPI: float(-np.pi)}


class Sched:
    def __init__(self, nc):
        self.nc = nc
        self.q = {e: [] for e in ENGINES}
        self.sems = {}
        self.cnt = {}

    def sem(self, name):
        self.sems[name] = self.nc.alloc_semaphore(name)
        self.cnt[name] = 0
        return name

    def op(self, eng, fn, waits=(), inc=None, inc_by=1):
        waits = [(s, v) for (s, v) in waits if v and v > 0]
        mark = None
        if inc is not None:
            self.cnt[inc] += inc_by
            mark = self.cnt[inc]
        self.q[eng].append((waits, fn, inc, inc_by))
        return mark

    def drain(self, eng):
        self.op(eng, lambda e: e.drain())

    def emit(self):
        nc = self.nc
        with nc.Block() as block:
            for ename in ENGINES:
                def make_body(name):
                    def body(eng):
                        for waits, fn, inc, n in self.q[name]:
                            for (s, v) in waits:
                                eng.wait_ge(self.sems[s], v)
                            ins = fn(eng)
                            if inc is not None:
                                ins.then_inc(self.sems[inc], n)
                    return body
                getattr(block, ename)(make_body(ename))


def build_program(nt=NT):
    nc = bass.Bass("TRN2", target_bir_lowering=False)

    x_in = nc.dram_tensor("x_in", [nt, 128, DC * T], F32, kind="ExternalInput")
    o_out = nc.dram_tensor("o_out", [nt, 128, DC * T], F32,
                           kind="ExternalOutput")
    w_f32 = {k: nc.dram_tensor(f"{k}_f32", [ng, kq, 128, ch], F32,
                               kind="ExternalInput")
             for k, (ng, kq, ch) in WSPEC.items()}
    w_scr = {k: nc.dram_tensor(f"{k}_scr", [ng, kq, 128, ch], BF16)
             for k, (ng, kq, ch) in WSPEC.items()}
    sc_in = {k: nc.dram_tensor(k, [128, n], F32, kind="ExternalInput")
             for k, n in (("s1c", 32), ("s2c", 32), ("sh1c", 32),
                          ("sh2c", 32), ("s3c", 8))}
    sel16_in = nc.dram_tensor("sel16", [16, 2048], BF16, kind="ExternalInput")
    colsel_in = nc.dram_tensor("colsel", [128, 256], BF16, kind="ExternalInput")

    SB = nc.alloc_sbuf_tensor
    x_res = [SB(f"x{i}", [128, DC * T], F32) for i in range(2)]
    ring = SB("ring", [128, NSLOT * SLOTE], BF16)
    slot = [ring[:, i * SLOTE:(i + 1) * SLOTE] for i in range(NSLOT)]
    A = SB("arnA", [128, HC * T], BF16)
    B = SB("arnB", [128, HC * T], BF16)
    C = SB("arnC", [128, NH * T], BF16)
    lnb = SB("lnb", [128, DC * T], BF16)
    tmp1 = SB("tmp1", [128, 2 * 4 * T], BF16)
    hq = SB("hq", [128, RC * T], BF16)
    rtile = SB("rtile", [128, RC * T], BF16)
    sqs = SB("sqs", [128, 2 * T], BF16)
    fscr = SB("fscr", [128, 2 * T], F32)
    rowscr = SB("rowscr", [128, 3 * T], F32)
    abrow = SB("abrow", [128, 2 * T], BF16)
    bcb = SB("bcb", [128, 2 * T], BF16)
    rstd16 = SB("rstd16", [16, T], BF16)
    rstd16k = SB("rstd16k", [16, T], BF16)
    cos32 = SB("cos32", [32, 2048], F32)
    sin32 = SB("sin32", [32, 2048], F32)
    Ct = SB("Ct", [128, T], BF16)
    St = SB("St", [128, T], BF16)
    onesT = SB("onesT", [128, T], BF16)
    ones_col = SB("ones_col", [128, 1], BF16)
    ones_row = SB("ones_row", [1, 128], BF16)
    sel16 = SB("sel16t", [16, 2048], BF16)
    colsel = SB("colselt", [128, 256], BF16)
    scl = {k: SB(f"scl_{k}", [128, 32], F32) for k in ("s1", "s2", "sh1", "sh2")}
    scl3 = SB("scl_s3", [128, 8], F32)
    s3h = SB("s3h", [128, 8], F32)
    carry = SB("carry", [128, NH], F32)
    consts = SB("consts", [128, 16], F32)

    qb = A[:, 0:NH * T]
    kb = A[:, NH * T:2 * NH * T]
    swp = A[:, 0:NH * T]
    vb = B[:, 0:NH * T]
    qh = B[:, NH * T:2 * NH * T]
    kh = C[:, 0:NH * T]
    cums = B[:, 0:NH * T]
    xb8 = C[:, 0:DC * T]
    hin = C[:, DC * T:2 * DC * T]
    tabf = None

    tabf = x_res[0][0:32, 0:2048]
    psH = [nc.alloc_psum_tensor("psA", [128, 4 * T], F32),
           nc.alloc_psum_tensor("psB", [128, 4 * T], F32)]

    S = Sched(nc)
    cF = [S.sem(f"cF{i}") for i in range(NSLOT)]
    cX = [S.sem(f"cX{i}") for i in range(2)]
    cL = S.sem("cL")
    cO = S.sem("cO")
    cP = S.sem("cP")
    cSW = S.sem("cSW")
    cPE = S.sem("cPE")
    cSC = S.sem("cSC")
    cVE = S.sem("cVE")
    cPO = S.sem("cPO")

    bk = {
        "slot_free": [0] * NSLOT,
        "slot_fill": [0] * NSLOT,
        "slot_uses": [0] * NSLOT,
        "half_free": [(None, 0), (None, 0)],
        "x_out": [0, 0],
        "x_fill": [0, 0],
        "prep": {},
        "ci": 0,
        "rope_done": 0,
    }

    def cc(idx, p=128):
        return consts[0:p, idx:idx + 1]

    def fill(layer, ng, kq, elems, tile0):
        s = bk["ci"] % NSLOT
        bk["ci"] += 1
        waits = []
        if bk["slot_free"][s] > 0:
            waits.append((cPE, bk["slot_free"][s]))
        if tile0:
            waits.append((cP, bk["prep"][layer]))
        src = w_scr[layer][ng, kq]
        dst = slot[s] if elems == SLOTE else slot[s][:, 0:elems]
        m = S.op("sync", lambda e, d=dst, sr=src: e.dma_start(out=d, in_=sr),
                 waits=waits, inc=cF[s], inc_by=16)
        bk["slot_fill"][s] = m
        return s

    def half_wait(h):
        sem, m = bk["half_free"][h]
        return [(sem, m)] if (sem is not None and m > 0) else []

    def mm(out_ap, lhsT_ap, rhs_ap, start, stop, waits=(), inc=None):
        return S.op("tensor",
                    lambda e, o=out_ap, l=lhsT_ap, r=rhs_ap, st=start, sp=stop:
                    e.matmul(o, l, r, start=st, stop=sp),
                    waits=waits, inc=inc)

    def emit_xload(t):
        xi = t % 2
        waits = [(cO, bk["x_out"][xi])] if bk["x_out"][xi] > 0 else []
        if t == 0:
            waits.append((cSC, bk["tabs_sc"]))
        m = S.op("gpsimd", lambda e, ti=t, b=xi: e.dma_start(
            out=x_res[b][:, :], in_=x_in[ti]),
            waits=waits, inc=cX[xi], inc_by=16)
        bk["x_fill"][xi] = m

    def emit_ln(t, mode, xready=None):
        x = x_res[t % 2]
        xw = t // 2 + 1
        hf = bk.get("hin_free", 0)
        if xready is None:
            xready = [(cX[t % 2], 16 * xw)] * 2
        m_cast = None
        m_sq = None
        for hhalf in range(2):
            r0 = hhalf * 4 * T
            S.drain("vector")
            m_cast = S.op("vector", lambda e, r_=r0: e.tensor_copy(
                xb8[:, r_:r_ + 4 * T], x[:, r_:r_ + 4 * T]),
                waits=[xready[hhalf]], inc=cVE)
            S.drain("vector")
            m_sq = S.op("vector", lambda e, r_=r0: e.tensor_tensor(
                hin[:, r_:r_ + 4 * T], xb8[:, r_:r_ + 4 * T],
                xb8[:, r_:r_ + 4 * T], ALU.mult),
                waits=([(cPE, hf)] if (hhalf == 0 and hf > 0) else []),
                inc=cVE)
        w0 = half_wait(0)
        sq_marks = []
        for k in range(DC):
            mm(psH[0][0:1, 0:T], ones_col[:, 0:1], xb8[:, k * T:(k + 1) * T],
               start=(k == 0), stop=(k == DC - 1),
               waits=(w0 + [(cVE, m_cast - 2)]) if k == 0 else
                     ([(cVE, m_cast)] if k == 4 else []))
            msq = mm(psH[0][0:1, T:2 * T], ones_col[:, 0:1],
                     hin[:, k * T:(k + 1) * T],
                     start=(k == 0), stop=(k == DC - 1),
                     waits=[(cVE, m_sq - 2)] if k == 0 else
                           ([(cVE, m_sq)] if k == 4 else []),
                     inc=(cPE if k == DC - 1 else None))
            sq_marks.append(msq)
        bk["hin_free"] = sq_marks[-1]
        r_m = rowscr[0:1, 0:T]
        r_v = rowscr[0:1, T:2 * T]
        r_r = rowscr[0:1, 2 * T:3 * T]
        S.drain("vector")
        S.op("vector", lambda e: e.tensor_scalar(
            r_m, psH[0][0:1, 0:T], 1.0 / D, None, ALU.mult),
            waits=[(cPE, bk["hin_free"])])
        S.drain("vector")
        S.op("vector", lambda e: e.tensor_tensor(r_v, r_m, r_m, ALU.mult))
        S.drain("vector")
        m_var = S.op("vector", lambda e: e.scalar_tensor_tensor(
            r_v, psH[0][0:1, T:2 * T], 1.0 / D, r_v, ALU.mult, ALU.subtract),
            inc=cVE)
        S.op("scalar", lambda e: e.activation(r_r, r_v, AF.Ln,
                                              bias=cc(C_EPSLN, 1)),
             waits=[(cVE, m_var)])
        S.drain("scalar")
        m_ex = S.op("scalar", lambda e: e.activation(
            r_r, r_r, AF.Exp, scale=cc(C_NHALF, 1)), inc=cSC)
        S.drain("vector")
        S.op("vector", lambda e: e.scalar_tensor_tensor(
            rowscr[0:1, T:2 * T], r_m, -1.0, r_r, ALU.mult, ALU.mult),
            waits=[(cSC, m_ex)])
        S.drain("vector")
        S.op("vector", lambda e: e.tensor_copy(abrow[0:1, 0:T], r_r))
        S.drain("vector")
        m_ab = S.op("vector", lambda e: e.tensor_copy(
            abrow[0:1, T:2 * T], rowscr[0:1, T:2 * T]), inc=cVE)
        w1h = half_wait(1)
        mm(psH[1][:, 0:T], ones_row[0:1, :], abrow[0:1, 0:T], True, True,
           waits=w1h + [(cVE, m_ab)])
        m_bc = mm(psH[1][:, T:2 * T], ones_row[0:1, :], abrow[0:1, T:2 * T],
                  True, True, inc=cPE)
        m_ev = S.op("scalar", lambda e: e.activation(
            bcb[:, 0:2 * T], psH[1][:, 0:2 * T], AF.Copy),
            waits=[(cPE, m_bc)], inc=cSC)
        bk["half_free"][1] = (cSC, m_ev)
        hmarks = []
        for hh2 in range(2):
            r0 = hh2 * 4
            xv = xb8.rearrange("p (c tt) -> p c tt", c=DC)[:, r0:r0 + 4, :]
            lv = lnb.rearrange("p (c tt) -> p c tt", c=DC)[:, r0:r0 + 4, :]
            av = bcb[:, 0:T][:, None, :].broadcast_to([128, 4, T])
            bv = bcb[:, T:2 * T][:, None, :].broadcast_to([128, 4, T])
            S.drain("vector")
            S.op("vector", lambda e, l_=lv, x_=xv, a_=av: e.tensor_tensor(
                l_, x_, a_, ALU.mult),
                waits=[(cSC, m_ev)] if hh2 == 0 else [])
            S.drain("vector")
            m = S.op("vector", lambda e, l_=lv, b_=bv: e.tensor_tensor(
                l_, l_, b_, ALU.add), inc=cVE)
            hmarks.append(m)
        last_m = hmarks[-1]
        bk["half_free"][0] = (cVE, m_var)
        bk["lnb_hmarks"] = hmarks
        return last_m

    def mm_group(lay, g, h, kq_n, ch, rhs_buf, rhs_sem, rhs_marks, tile0,
                 extra_waits=()):
        nq = 8
        mbw = ch // nq // 128
        grp_mark = None
        for kq in range(kq_n):
            s = fill(lay, g, kq, ch, tile0)
            waits = [(cF[s], bk["slot_fill"][s])] + list(extra_waits if kq == 0
                                                         else [])
            if kq == 0:
                waits += half_wait(h)
            waited = 0
            for j in range(nq):
                k = kq * nq + j
                if rhs_marks is not None:
                    mkk = rhs_marks[min(k, len(rhs_marks) - 1)]
                    if mkk > waited:
                        waits.append((rhs_sem, mkk))
                        waited = mkk
                rhs = rhs_buf[:, k * T:(k + 1) * T]
                for ml in range(mbw):
                    is_chunk_last = (j == nq - 1 and ml == mbw - 1)
                    lhsT = slot[s][:, j * mbw * 128 + ml * 128:
                                   j * mbw * 128 + (ml + 1) * 128]
                    mk = mm(psH[h][:, ml * T:(ml + 1) * T], lhsT, rhs,
                            start=(kq == 0 and j == 0),
                            stop=(kq == kq_n - 1 and j == nq - 1),
                            waits=waits,
                            inc=(cPE if is_chunk_last else None))
                    waits = []
            bk["slot_free"][s] = mk
            grp_mark = mk
        return grp_mark

    def ffn_w1w2(t, tile0, hm):
        lnmarks = [hm[0]] * 4 + [hm[1]] * 4
        out_marks = []
        silu_mark = {}
        for q0 in range(0, 8, 2):
            for g in (q0, q0 + 1):
                h = g % 2
                gm = mm_group("w1", g, h, 1, 4096, lnb, cVE, lnmarks,
                              tile0)
                lastc = None
                for ml in range(4):
                    mch = g * 4 + ml
                    w_t = []
                    if ml == 0:
                        w_t = [(cPE, gm)]
                        tf = bk.get(f"tmp1_free{h}", 0)
                        if tf > 0:
                            w_t.append((cVE, tf))
                    lastc = S.op("scalar", lambda e, h_=h, ml_=ml, m_=mch:
                                 e.activation(
                        tmp1[:, (h_ * 4 + ml_) * T:(h_ * 4 + ml_ + 1) * T],
                        psH[h_][:, ml_ * T:(ml_ + 1) * T], AF.Silu,
                        scale=scl["s1"][:, m_:m_ + 1]),
                        waits=w_t, inc=(cSC if ml == 3 else None))
                silu_mark[g] = lastc
                bk["half_free"][h] = (cSC, lastc)
            for g in (q0, q0 + 1):
                h = g % 2
                gm = mm_group("w2", g, h, 1, 4096, lnb, cVE, lnmarks,
                              tile0)
                lastc = None
                for ml in range(4):
                    mch = g * 4 + ml
                    if ml == 0:
                        S.drain("vector")
                    lastc = S.op("vector", lambda e, h_=h, ml_=ml, m_=mch:
                                 e.scalar_tensor_tensor(
                        A[:, m_ * T:(m_ + 1) * T],
                        psH[h_][:, ml_ * T:(ml_ + 1) * T],
                        scl["s2"][:, m_:m_ + 1],
                        tmp1[:, (h_ * 4 + ml_) * T:(h_ * 4 + ml_ + 1) * T],
                        ALU.mult, ALU.mult),
                        waits=([(cPE, gm), (cSC, silu_mark[g])]
                               if ml == 0 else []),
                        inc=(cVE if ml == 3 else None))
                out_marks += [lastc] * 4
                bk["half_free"][h] = (cVE, lastc)
                bk[f"tmp1_free{h}"] = lastc
        return out_marks

    def ffn_layer(t, layer, rhs_buf, out_kind, out_buf, scale_sb,
                  rhs_sem, rhs_marks, act, tile0):
        ng, kq_n, ch = WSPEC[layer]
        mbw = ch // 8 // 128
        out_marks = []
        for g in range(ng):
            h = g % 2
            grp_mark = mm_group(layer, g, h, kq_n, ch, rhs_buf, rhs_sem,
                                rhs_marks, tile0)
            if out_kind == "gelu":
                lastc = None
                for ml in range(mbw):
                    mch = g * mbw + ml
                    lastc = S.op("scalar", lambda e, h_=h, ml_=ml, m_=mch:
                                 e.activation(
                        out_buf[:, m_ * T:(m_ + 1) * T],
                        psH[h_][:, ml_ * T:(ml_ + 1) * T], act,
                        scale=scale_sb[:, m_:m_ + 1]),
                        waits=[(cPE, grp_mark)] if ml == 0 else [],
                        inc=(cSC if ml == mbw - 1 else None))
                out_marks += [lastc] * mbw
                bk["half_free"][h] = (cSC, lastc)
            elif out_kind == "resid":
                lastc = None
                for ml in range(mbw):
                    mch = g * mbw + ml
                    if ml == 0:
                        S.drain("vector")
                    lastc = S.op("vector", lambda e, h_=h, ml_=ml, m_=mch:
                                 e.scalar_tensor_tensor(
                        x_res[t % 2][:, m_ * T:(m_ + 1) * T],
                        psH[h_][:, ml_ * T:(ml_ + 1) * T],
                        s3h[:, m_:m_ + 1],
                        x_res[t % 2][:, m_ * T:(m_ + 1) * T],
                        ALU.mult, ALU.add),
                        waits=[(cPE, grp_mark)] if ml == 0 else [],
                        inc=(cVE if ml == mbw - 1 else None))
                    out_marks.append(lastc)
                bk["half_free"][h] = (cVE, lastc)
        return out_marks

    def emit_ffn(t, tile0, lnb_mark):
        hm = bk.get("lnb_hmarks", [lnb_mark, lnb_mark])
        a_marks = ffn_w1w2(t, tile0, hm)
        b_marks = ffn_layer(t, "h1", A, "gelu", B, scl["sh1"],
                            cVE, a_marks, AF.Gelu, tile0)
        a2_marks = ffn_layer(t, "h2", B, "gelu", A, scl["sh2"],
                             cSC, b_marks, AF.Gelu, tile0)
        r_marks = ffn_layer(t, "w3", A, "resid", None, None,
                            cSC, a2_marks, None, tile0)
        return r_marks

    def emit_rms_qkv1(t, tile0, xlnb_mark):
        w0 = half_wait(0)
        hf = bk.get("hin_free", 0)
        S.drain("vector")
        m_sq = S.op("vector", lambda e: e.tensor_tensor(
            hin[:, :], lnb[:, :], lnb[:, :], ALU.mult),
            waits=([(cVE, xlnb_mark)] + ([(cPE, hf)] if hf > 0 else [])),
            inc=cVE)
        marks = []
        for k in range(DC):
            msq = mm(psH[0][0:1, 0:T], ones_col[:, 0:1],
                     hin[:, k * T:(k + 1) * T],
                     start=(k == 0), stop=(k == DC - 1),
                     waits=(w0 + [(cVE, m_sq)]) if k == 0 else [],
                     inc=(cPE if k == DC - 1 else None))
            marks.append(msq)
        bk["hin_free"] = marks[-1]
        r_r = rowscr[0:1, 0:T]
        S.op("scalar", lambda e: e.activation(
            r_r, psH[0][0:1, 0:T], AF.Ln, scale=cc(C_I1024, 1),
            bias=cc(C_EPSRMS, 1)), waits=[(cPE, marks[-1])])
        S.drain("scalar")
        m_ex = S.op("scalar", lambda e: e.activation(
            r_r, r_r, AF.Exp, scale=cc(C_NHALF, 1)), inc=cSC)
        S.drain("vector")
        m_ab = S.op("vector", lambda e: e.tensor_copy(abrow[0:1, 0:T], r_r),
                    waits=[(cSC, m_ex)], inc=cVE)
        w1h = half_wait(1)
        m_bc = mm(psH[1][:, 0:T], ones_row[0:1, :], abrow[0:1, 0:T], True,
                  True, waits=w1h + [(cVE, m_ab)], inc=cPE)
        bk["half_free"][0] = (cSC, m_ex)
        m_evr = S.op("scalar", lambda e: e.activation(
            bcb[:, 0:T], psH[1][:, 0:T], AF.Copy),
            waits=[(cPE, m_bc)], inc=cSC)
        bk["half_free"][1] = (cSC, m_evr)
        rv_ = bcb[:, 0:T][:, None, :].broadcast_to([128, DC, T])
        S.drain("vector")
        m_hin = S.op("vector", lambda e: e.tensor_tensor(
            hin.rearrange("p (c tt) -> p c tt", c=DC),
            lnb.rearrange("p (c tt) -> p c tt", c=DC), rv_, ALU.mult),
            waits=[(cSC, m_evr), (cPE, bk["hin_free"])], inc=cVE)
        s = fill("qw1", 0, 0, 2048, tile0)
        waits = [(cF[s], bk["slot_fill"][s]), (cVE, m_hin)] + half_wait(1)
        m_q1 = None
        for j in range(DC):
            for ml in range(2):
                m_q1 = mm(psH[1][:, ml * T:(ml + 1) * T],
                          slot[s][:, j * 256 + ml * 128:
                                  j * 256 + (ml + 1) * 128],
                          hin[:, j * T:(j + 1) * T],
                          start=(j == 0), stop=(j == DC - 1),
                          waits=waits,
                          inc=(cPE if (j == DC - 1 and ml == 1) else None))
                waits = []
        bk["slot_free"][s] = m_q1
        bk["hin_free"] = m_q1
        m_r = S.op("scalar", lambda e: e.activation(
            rtile[:, 0:2 * T], psH[1][:, 0:2 * T], AF.Relu),
            waits=[(cPE, m_q1)], inc=cSC)
        bk["half_free"][1] = (cSC, m_r)
        S.drain("vector")
        m_hq = S.op("vector", lambda e: e.tensor_tensor(
            hq[:, :], rtile[:, 0:2 * T], rtile[:, 0:2 * T], ALU.mult),
            waits=[(cSC, m_r)], inc=cVE)
        return m_hq

    def emit_attn_front(t, tile0, hq_mark):
        fsb = fscr[:, :].bitcast(BF16)
        arena = {0: qb, 1: kb, 2: vb}
        ev_marks = []
        st_busy = {"q": 0, "k": 0}
        res = {}

        def stats_batch(which, j, src_, ev_need):
            sth = 0 if which == "q" else 1
            S.drain("vector")
            mq = S.op("vector", lambda e, j_=j, sb=src_: e.tensor_tensor(
                fsb, sb[:, j_ * 4 * T:(j_ + 1) * 4 * T],
                sb[:, j_ * 4 * T:(j_ + 1) * 4 * T], ALU.mult),
                waits=[(cSC, ev_need)] +
                      ([(cPE, st_busy[which])] if st_busy[which] else []),
                inc=cVE)
            m_st = None
            for i in range(4):
                hg = 4 * j + i
                m_st = mm(psH[sth][0:16, 2 * T:3 * T],
                          colsel[:, hg * 16:(hg + 1) * 16],
                          fsb[:, i * T:(i + 1) * T],
                          start=(hg == 0), stop=(hg == NH - 1),
                          waits=([(cVE, mq)] if i == 0 else []) +
                                (half_wait(sth) if hg == 0 else []),
                          inc=(cPE if i == 3 else None))
            st_busy[which] = m_st
            return m_st

        def rows_emit(which):
            sth = 0 if which == "q" else 1
            m_st = st_busy[which]
            S.op("scalar", lambda e, h_=sth: e.activation(
                rowscr[0:16, 0:T], psH[h_][0:16, 2 * T:3 * T], AF.Ln,
                scale=cc(C_I128, 16), bias=cc(C_EPSRMS, 16)),
                waits=[(cPE, m_st)])
            S.drain("scalar")
            m_ex = S.op("scalar", lambda e: e.activation(
                rowscr[0:16, 0:T], rowscr[0:16, 0:T], AF.Exp,
                scale=cc(C_NHALF, 16)), inc=cSC)
            S.drain("vector")
            rdst = rstd16[0:16, 0:T] if which == "q" else rstd16k[0:16, 0:T]
            m_r16 = S.op("vector", lambda e, r_=rdst: e.tensor_copy(
                r_, rowscr[0:16, 0:T]), waits=[(cSC, m_ex)], inc=cVE)
            return m_r16

        mult_state = {"bank": 0, "mark": {}, "last": {}}

        def mult_pair(which, hg, src_, dst, m_r16):
            sth = 0 if which == "q" else 1
            bnk = 2 + mult_state["bank"] % 2
            mult_state["bank"] += 1
            wts = [(cVE, m_r16)]
            pm = mult_state["mark"].get((sth, bnk))
            if pm:
                wts.append((cVE, pm))
            rsrc = rstd16[0:16, 0:T] if which == "q" else rstd16k[0:16, 0:T]
            m_bc = mm(psH[sth][:, bnk * T:(bnk + 1) * T],
                      sel16[0:16, hg * 128:(hg + 1) * 128],
                      rsrc, True, True, waits=wts, inc=cPE)
            S.drain("vector")
            m_mu = S.op("vector", lambda e, g_=hg, sb=src_, db=dst, h_=sth,
                        b_=bnk: e.tensor_tensor(
                db[:, g_ * T:(g_ + 1) * T], sb[:, g_ * T:(g_ + 1) * T],
                psH[h_][:, b_ * T:(b_ + 1) * T], ALU.mult),
                waits=[(cPE, m_bc)], inc=cVE)
            mult_state["mark"][(sth, bnk)] = m_mu
            mult_state["last"][which] = m_mu
            return m_mu

        gm = [0, 0]
        pending = []
        for g in range(24):
            hh = g % 2
            s = fill("qw2", g, 0, 512, tile0)
            waits = [(cF[s], bk["slot_fill"][s])]
            if g == 0:
                waits.append((cVE, hq_mark))
            if g in (0, 1):
                waits += half_wait(hh)
            else:
                waits += [(cSC, ev_marks[g - 2])]
            m_g = None
            for j in range(RC):
                for ml in range(2):
                    m_g = mm(psH[hh][:, ml * T:(ml + 1) * T],
                             slot[s][:, j * 256 + ml * 128:
                                     j * 256 + (ml + 1) * 128],
                             hq[:, j * T:(j + 1) * T],
                             start=(j == 0), stop=(j == RC - 1),
                             waits=waits,
                             inc=(cPE if (j == RC - 1 and ml == 1) else None))
                    waits = []
            bk["slot_free"][s] = m_g
            gm[hh] = m_g
            part = arena[g // 8]
            off = (g % 8) * 2 * T
            m_ev = S.op("scalar", lambda e, h_=hh, p_=part, o_=off:
                        e.activation(p_[:, o_:o_ + 2 * T],
                                     psH[h_][:, 0:2 * T], AF.Copy),
                        waits=[(cPE, m_g)], inc=cSC)
            ev_marks.append(m_ev)
            if g in (1, 3, 5, 7):
                stats_batch("q", g // 2, qb, m_ev)
            if g in (9, 11, 13, 15):
                stats_batch("k", (g - 8) // 2, kb, m_ev)
            if g == 9:
                m_r16q = rows_emit("q")
                pending += [("q", h_, qb, qh, m_r16q) for h_ in range(NH)]
            if g == 17:
                m_r16k = rows_emit("k")
                pending += [("k", h_, kb, kh, m_r16k) for h_ in range(NH)]
            for _ in range(2):
                if pending:
                    mult_pair(*pending.pop(0))
        while pending:
            mult_pair(*pending.pop(0))
        res["mq_hat"] = mult_state["last"]["q"]
        res["mk_hat"] = mult_state["last"]["k"]
        bk["half_free"][0] = (cSC, ev_marks[-2] if len(ev_marks) >= 2 else 0)
        bk["half_free"][1] = (cSC, ev_marks[-1])
        bk["bank23_free"] = (cVE, res["mk_hat"])
        return res["mq_hat"], res["mk_hat"]

    def emit_cs_tiles(t):
        cosv, sinv = sin32, cos32
        p0 = (t % TPS) * T
        S.drain("vector")
        S.op("vector", lambda e: e.tensor_copy(Ct[0:32, :],
                                               cosv[0:32, p0:p0 + T]),
             waits=[(cPO, bk["tabs_po"]), (cSC, bk["tabs_sc"])])
        S.op("vector", lambda e: e.tensor_copy(Ct[64:96, :],
                                               cosv[0:32, p0:p0 + T]))
        S.op("vector", lambda e: e.tensor_copy(St[0:32, :],
                                               sinv[0:32, p0:p0 + T]))
        m3 = S.op("vector", lambda e: e.tensor_scalar(
            St[64:96, :], sinv[0:32, p0:p0 + T], -1.0, None, ALU.mult),
            inc=cVE)
        return m3

    def hv(buf, o, L):
        return buf.rearrange("p (h tt) -> p h tt", h=NH)[:, :, o:o + L]

    def emit_attn(t, tile0, mq_hat, mk_hat, cs_mark):
        HL = T // 2

        def rope(buf, o, mark_in, sw_wait):
            bv3 = buf.rearrange("p (h tt) -> p h tt", h=NH)
            sv3 = swp.rearrange("p (h tt) -> p h tt", h=NH)
            S.op("scalar", lambda e: e.dma_start(
                out=sv3[0:32, :, o:o + HL], in_=bv3[64:96, :, o:o + HL]),
                waits=[(cVE, mark_in), (cSW, S.cnt["cSW"])] + sw_wait,
                inc=cSW, inc_by=16)
            m_sw = S.op("scalar", lambda e: e.dma_start(
                out=sv3[64:96, :, o:o + HL], in_=bv3[0:32, :, o:o + HL]),
                inc=cSW, inc_by=16)
            bv = bv3[:, :, o:o + HL]
            sv = sv3[:, :, o:o + HL]
            cvw = Ct[:, o:o + HL][:, None, :].broadcast_to([128, NH, HL])
            svw = St[:, o:o + HL][:, None, :].broadcast_to([128, NH, HL])
            S.drain("vector")
            S.op("vector", lambda e: e.tensor_tensor(bv, bv, cvw, ALU.mult),
                 waits=[(cVE, cs_mark), (cSW, m_sw)])
            S.drain("vector")
            S.op("vector", lambda e: e.tensor_tensor(sv, sv, svw, ALU.mult))
            S.drain("vector")
            m = S.op("vector", lambda e: e.tensor_tensor(bv, bv, sv, ALU.add),
                     inc=cVE)
            return m

        m_y = [None, None]
        cp_state = {"hh": 0}
        sv3f = swp.rearrange("p (h tt) -> p h tt", h=NH)
        bv3kf = kh.rearrange("p (h tt) -> p h tt", h=NH)
        swl = bk.get("swp_last", 0)
        S.op("scalar", lambda e: e.dma_start(
            out=swp[0:32, :], in_=kh[64:96, :]),
            waits=[(cVE, mk_hat), (cSW, S.cnt["cSW"])] +
                  ([(cVE, swl)] if swl else []),
            inc=cSW, inc_by=16)
        m_kswap = S.op("scalar", lambda e: e.dma_start(
            out=swp[64:96, :], in_=kh[0:32, :]), inc=cSW, inc_by=16)
        for o, i in ((0, 0), (HL, 1)):
            bv3q = qh.rearrange("p (h tt) -> p h tt", h=NH)
            bv3k = kh.rearrange("p (h tt) -> p h tt", h=NH)
            sv3 = swp.rearrange("p (h tt) -> p h tt", h=NH)
            cvw = Ct[:, o:o + HL][:, None, :].broadcast_to([128, NH, HL])
            svw = St[:, o:o + HL][:, None, :].broadcast_to([128, NH, HL])

            def swap_dma(bv3, dep):
                d1 = sv3[0:32, :, o:o + HL]
                s1 = bv3[64:96, :, o:o + HL]
                d2 = sv3[64:96, :, o:o + HL]
                s2 = bv3[0:32, :, o:o + HL]
                S.op("scalar", lambda e, d_=d1, s_=s1: e.dma_start(
                    out=d_, in_=s_),
                    waits=[(cVE, dep), (cSW, S.cnt["cSW"])],
                    inc=cSW, inc_by=16)
                return S.op("scalar", lambda e, d_=d2, s_=s2: e.dma_start(
                    out=d_, in_=s_), inc=cSW, inc_by=16)

            def rope3(bv3, m_sw, extra=()):
                bv = bv3[:, :, o:o + HL]
                sv = sv3[:, :, o:o + HL]
                S.drain("vector")
                S.op("vector", lambda e, b=bv, c_=cvw: e.tensor_tensor(
                    b, b, c_, ALU.mult),
                    waits=[(cVE, cs_mark), (cSW, m_sw)] + list(extra))
                S.drain("vector")
                S.op("vector", lambda e, s_=sv, w_=svw: e.tensor_tensor(
                    s_, s_, w_, ALU.mult))
                S.drain("vector")
                return S.op("vector", lambda e, b=bv, s_=sv: e.tensor_tensor(
                    b, b, s_, ALU.add), inc=cVE)

            m_k = rope3(bv3k, m_kswap)
            m_sg = S.op("scalar", lambda e, o_=o: e.activation(
                hv(kh, o_, HL), hv(kh, o_, HL), AF.Sigmoid),
                waits=[(cVE, m_k)], inc=cSC)
            m_swq = swap_dma(bv3q, max(m_k, mq_hat))
            m_rq = rope3(bv3q, m_swq)
            S.drain("vector")
            m_ropeq = S.op("vector", lambda e, o_=o: e.tensor_scalar(
                hv(qh, o_, HL), hv(qh, o_, HL), cc(C_ZERO), None, ALU.max),
                inc=cVE)
            if i == 1:
                bk["swp_last"] = m_rq
            if i == 1 and cp_state.get("pending") is not None:
                emit_cproj_uv(t, cp_state.pop("pending"), cp_state)
            S.drain("vector")
            m_s = S.op("vector", lambda e, o_=o: e.tensor_tensor(
                hv(kh, o_, HL), hv(kh, o_, HL), hv(vb, o_, HL), ALU.mult),
                waits=[(cSC, m_sg)], inc=cVE)
            seq_start = (t % TPS) == 0
            S.drain("vector")
            for hg in range(NH):
                if i == 0:
                    init = 0.0 if seq_start else carry[:, hg:hg + 1]
                else:
                    init = cums[:, hg * T + HL - 1:hg * T + HL]
                S.op("vector", lambda e, g_=hg, ini=init, o_=o:
                     e.tensor_tensor_scan(
                    cums[:, g_ * T + o_:g_ * T + o_ + HL], onesT[:, 0:HL],
                    kh[:, g_ * T + o_:g_ * T + o_ + HL], ini,
                    ALU.mult, ALU.add),
                    waits=[(cVE, m_s)] if hg == 0 else [])
            S.drain("vector")
            if i == 1:
                cv = cums.rearrange("p (h tt) -> p h tt", h=NH)[:, :, T - 1:T]
                S.op("vector", lambda e: e.tensor_copy(
                    carry[:, 0:NH][:, :, None], cv))
                S.drain("vector")
            m_y[i] = S.op("vector", lambda e, o_=o: e.tensor_tensor(
                hv(qh, o_, HL), hv(qh, o_, HL), hv(cums, o_, HL), ALU.mult),
                inc=cVE)
            info = emit_cproj_mms(t, tile0, o, m_y[i], cp_state)
            if i == 0:
                cp_state["pending"] = info
            else:
                emit_cproj_uv(t, info, cp_state)
        return cp_state

    def emit_cproj_mms(t, tile0, o, y_mark, st):
        HL = T // 2
        s = fill("cw1", 0, 0, 4096, tile0)
        hA = st["hh"]
        st["hh"] ^= 1
        waits = [(cF[s], bk["slot_fill"][s]), (cVE, y_mark)] + half_wait(hA)
        b23 = bk.get("bank23_free")
        if b23:
            waits.append(b23)
            bk["bank23_free"] = None
        m_c1 = None
        for j in range(NH):
            for ml in range(2):
                m_c1 = mm(psH[hA][:, ml * T:ml * T + HL],
                          slot[s][:, j * 256 + ml * 128:
                                  j * 256 + (ml + 1) * 128],
                          qh[:, j * T + o:j * T + o + HL],
                          start=(j == 0), stop=(j == NH - 1), waits=waits,
                          inc=(cPE if (j == NH - 1 and ml == 1) else None))
                waits = []
        bk["slot_free"][s] = m_c1
        m_rt = S.op("scalar", lambda e, h_=hA, o_=o: e.activation(
            rtile.rearrange("p (c tt) -> p c tt", c=2)[:, :, o_:o_ + HL],
            psH[h_][:, 0:2 * T].rearrange(
                "p (c tt) -> p c tt", c=2)[:, :, 0:HL],
            AF.Copy), waits=[(cPE, m_c1)], inc=cSC)
        bk["half_free"][hA] = (cSC, m_rt)
        hB = st["hh"]
        st["hh"] ^= 1
        info = {"o": o, "grp": [], "halves": (hA, hB)}
        for ci in range(4):
            s2 = fill("cw2", ci, 0, 1024, tile0)
            hh = (hB, hB, hA, hA)[ci]
            col = 256 * (ci % 2)
            waits = [(cF[s2], bk["slot_fill"][s2]), (cSC, m_rt)]
            if ci % 2 == 0:
                waits += half_wait(hh)
            m_g = None
            for j in range(RC):
                for ml in range(4):
                    m_g = mm(psH[hh][:, ml * T + col:ml * T + col + HL],
                             slot[s2][:, j * 512 + ml * 128:
                                     j * 512 + (ml + 1) * 128],
                             rtile[:, j * T + o:j * T + o + HL],
                             start=(j == 0), stop=(j == RC - 1), waits=waits,
                             inc=(cPE if (j == RC - 1 and ml == 3) else None))
                    waits = []
            bk["slot_free"][s2] = m_g
            info["grp"].append((hh, col, m_g))
        info["hB_done"] = info["grp"][1][2]
        info["all_done"] = info["grp"][-1][2]
        return info

    def emit_cproj_uv(t, info, st):
        HL = T // 2
        o = info["o"]
        x = x_res[t % 2]
        hA, hB = info["halves"]
        lastc = None
        uv_marks = []
        for ci in range(4):
            hh, col, m_g = info["grp"][ci]
            gview = psH[hh].rearrange("p (c tt) -> p c tt", c=4)[
                :, 1::2, col:col + HL]
            uview = psH[hh].rearrange("p (c tt) -> p c tt", c=4)[
                :, 0::2, col:col + HL]
            sqv = sqs.rearrange("p (c tt) -> p c tt", c=2)[:, :, 0:HL]
            fsv = fscr.rearrange("p (c tt) -> p c tt", c=2)[:, :, 0:HL]
            w_sl = []
            if ci == 0:
                w_sl.append((cPE, info["hB_done"]))
            elif ci == 2:
                w_sl.append((cPE, info["all_done"]))
            uvf = bk.get("uv_free", 0)
            if uvf > 0:
                w_sl.append((cVE, uvf))
            m_sl = S.op("scalar", lambda e, g_=gview: e.activation(
                sqv, g_, AF.Silu), waits=w_sl, inc=cSC)
            S.drain("vector")
            S.op("vector", lambda e, u_=uview: e.tensor_tensor(
                fsv, u_, sqv, ALU.mult),
                waits=[(cSC, m_sl)] + ([(cVE, uvf)] if uvf > 0 else []))
            S.drain("vector")
            xv = x.rearrange("p (c tt) -> p c tt", c=DC)[
                :, 2 * ci:2 * ci + 2, o:o + HL]
            lastc = S.op("vector", lambda e, x_=xv: e.tensor_tensor(
                x_, x_, fsv, ALU.add), inc=cVE)
            bk["uv_free"] = lastc
            uv_marks.append(lastc)
            if ci == 1:
                bk["half_free"][hB] = (cVE, lastc)
        bk["half_free"][hA] = (cVE, lastc)
        st["last"] = lastc
        st["uv_marks"] = uv_marks

    def emit_out(t, done_mark):
        xi = t % 2
        m = S.op("scalar", lambda e, ti=t, b=xi: e.dma_start(
            out=o_out[ti], in_=x_res[b][:, :]),
            waits=[(cVE, done_mark), (cO, S.cnt["cO"])], inc=cO, inc_by=16)
        bk["x_out"][xi] = m

    def emit_startup():
        S.op("gpsimd", lambda e: e.memset(ones_col[:, :], 1.0))
        S.op("gpsimd", lambda e: e.memset(ones_row[:, :], 1.0))
        S.op("gpsimd", lambda e: e.memset(onesT[:, :], 1.0))
        S.op("gpsimd", lambda e: e.memset(carry[:, :], 0.0))
        for idx, v in CONSTV.items():
            S.op("gpsimd", lambda e, i=idx, vv=float(v): e.memset(
                consts[:, i:i + 1], vv))
        S.op("gpsimd", lambda e: e.memset(Ct[32:64, :], 1.0))
        S.op("gpsimd", lambda e: e.memset(Ct[64:128, :], 1.0))
        S.op("gpsimd", lambda e: e.memset(St[32:64, :], 0.0))
        S.op("gpsimd", lambda e: e.memset(St[64:128, :], 0.0))
        S.op("gpsimd", lambda e: e.iota(cos32[0:32, :], [[1, 2048]],
                                        channel_multiplier=0,
                                        allow_small_or_imprecise_dtypes=True))
        m_po = S.op("gpsimd", lambda e: e.iota(
            sin32[0:32, 0:1], [[1, 1]], channel_multiplier=1,
            allow_small_or_imprecise_dtypes=True), inc=cPO)
        bk["tabs_po"] = m_po
        nload = 0
        for dst, src in [(scl["s1"], sc_in["s1c"]), (scl["s2"], sc_in["s2c"]),
                         (scl["sh1"], sc_in["sh1c"]),
                         (scl["sh2"], sc_in["sh2c"]), (scl3, sc_in["s3c"]),
                         (sel16, sel16_in), (colsel, colsel_in)]:
            S.op("sync", lambda e, d=dst, s_=src: e.dma_start(out=d[:, :],
                                                              in_=s_[:, :]),
                 inc=cL, inc_by=16)
            nload += 1
        m_ld = 16 * nload
        S.drain("vector")
        S.op("vector", lambda e: e.tensor_scalar(
            s3h[:, :], scl3[:, :], cc(C_HALF), None, ALU.mult),
            waits=[(cL, m_ld), (cPO, m_po)])
        m_fr = S.op("scalar", lambda e: e.activation(
            rowscr[0:32, 0:1], sin32[0:32, 0:1], AF.Exp,
            scale=cc(C_FRQ, 32)), waits=[(cPO, m_po)], inc=cSC)
        S.drain("vector")
        S.op("vector", lambda e: e.tensor_scalar(
            cos32[0:32, :], cos32[0:32, :], rowscr[0:32, 0:1], None, ALU.mult),
            waits=[(cSC, m_fr)])
        S.drain("vector")
        S.op("vector", lambda e: e.tensor_scalar(
            sin32[0:32, :], cos32[0:32, :], cc(C_PIH, 32), None, ALU.add))
        for buf in (sin32, cos32):
            S.drain("vector")
            S.op("vector", lambda e, b=buf: e.tensor_scalar(
                tabf, b[0:32, :], cc(C_I2PI, 32), cc(C_MAGIC, 32),
                ALU.mult, ALU.add))
            S.drain("vector")
            S.op("vector", lambda e: e.tensor_scalar(
                tabf, tabf, cc(C_MAGIC, 32), None, ALU.subtract))
            S.drain("vector")
            S.op("vector", lambda e, b=buf: e.scalar_tensor_tensor(
                b[0:32, :], tabf, cc(C_N2PI, 32), b[0:32, :],
                ALU.mult, ALU.add))
            S.drain("vector")
            S.op("vector", lambda e, b=buf: e.tensor_scalar(
                b[0:32, :], b[0:32, :], cc(C_PI, 32), None, ALU.min))
            S.drain("vector")
            S.op("vector", lambda e, b=buf: e.tensor_scalar(
                b[0:32, :], b[0:32, :], cc(C_NPI, 32), None, ALU.max))
        S.drain("vector")
        m_v = S.op("vector", lambda e: e.engine_nop(), inc=cVE)
        S.op("scalar", lambda e: e.activation(sin32[0:32, :], sin32[0:32, :],
                                              AF.Sin), waits=[(cVE, m_v)])
        S.drain("scalar")
        m_tab = S.op("scalar", lambda e: e.activation(
            cos32[0:32, :], cos32[0:32, :], AF.Sin), inc=cSC)
        bk["tabs_sc"] = m_tab
        order = ["w1", "w2", "h1", "h2", "w3", "qw1", "qw2", "cw1", "cw2"]
        for i, lay in enumerate(order):
            waits = [(cP, 16 * i)] if i > 0 else []
            m = S.op("gpsimd", lambda e, l=lay: e.dma_start(
                out=w_scr[l][:], in_=w_f32[l][:]),
                waits=waits, inc=cP, inc_by=16)
            bk["prep"][lay] = m

    emit_startup()
    for _e in ("vector", "scalar", "tensor"):
        S.op(_e, lambda e: e.nop(),
             waits=[(cPO, bk["tabs_po"]), (cL, 16 * 7)])
    emit_xload(0)
    for t in range(nt):
        tile0 = (t == 0)
        if t + 1 < nt:
            emit_xload(t + 1)
        m_ln1 = emit_ln(t, "ln")
        f1_marks = emit_ffn(t, tile0, m_ln1)
        m_ln2 = emit_ln(t, "ln2", xready=[(cVE, f1_marks[3]),
                                          (cVE, f1_marks[7])])
        m_cs = emit_cs_tiles(t)
        m_hq = emit_rms_qkv1(t, tile0, m_ln2)
        S.drain("vector")
        S.op("vector", lambda e, b=t % 2: e.tensor_tensor(
            x_res[b][:, :], x_res[b][:, :], lnb[:, :], ALU.add))
        m_qhat, m_khat = emit_attn_front(t, tile0, m_hq)
        cp_state_holder = emit_attn(t, tile0, m_qhat, m_khat, m_cs)
        uvm = cp_state_holder.get("uv_marks")
        m_ln3 = emit_ln(t, "ln", xready=[(cVE, uvm[1]), (cVE, uvm[3])])
        m_f2 = emit_ffn(t, tile0, m_ln3)
        emit_out(t, m_f2[-1])
    S.op("scalar", lambda e: e.nop(), waits=[(cO, 16 * nt)])
    S.emit()
    return nc


def _pack_x(x_full):
    cores = []
    for c in range(NCORE):
        toks = np.ascontiguousarray(x_full[2 * c:2 * c + 2]).reshape(
            2 * 2048, D)
        tiles = []
        for t in range(NT):
            blk = toks[t * T:(t + 1) * T]
            fm = np.ascontiguousarray(blk.T).reshape(DC, 128, T)
            fm = np.ascontiguousarray(fm.transpose(1, 0, 2)).reshape(
                128, DC * T)
            tiles.append(fm)
        cores.append(np.ascontiguousarray(np.stack(tiles)))
    return cores


def _unpack_out(tiles_by_core):
    out = np.empty((16, 2048, D), dtype=np.float32)
    for c in range(NCORE):
        toks = np.empty((2 * 2048, D), dtype=np.float32)
        for t in range(NT):
            fm = np.asarray(tiles_by_core[c][t]).reshape(
                128, DC, T).transpose(1, 0, 2)
            toks[t * T:(t + 1) * T] = fm.reshape(D, T).T
        out[2 * c:2 * c + 2] = toks.reshape(2, 2048, D)
    return out


def _block_w(w, MB, NQ):
    O, I = w.shape
    KC_, MC_ = I // 128, O // 128
    NG, KQ = MC_ // MB, KC_ // NQ
    lhsT = np.ascontiguousarray(w.T)
    a = lhsT.reshape(KQ, NQ, 128, NG, MB * 128)
    a = a.transpose(3, 0, 2, 1, 4)
    return np.ascontiguousarray(a.reshape(NG, KQ, 128, NQ * MB * 128))


def _scale_cols(s):
    n = s.shape[0] // 128
    return np.ascontiguousarray(s.reshape(n, 128).T)


def prep_inputs(L0_attn_qkv_w1, L0_attn_qkv_w2, L0_attn_cproj_w1,
                L0_attn_cproj_w2, L0_ffn_w1_w, L0_ffn_w1_s, L0_ffn_w2_w,
                L0_ffn_w2_s, L0_ffn_h1_w, L0_ffn_h1_s, L0_ffn_h2_w,
                L0_ffn_h2_s, L0_ffn_w3_w, L0_ffn_w3_s):
    qw2p = np.asarray(L0_attn_qkv_w2, dtype=np.float32)
    cw2 = np.asarray(L0_attn_cproj_w2, dtype=np.float32)
    perm2 = []
    for i in range(8):
        perm2 += [i, 8 + i]
    cw2p = cw2.reshape(16, 128, DR)[perm2].reshape(16 * 128, DR)

    base = {
        "w1_f32": _block_w(np.asarray(L0_ffn_w1_w, np.float32), 4, 8),
        "w2_f32": _block_w(np.asarray(L0_ffn_w2_w, np.float32), 4, 8),
        "h1_f32": _block_w(np.asarray(L0_ffn_h1_w, np.float32), 4, 8),
        "h2_f32": _block_w(np.asarray(L0_ffn_h2_w, np.float32), 4, 8),
        "w3_f32": _block_w(np.asarray(L0_ffn_w3_w, np.float32), 4, 8),
        "qw1_f32": _block_w(np.asarray(L0_attn_qkv_w1, np.float32), 2, 8),
        "qw2_f32": _block_w(qw2p, 2, 2),
        "cw1_f32": _block_w(np.asarray(L0_attn_cproj_w1, np.float32), 2, 16),
        "cw2_f32": _block_w(cw2p, 4, 2),
        "s1c": _scale_cols(np.asarray(L0_ffn_w1_s, np.float32)),
        "s2c": _scale_cols(np.asarray(L0_ffn_w2_s, np.float32)),
        "sh1c": _scale_cols(np.asarray(L0_ffn_h1_s, np.float32)),
        "sh2c": _scale_cols(np.asarray(L0_ffn_h2_s, np.float32)),
        "s3c": _scale_cols(np.asarray(L0_ffn_w3_s, np.float32)),
    }
    sel16 = np.repeat(np.eye(16, dtype=np.float32), 128, axis=1)
    colsel = np.zeros((128, 256), np.float32)
    for h in range(16):
        colsel[:, h * 16 + h] = 1.0
    base["sel16"] = sel16.astype(ml_dtypes.bfloat16)
    base["colsel"] = colsel.astype(ml_dtypes.bfloat16)
    return base


def kernel(L0_attn_x, L0_attn_qkv_w1, L0_attn_qkv_w2, L0_attn_cproj_w1,
           L0_attn_cproj_w2, L0_ffn_w1_w, L0_ffn_w1_s, L0_ffn_w2_w,
           L0_ffn_w2_s, L0_ffn_h1_w, L0_ffn_h1_s, L0_ffn_h2_w, L0_ffn_h2_s,
           L0_ffn_w3_w, L0_ffn_w3_s):
    x_cores = _pack_x(np.asarray(L0_attn_x, dtype=np.float32))
    base = prep_inputs(L0_attn_qkv_w1, L0_attn_qkv_w2, L0_attn_cproj_w1,
                       L0_attn_cproj_w2, L0_ffn_w1_w, L0_ffn_w1_s,
                       L0_ffn_w2_w, L0_ffn_w2_s, L0_ffn_h1_w, L0_ffn_h1_s,
                       L0_ffn_h2_w, L0_ffn_h2_s, L0_ffn_w3_w, L0_ffn_w3_s)
    nc = build_program(NT)
    in_maps = [dict(base, x_in=x_cores[c]) for c in range(NCORE)]
    res = run_bass_kernel_spmd(nc, in_maps, core_ids=list(range(NCORE)))
    return _unpack_out([res.results[c]["o_out"] for c in range(NCORE)])


# version 13
# speedup vs baseline: 1.1028x; 1.0137x over previous
import numpy as np
import ml_dtypes
import concourse.bass as bass
import concourse.mybir as mybir
from concourse.bass_utils import run_bass_kernel_spmd

F32 = mybir.dt.float32
BF16 = mybir.dt.bfloat16
AF = mybir.ActivationFunctionType
ALU = mybir.AluOpType

D = 1024
DC = 8
H = 4096
HC = 32
T = 512
NT = 8
TPS = 4
NH = 16
DR = 256
RC = 2
NCORE = 8

EPS_LN = 1e-5
EPS_RMS = float(np.finfo(np.float32).eps)
NSLOT = 4
SLOTE = 4096

ENGINES = ("sync", "gpsimd", "vector", "scalar", "tensor")

WSPEC = {
    "w1": (8, 1, 4096), "w2": (8, 1, 4096),
    "h1": (8, 4, 4096), "h2": (8, 4, 4096),
    "w3": (2, 4, 4096),
    "qw1": (1, 1, 2048), "qw2": (24, 1, 512),
    "cw1": (1, 1, 4096), "cw2": (4, 1, 1024),
}

C_HALF, C_NEG1, C_ZERO, C_FRQ, C_I2PI, C_MAGIC, C_N2PI, C_PIH, \
    C_EPSLN, C_NHALF, C_I1024, C_I128, C_EPSRMS, C_PI, C_NPI = range(15)
CONSTV = {C_HALF: 0.5, C_NEG1: -1.0, C_ZERO: 0.0,
          C_FRQ: -float(np.log(1024.0)) / 31.0,
          C_I2PI: float(1.0 / (2 * np.pi)), C_MAGIC: 12582912.0,
          C_N2PI: float(-2 * np.pi), C_PIH: float(np.pi / 2),
          C_EPSLN: EPS_LN, C_NHALF: -0.5, C_I1024: 1.0 / 1024.0,
          C_I128: 1.0 / 128.0, C_EPSRMS: EPS_RMS,
          C_PI: float(np.pi), C_NPI: float(-np.pi)}


class Sched:
    def __init__(self, nc):
        self.nc = nc
        self.q = {e: [] for e in ENGINES}
        self.sems = {}
        self.cnt = {}

    def sem(self, name):
        self.sems[name] = self.nc.alloc_semaphore(name)
        self.cnt[name] = 0
        return name

    def op(self, eng, fn, waits=(), inc=None, inc_by=1):
        waits = [(s, v) for (s, v) in waits if v and v > 0]
        mark = None
        if inc is not None:
            self.cnt[inc] += inc_by
            mark = self.cnt[inc]
        self.q[eng].append((waits, fn, inc, inc_by))
        return mark

    def drain(self, eng):
        self.op(eng, lambda e: e.drain())

    def emit(self):
        nc = self.nc
        with nc.Block() as block:
            for ename in ENGINES:
                def make_body(name):
                    def body(eng):
                        for waits, fn, inc, n in self.q[name]:
                            for (s, v) in waits:
                                eng.wait_ge(self.sems[s], v)
                            ins = fn(eng)
                            if inc is not None:
                                ins.then_inc(self.sems[inc], n)
                    return body
                getattr(block, ename)(make_body(ename))


def build_program(nt=NT):
    nc = bass.Bass("TRN2", target_bir_lowering=False)

    x_in = nc.dram_tensor("x_in", [nt, 128, DC * T], F32, kind="ExternalInput")
    o_out = nc.dram_tensor("o_out", [nt, 128, DC * T], F32,
                           kind="ExternalOutput")
    w_f32 = {k: nc.dram_tensor(f"{k}_f32", [ng, kq, 128, ch], F32,
                               kind="ExternalInput")
             for k, (ng, kq, ch) in WSPEC.items()}
    w_scr = {k: nc.dram_tensor(f"{k}_scr", [ng, kq, 128, ch], BF16)
             for k, (ng, kq, ch) in WSPEC.items()}
    sc_in = {k: nc.dram_tensor(k, [128, n], F32, kind="ExternalInput")
             for k, n in (("s1c", 32), ("s2c", 32), ("sh1c", 32),
                          ("sh2c", 32), ("s3c", 8))}
    sel16_in = nc.dram_tensor("sel16", [16, 2048], BF16, kind="ExternalInput")
    colsel_in = nc.dram_tensor("colsel", [128, 256], BF16, kind="ExternalInput")

    SB = nc.alloc_sbuf_tensor
    x_res = [SB(f"x{i}", [128, DC * T], F32) for i in range(2)]
    ring = SB("ring", [128, NSLOT * SLOTE], BF16)
    slot = [ring[:, i * SLOTE:(i + 1) * SLOTE] for i in range(NSLOT)]
    A = SB("arnA", [128, HC * T], BF16)
    B = SB("arnB", [128, HC * T], BF16)
    C = SB("arnC", [128, NH * T], BF16)
    lnb = SB("lnb", [128, DC * T], BF16)
    tmp1 = SB("tmp1", [128, 2 * 4 * T], BF16)
    hq = SB("hq", [128, RC * T], BF16)
    rtile = SB("rtile", [128, RC * T], BF16)
    sqs = SB("sqs", [128, 2 * T], BF16)
    fscr = SB("fscr", [128, 2 * T], F32)
    rowscr = SB("rowscr", [128, 3 * T], F32)
    abrow = SB("abrow", [128, 2 * T], BF16)
    bcb = SB("bcb", [128, 2 * T], BF16)
    rstd16 = SB("rstd16", [16, T], BF16)
    rstd16k = SB("rstd16k", [16, T], BF16)
    cos32 = SB("cos32", [32, 2048], F32)
    sin32 = SB("sin32", [32, 2048], F32)
    Ct = SB("Ct", [128, T], BF16)
    St = SB("St", [128, T], BF16)
    onesT = SB("onesT", [128, T], BF16)
    ones_col = SB("ones_col", [128, 1], BF16)
    ones_row = SB("ones_row", [1, 128], BF16)
    sel16 = SB("sel16t", [16, 2048], BF16)
    colsel = SB("colselt", [128, 256], BF16)
    scl = {k: SB(f"scl_{k}", [128, 32], F32) for k in ("s1", "s2", "sh1", "sh2")}
    scl3 = SB("scl_s3", [128, 8], F32)
    s3h = SB("s3h", [128, 8], F32)
    carry = SB("carry", [128, NH], F32)
    consts = SB("consts", [128, 16], F32)

    qb = A[:, 0:NH * T]
    kb = A[:, NH * T:2 * NH * T]
    swp = A[:, 0:NH * T]
    vb = B[:, 0:NH * T]
    qh = B[:, NH * T:2 * NH * T]
    kh = C[:, 0:NH * T]
    cums = B[:, 0:NH * T]
    xb8 = C[:, 0:DC * T]
    hin = C[:, DC * T:2 * DC * T]
    tabf = None

    tabf = x_res[0][0:32, 0:2048]
    psH = [nc.alloc_psum_tensor("psA", [128, 4 * T], F32),
           nc.alloc_psum_tensor("psB", [128, 4 * T], F32)]

    S = Sched(nc)
    cF = [S.sem(f"cF{i}") for i in range(NSLOT)]
    cX = [S.sem(f"cX{i}") for i in range(2)]
    cL = S.sem("cL")
    cO = S.sem("cO")
    cP = S.sem("cP")
    cSW = S.sem("cSW")
    cPE = S.sem("cPE")
    cSC = S.sem("cSC")
    cVE = S.sem("cVE")
    cPO = S.sem("cPO")

    bk = {
        "slot_free": [0] * NSLOT,
        "slot_fill": [0] * NSLOT,
        "slot_uses": [0] * NSLOT,
        "half_free": [(None, 0), (None, 0)],
        "x_out": [0, 0],
        "x_fill": [0, 0],
        "prep": {},
        "ci": 0,
        "rope_done": 0,
    }

    def cc(idx, p=128):
        return consts[0:p, idx:idx + 1]

    def fill(layer, ng, kq, elems, tile0):
        s = bk["ci"] % NSLOT
        bk["ci"] += 1
        waits = []
        if bk["slot_free"][s] > 0:
            waits.append((cPE, bk["slot_free"][s]))
        if tile0:
            waits.append((cP, bk["prep"][layer]))
        src = w_scr[layer][ng, kq]
        dst = slot[s] if elems == SLOTE else slot[s][:, 0:elems]
        m = S.op("sync", lambda e, d=dst, sr=src: e.dma_start(out=d, in_=sr),
                 waits=waits, inc=cF[s], inc_by=16)
        bk["slot_fill"][s] = m
        return s

    def half_wait(h):
        sem, m = bk["half_free"][h]
        return [(sem, m)] if (sem is not None and m > 0) else []

    def mm(out_ap, lhsT_ap, rhs_ap, start, stop, waits=(), inc=None):
        return S.op("tensor",
                    lambda e, o=out_ap, l=lhsT_ap, r=rhs_ap, st=start, sp=stop:
                    e.matmul(o, l, r, start=st, stop=sp),
                    waits=waits, inc=inc)

    def emit_xload(t):
        xi = t % 2
        waits = [(cO, bk["x_out"][xi])] if bk["x_out"][xi] > 0 else []
        if t == 0:
            waits.append((cSC, bk["tabs_sc"]))
        m = S.op("gpsimd", lambda e, ti=t, b=xi: e.dma_start(
            out=x_res[b][:, :], in_=x_in[ti]),
            waits=waits, inc=cX[xi], inc_by=16)
        bk["x_fill"][xi] = m

    def emit_ln(t, mode, xready=None):
        x = x_res[t % 2]
        xw = t // 2 + 1
        hf = bk.get("hin_free", 0)
        if xready is None:
            xready = [(cX[t % 2], 16 * xw)] * 2
        m_cast = None
        m_sq = None
        for hhalf in range(2):
            r0 = hhalf * 4 * T
            S.drain("vector")
            m_cast = S.op("vector", lambda e, r_=r0: e.tensor_copy(
                xb8[:, r_:r_ + 4 * T], x[:, r_:r_ + 4 * T]),
                waits=[xready[hhalf]], inc=cVE)
            S.drain("vector")
            m_sq = S.op("vector", lambda e, r_=r0: e.tensor_tensor(
                hin[:, r_:r_ + 4 * T], xb8[:, r_:r_ + 4 * T],
                xb8[:, r_:r_ + 4 * T], ALU.mult),
                waits=([(cPE, hf)] if (hhalf == 0 and hf > 0) else []),
                inc=cVE)
        w0 = half_wait(0)
        sq_marks = []
        for k in range(DC):
            mm(psH[0][0:1, 0:T], ones_col[:, 0:1], xb8[:, k * T:(k + 1) * T],
               start=(k == 0), stop=(k == DC - 1),
               waits=(w0 + [(cVE, m_cast - 2)]) if k == 0 else
                     ([(cVE, m_cast)] if k == 4 else []))
            msq = mm(psH[0][0:1, T:2 * T], ones_col[:, 0:1],
                     hin[:, k * T:(k + 1) * T],
                     start=(k == 0), stop=(k == DC - 1),
                     waits=[(cVE, m_sq - 2)] if k == 0 else
                           ([(cVE, m_sq)] if k == 4 else []),
                     inc=(cPE if k == DC - 1 else None))
            sq_marks.append(msq)
        bk["hin_free"] = sq_marks[-1]
        r_m = rowscr[0:1, 0:T]
        r_v = rowscr[0:1, T:2 * T]
        r_r = rowscr[0:1, 2 * T:3 * T]
        S.drain("vector")
        S.op("vector", lambda e: e.tensor_scalar(
            r_m, psH[0][0:1, 0:T], 1.0 / D, None, ALU.mult),
            waits=[(cPE, bk["hin_free"])])
        S.drain("vector")
        S.op("vector", lambda e: e.tensor_tensor(r_v, r_m, r_m, ALU.mult))
        S.drain("vector")
        m_var = S.op("vector", lambda e: e.scalar_tensor_tensor(
            r_v, psH[0][0:1, T:2 * T], 1.0 / D, r_v, ALU.mult, ALU.subtract),
            inc=cVE)
        S.op("scalar", lambda e: e.activation(r_r, r_v, AF.Ln,
                                              bias=cc(C_EPSLN, 1)),
             waits=[(cVE, m_var)])
        S.drain("scalar")
        m_ex = S.op("scalar", lambda e: e.activation(
            r_r, r_r, AF.Exp, scale=cc(C_NHALF, 1)), inc=cSC)
        S.drain("vector")
        S.op("vector", lambda e: e.scalar_tensor_tensor(
            rowscr[0:1, T:2 * T], r_m, -1.0, r_r, ALU.mult, ALU.mult),
            waits=[(cSC, m_ex)])
        S.drain("vector")
        S.op("vector", lambda e: e.tensor_copy(abrow[0:1, 0:T], r_r))
        S.drain("vector")
        m_ab = S.op("vector", lambda e: e.tensor_copy(
            abrow[0:1, T:2 * T], rowscr[0:1, T:2 * T]), inc=cVE)
        w1h = half_wait(1)
        mm(psH[1][:, 0:T], ones_row[0:1, :], abrow[0:1, 0:T], True, True,
           waits=w1h + [(cVE, m_ab)])
        m_bc = mm(psH[1][:, T:2 * T], ones_row[0:1, :], abrow[0:1, T:2 * T],
                  True, True, inc=cPE)
        m_ev = S.op("scalar", lambda e: e.activation(
            bcb[:, 0:2 * T], psH[1][:, 0:2 * T], AF.Copy),
            waits=[(cPE, m_bc)], inc=cSC)
        bk["half_free"][1] = (cSC, m_ev)
        hmarks = []
        for hh2 in range(2):
            r0 = hh2 * 4
            xv = xb8.rearrange("p (c tt) -> p c tt", c=DC)[:, r0:r0 + 4, :]
            lv = lnb.rearrange("p (c tt) -> p c tt", c=DC)[:, r0:r0 + 4, :]
            av = bcb[:, 0:T][:, None, :].broadcast_to([128, 4, T])
            bv = bcb[:, T:2 * T][:, None, :].broadcast_to([128, 4, T])
            S.drain("vector")
            S.op("vector", lambda e, l_=lv, x_=xv, a_=av: e.tensor_tensor(
                l_, x_, a_, ALU.mult),
                waits=[(cSC, m_ev)] if hh2 == 0 else [])
            S.drain("vector")
            m = S.op("vector", lambda e, l_=lv, b_=bv: e.tensor_tensor(
                l_, l_, b_, ALU.add), inc=cVE)
            hmarks.append(m)
        last_m = hmarks[-1]
        bk["half_free"][0] = (cVE, m_var)
        bk["lnb_hmarks"] = hmarks
        return last_m

    def mm_group(lay, g, h, kq_n, ch, rhs_buf, rhs_sem, rhs_marks, tile0,
                 extra_waits=()):
        nq = 8
        mbw = ch // nq // 128
        grp_mark = None
        for kq in range(kq_n):
            s = fill(lay, g, kq, ch, tile0)
            waits = [(cF[s], bk["slot_fill"][s])] + list(extra_waits if kq == 0
                                                         else [])
            if kq == 0:
                waits += half_wait(h)
            waited = 0
            for j in range(nq):
                k = kq * nq + j
                if rhs_marks is not None:
                    mkk = rhs_marks[min(k, len(rhs_marks) - 1)]
                    if mkk > waited:
                        waits.append((rhs_sem, mkk))
                        waited = mkk
                rhs = rhs_buf[:, k * T:(k + 1) * T]
                for ml in range(mbw):
                    is_chunk_last = (j == nq - 1 and ml == mbw - 1)
                    lhsT = slot[s][:, j * mbw * 128 + ml * 128:
                                   j * mbw * 128 + (ml + 1) * 128]
                    mk = mm(psH[h][:, ml * T:(ml + 1) * T], lhsT, rhs,
                            start=(kq == 0 and j == 0),
                            stop=(kq == kq_n - 1 and j == nq - 1),
                            waits=waits,
                            inc=(cPE if is_chunk_last else None))
                    waits = []
            bk["slot_free"][s] = mk
            grp_mark = mk
        return grp_mark

    def ffn_w1w2(t, tile0, hm):
        lnmarks = [hm[0]] * 4 + [hm[1]] * 4
        out_marks = []
        silu_mark = {}
        for q0 in range(0, 8, 2):
            for g in (q0, q0 + 1):
                h = g % 2
                gm = mm_group("w1", g, h, 1, 4096, lnb, cVE, lnmarks,
                              tile0)
                lastc = None
                for ml in range(4):
                    mch = g * 4 + ml
                    w_t = []
                    if ml == 0:
                        w_t = [(cPE, gm)]
                        tf = bk.get(f"tmp1_free{h}", 0)
                        if tf > 0:
                            w_t.append((cVE, tf))
                    lastc = S.op("scalar", lambda e, h_=h, ml_=ml, m_=mch:
                                 e.activation(
                        tmp1[:, (h_ * 4 + ml_) * T:(h_ * 4 + ml_ + 1) * T],
                        psH[h_][:, ml_ * T:(ml_ + 1) * T], AF.Silu,
                        scale=scl["s1"][:, m_:m_ + 1]),
                        waits=w_t, inc=(cSC if ml == 3 else None))
                silu_mark[g] = lastc
                bk["half_free"][h] = (cSC, lastc)
            for g in (q0, q0 + 1):
                h = g % 2
                gm = mm_group("w2", g, h, 1, 4096, lnb, cVE, lnmarks,
                              tile0)
                lastc = None
                for ml in range(4):
                    mch = g * 4 + ml
                    if ml == 0:
                        S.drain("vector")
                    lastc = S.op("vector", lambda e, h_=h, ml_=ml, m_=mch:
                                 e.scalar_tensor_tensor(
                        A[:, m_ * T:(m_ + 1) * T],
                        psH[h_][:, ml_ * T:(ml_ + 1) * T],
                        scl["s2"][:, m_:m_ + 1],
                        tmp1[:, (h_ * 4 + ml_) * T:(h_ * 4 + ml_ + 1) * T],
                        ALU.mult, ALU.mult),
                        waits=([(cPE, gm), (cSC, silu_mark[g])]
                               if ml == 0 else []),
                        inc=(cVE if ml == 3 else None))
                out_marks += [lastc] * 4
                bk["half_free"][h] = (cVE, lastc)
                bk[f"tmp1_free{h}"] = lastc
        return out_marks

    def ffn_layer(t, layer, rhs_buf, out_kind, out_buf, scale_sb,
                  rhs_sem, rhs_marks, act, tile0):
        ng, kq_n, ch = WSPEC[layer]
        mbw = ch // 8 // 128
        out_marks = []
        for g in range(ng):
            h = g % 2
            grp_mark = mm_group(layer, g, h, kq_n, ch, rhs_buf, rhs_sem,
                                rhs_marks, tile0)
            if out_kind == "gelu":
                lastc = None
                for ml in range(mbw):
                    mch = g * mbw + ml
                    lastc = S.op("scalar", lambda e, h_=h, ml_=ml, m_=mch:
                                 e.activation(
                        out_buf[:, m_ * T:(m_ + 1) * T],
                        psH[h_][:, ml_ * T:(ml_ + 1) * T], act,
                        scale=scale_sb[:, m_:m_ + 1]),
                        waits=[(cPE, grp_mark)] if ml == 0 else [],
                        inc=(cSC if ml == mbw - 1 else None))
                out_marks += [lastc] * mbw
                bk["half_free"][h] = (cSC, lastc)
            elif out_kind == "resid":
                lastc = None
                for ml in range(mbw):
                    mch = g * mbw + ml
                    if ml == 0:
                        S.drain("vector")
                    lastc = S.op("vector", lambda e, h_=h, ml_=ml, m_=mch:
                                 e.scalar_tensor_tensor(
                        x_res[t % 2][:, m_ * T:(m_ + 1) * T],
                        psH[h_][:, ml_ * T:(ml_ + 1) * T],
                        s3h[:, m_:m_ + 1],
                        x_res[t % 2][:, m_ * T:(m_ + 1) * T],
                        ALU.mult, ALU.add),
                        waits=[(cPE, grp_mark)] if ml == 0 else [],
                        inc=(cVE if ml == mbw - 1 else None))
                    out_marks.append(lastc)
                bk["half_free"][h] = (cVE, lastc)
        return out_marks

    def emit_ffn(t, tile0, lnb_mark):
        hm = bk.get("lnb_hmarks", [lnb_mark, lnb_mark])
        a_marks = ffn_w1w2(t, tile0, hm)
        b_marks = ffn_layer(t, "h1", A, "gelu", B, scl["sh1"],
                            cVE, a_marks, AF.Gelu, tile0)
        a2_marks = ffn_layer(t, "h2", B, "gelu", A, scl["sh2"],
                             cSC, b_marks, AF.Gelu, tile0)
        r_marks = ffn_layer(t, "w3", A, "resid", None, None,
                            cSC, a2_marks, None, tile0)
        return r_marks

    def emit_rms_qkv1(t, tile0, xlnb_mark):
        w0 = half_wait(0)
        hf = bk.get("hin_free", 0)
        S.drain("vector")
        m_sq = S.op("vector", lambda e: e.tensor_tensor(
            hin[:, :], lnb[:, :], lnb[:, :], ALU.mult),
            waits=([(cVE, xlnb_mark)] + ([(cPE, hf)] if hf > 0 else [])),
            inc=cVE)
        marks = []
        for k in range(DC):
            msq = mm(psH[0][0:1, 0:T], ones_col[:, 0:1],
                     hin[:, k * T:(k + 1) * T],
                     start=(k == 0), stop=(k == DC - 1),
                     waits=(w0 + [(cVE, m_sq)]) if k == 0 else [],
                     inc=(cPE if k == DC - 1 else None))
            marks.append(msq)
        bk["hin_free"] = marks[-1]
        r_r = rowscr[0:1, 0:T]
        S.op("scalar", lambda e: e.activation(
            r_r, psH[0][0:1, 0:T], AF.Ln, scale=cc(C_I1024, 1),
            bias=cc(C_EPSRMS, 1)), waits=[(cPE, marks[-1])])
        S.drain("scalar")
        m_ex = S.op("scalar", lambda e: e.activation(
            r_r, r_r, AF.Exp, scale=cc(C_NHALF, 1)), inc=cSC)
        S.drain("vector")
        m_ab = S.op("vector", lambda e: e.tensor_copy(abrow[0:1, 0:T], r_r),
                    waits=[(cSC, m_ex)], inc=cVE)
        w1h = half_wait(1)
        m_bc = mm(psH[1][:, 0:T], ones_row[0:1, :], abrow[0:1, 0:T], True,
                  True, waits=w1h + [(cVE, m_ab)], inc=cPE)
        bk["half_free"][0] = (cSC, m_ex)
        m_evr = S.op("scalar", lambda e: e.activation(
            bcb[:, 0:T], psH[1][:, 0:T], AF.Copy),
            waits=[(cPE, m_bc)], inc=cSC)
        bk["half_free"][1] = (cSC, m_evr)
        rv_ = bcb[:, 0:T][:, None, :].broadcast_to([128, DC, T])
        S.drain("vector")
        m_hin = S.op("vector", lambda e: e.tensor_tensor(
            hin.rearrange("p (c tt) -> p c tt", c=DC),
            lnb.rearrange("p (c tt) -> p c tt", c=DC), rv_, ALU.mult),
            waits=[(cSC, m_evr), (cPE, bk["hin_free"])], inc=cVE)
        s = fill("qw1", 0, 0, 2048, tile0)
        waits = [(cF[s], bk["slot_fill"][s]), (cVE, m_hin)] + half_wait(1)
        m_q1 = None
        for j in range(DC):
            for ml in range(2):
                m_q1 = mm(psH[1][:, ml * T:(ml + 1) * T],
                          slot[s][:, j * 256 + ml * 128:
                                  j * 256 + (ml + 1) * 128],
                          hin[:, j * T:(j + 1) * T],
                          start=(j == 0), stop=(j == DC - 1),
                          waits=waits,
                          inc=(cPE if (j == DC - 1 and ml == 1) else None))
                waits = []
        bk["slot_free"][s] = m_q1
        bk["hin_free"] = m_q1
        m_r = S.op("scalar", lambda e: e.activation(
            rtile[:, 0:2 * T], psH[1][:, 0:2 * T], AF.Relu),
            waits=[(cPE, m_q1)], inc=cSC)
        bk["half_free"][1] = (cSC, m_r)
        S.drain("vector")
        m_hq = S.op("vector", lambda e: e.tensor_tensor(
            hq[:, :], rtile[:, 0:2 * T], rtile[:, 0:2 * T], ALU.mult),
            waits=[(cSC, m_r)], inc=cVE)
        return m_hq

    def emit_attn_front(t, tile0, hq_mark):
        fsb = lnb[:, 0:4 * T]
        arena = {0: qb, 1: kb, 2: vb}
        ev_marks = []
        st_busy = {"q": 0, "k": 0}
        res = {}

        def stats_batch(which, j, src_, ev_need):
            sth = 0 if which == "q" else 1
            S.drain("vector")
            mq = S.op("vector", lambda e, j_=j, sb=src_: e.tensor_tensor(
                fsb, sb[:, j_ * 4 * T:(j_ + 1) * 4 * T],
                sb[:, j_ * 4 * T:(j_ + 1) * 4 * T], ALU.mult),
                waits=[(cSC, ev_need)] +
                      ([(cPE, st_busy[which])] if st_busy[which] else []),
                inc=cVE)
            m_st = None
            for i in range(4):
                hg = 4 * j + i
                m_st = mm(psH[sth][0:16, 2 * T:3 * T],
                          colsel[:, hg * 16:(hg + 1) * 16],
                          fsb[:, i * T:(i + 1) * T],
                          start=(hg == 0), stop=(hg == NH - 1),
                          waits=([(cVE, mq)] if i == 0 else []) +
                                (half_wait(sth) if hg == 0 else []),
                          inc=(cPE if i == 3 else None))
            st_busy[which] = m_st
            return m_st

        def rows_emit(which):
            sth = 0 if which == "q" else 1
            m_st = st_busy[which]
            S.op("scalar", lambda e, h_=sth: e.activation(
                rowscr[0:16, 0:T], psH[h_][0:16, 2 * T:3 * T], AF.Ln,
                scale=cc(C_I128, 16), bias=cc(C_EPSRMS, 16)),
                waits=[(cPE, m_st)])
            S.drain("scalar")
            m_ex = S.op("scalar", lambda e: e.activation(
                rowscr[0:16, 0:T], rowscr[0:16, 0:T], AF.Exp,
                scale=cc(C_NHALF, 16)), inc=cSC)
            S.drain("vector")
            rdst = rstd16[0:16, 0:T] if which == "q" else rstd16k[0:16, 0:T]
            m_r16 = S.op("vector", lambda e, r_=rdst: e.tensor_copy(
                r_, rowscr[0:16, 0:T]), waits=[(cSC, m_ex)], inc=cVE)
            return m_r16

        mult_state = {"bank": 0, "mark": {}, "last": {}}

        def mult_pair(which, hg, src_, dst, m_r16):
            sth = 0 if which == "q" else 1
            bnk = 2 + mult_state["bank"] % 2
            mult_state["bank"] += 1
            wts = [(cVE, m_r16)]
            pm = mult_state["mark"].get((sth, bnk))
            if pm:
                wts.append((cVE, pm))
            rsrc = rstd16[0:16, 0:T] if which == "q" else rstd16k[0:16, 0:T]
            m_bc = mm(psH[sth][:, bnk * T:(bnk + 1) * T],
                      sel16[0:16, hg * 128:(hg + 1) * 128],
                      rsrc, True, True, waits=wts, inc=cPE)
            S.drain("vector")
            m_mu = S.op("vector", lambda e, g_=hg, sb=src_, db=dst, h_=sth,
                        b_=bnk: e.tensor_tensor(
                db[:, g_ * T:(g_ + 1) * T], sb[:, g_ * T:(g_ + 1) * T],
                psH[h_][:, b_ * T:(b_ + 1) * T], ALU.mult),
                waits=[(cPE, m_bc)], inc=cVE)
            mult_state["mark"][(sth, bnk)] = m_mu
            mult_state["last"][which] = m_mu
            return m_mu

        gm = [0, 0]
        pending = []
        for g in range(24):
            hh = g % 2
            s = fill("qw2", g, 0, 512, tile0)
            waits = [(cF[s], bk["slot_fill"][s])]
            if g == 0:
                waits.append((cVE, hq_mark))
            if g in (0, 1):
                waits += half_wait(hh)
            else:
                waits += [(cSC, ev_marks[g - 2])]
            m_g = None
            for j in range(RC):
                for ml in range(2):
                    m_g = mm(psH[hh][:, ml * T:(ml + 1) * T],
                             slot[s][:, j * 256 + ml * 128:
                                     j * 256 + (ml + 1) * 128],
                             hq[:, j * T:(j + 1) * T],
                             start=(j == 0), stop=(j == RC - 1),
                             waits=waits,
                             inc=(cPE if (j == RC - 1 and ml == 1) else None))
                    waits = []
            bk["slot_free"][s] = m_g
            gm[hh] = m_g
            part = arena[g // 8]
            off = (g % 8) * 2 * T
            m_ev = S.op("scalar", lambda e, h_=hh, p_=part, o_=off:
                        e.activation(p_[:, o_:o_ + 2 * T],
                                     psH[h_][:, 0:2 * T], AF.Copy),
                        waits=[(cPE, m_g)], inc=cSC)
            ev_marks.append(m_ev)
            if g in (1, 3, 5, 7):
                stats_batch("q", g // 2, qb, m_ev)
            if g in (9, 11, 13, 15):
                stats_batch("k", (g - 8) // 2, kb, m_ev)
            if g == 9:
                m_r16q = rows_emit("q")
                pending += [("q", h_, qb, qh, m_r16q) for h_ in range(NH)]
            if g == 17:
                m_r16k = rows_emit("k")
                pending += [("k", h_, kb, kh, m_r16k) for h_ in range(NH)]
            for _ in range(2):
                if pending:
                    mult_pair(*pending.pop(0))
        while pending:
            mult_pair(*pending.pop(0))
        res["mq_hat"] = mult_state["last"]["q"]
        res["mk_hat"] = mult_state["last"]["k"]
        bk["half_free"][0] = (cSC, ev_marks[-2] if len(ev_marks) >= 2 else 0)
        bk["half_free"][1] = (cSC, ev_marks[-1])
        bk["bank23_free"] = (cVE, res["mk_hat"])
        return res["mq_hat"], res["mk_hat"]

    def emit_cs_tiles(t):
        cosv, sinv = sin32, cos32
        p0 = (t % TPS) * T
        S.drain("vector")
        S.op("vector", lambda e: e.tensor_copy(Ct[0:32, :],
                                               cosv[0:32, p0:p0 + T]),
             waits=[(cPO, bk["tabs_po"]), (cSC, bk["tabs_sc"])])
        S.op("vector", lambda e: e.tensor_copy(Ct[64:96, :],
                                               cosv[0:32, p0:p0 + T]))
        S.op("vector", lambda e: e.tensor_copy(St[0:32, :],
                                               sinv[0:32, p0:p0 + T]))
        m3 = S.op("vector", lambda e: e.tensor_scalar(
            St[64:96, :], sinv[0:32, p0:p0 + T], -1.0, None, ALU.mult),
            inc=cVE)
        return m3

    def hv(buf, o, L):
        return buf.rearrange("p (h tt) -> p h tt", h=NH)[:, :, o:o + L]

    def emit_attn(t, tile0, mq_hat, mk_hat, cs_mark):
        HL = T // 2

        def rope(buf, o, mark_in, sw_wait):
            bv3 = buf.rearrange("p (h tt) -> p h tt", h=NH)
            sv3 = swp.rearrange("p (h tt) -> p h tt", h=NH)
            S.op("scalar", lambda e: e.dma_start(
                out=sv3[0:32, :, o:o + HL], in_=bv3[64:96, :, o:o + HL]),
                waits=[(cVE, mark_in), (cSW, S.cnt["cSW"])] + sw_wait,
                inc=cSW, inc_by=16)
            m_sw = S.op("scalar", lambda e: e.dma_start(
                out=sv3[64:96, :, o:o + HL], in_=bv3[0:32, :, o:o + HL]),
                inc=cSW, inc_by=16)
            bv = bv3[:, :, o:o + HL]
            sv = sv3[:, :, o:o + HL]
            cvw = Ct[:, o:o + HL][:, None, :].broadcast_to([128, NH, HL])
            svw = St[:, o:o + HL][:, None, :].broadcast_to([128, NH, HL])
            S.drain("vector")
            S.op("vector", lambda e: e.tensor_tensor(bv, bv, cvw, ALU.mult),
                 waits=[(cVE, cs_mark), (cSW, m_sw)])
            S.drain("vector")
            S.op("vector", lambda e: e.tensor_tensor(sv, sv, svw, ALU.mult))
            S.drain("vector")
            m = S.op("vector", lambda e: e.tensor_tensor(bv, bv, sv, ALU.add),
                     inc=cVE)
            return m

        m_y = [None, None]
        cp_state = {"hh": 0}
        sv3f = swp.rearrange("p (h tt) -> p h tt", h=NH)
        bv3kf = kh.rearrange("p (h tt) -> p h tt", h=NH)
        swl = bk.get("swp_last", 0)
        S.op("scalar", lambda e: e.dma_start(
            out=swp[0:32, :], in_=kh[64:96, :]),
            waits=[(cVE, mk_hat), (cSW, S.cnt["cSW"])] +
                  ([(cVE, swl)] if swl else []),
            inc=cSW, inc_by=16)
        m_kswap = S.op("scalar", lambda e: e.dma_start(
            out=swp[64:96, :], in_=kh[0:32, :]), inc=cSW, inc_by=16)
        for o, i in ((0, 0), (HL, 1)):
            bv3q = qh.rearrange("p (h tt) -> p h tt", h=NH)
            bv3k = kh.rearrange("p (h tt) -> p h tt", h=NH)
            sv3 = swp.rearrange("p (h tt) -> p h tt", h=NH)
            cvw = Ct[:, o:o + HL][:, None, :].broadcast_to([128, NH, HL])
            svw = St[:, o:o + HL][:, None, :].broadcast_to([128, NH, HL])

            def swap_dma(bv3, dep):
                d1 = sv3[0:32, :, o:o + HL]
                s1 = bv3[64:96, :, o:o + HL]
                d2 = sv3[64:96, :, o:o + HL]
                s2 = bv3[0:32, :, o:o + HL]
                S.op("scalar", lambda e, d_=d1, s_=s1: e.dma_start(
                    out=d_, in_=s_),
                    waits=[(cVE, dep), (cSW, S.cnt["cSW"])],
                    inc=cSW, inc_by=16)
                return S.op("scalar", lambda e, d_=d2, s_=s2: e.dma_start(
                    out=d_, in_=s_), inc=cSW, inc_by=16)

            def rope3(bv3, m_sw, extra=()):
                bv = bv3[:, :, o:o + HL]
                sv = sv3[:, :, o:o + HL]
                S.drain("vector")
                S.op("vector", lambda e, b=bv, c_=cvw: e.tensor_tensor(
                    b, b, c_, ALU.mult),
                    waits=[(cVE, cs_mark), (cSW, m_sw)] + list(extra))
                S.drain("vector")
                S.op("vector", lambda e, s_=sv, w_=svw: e.tensor_tensor(
                    s_, s_, w_, ALU.mult))
                S.drain("vector")
                return S.op("vector", lambda e, b=bv, s_=sv: e.tensor_tensor(
                    b, b, s_, ALU.add), inc=cVE)

            m_k = rope3(bv3k, m_kswap)
            m_sg = S.op("scalar", lambda e, o_=o: e.activation(
                hv(kh, o_, HL), hv(kh, o_, HL), AF.Sigmoid),
                waits=[(cVE, m_k)], inc=cSC)
            m_swq = swap_dma(bv3q, max(m_k, mq_hat))
            m_rq = rope3(bv3q, m_swq)
            S.drain("vector")
            m_ropeq = S.op("vector", lambda e, o_=o: e.tensor_scalar(
                hv(qh, o_, HL), hv(qh, o_, HL), cc(C_ZERO), None, ALU.max),
                inc=cVE)
            if i == 1:
                bk["swp_last"] = m_rq
            if i == 1 and cp_state.get("pending") is not None:
                emit_cproj_uv(t, cp_state.pop("pending"), cp_state)
            S.drain("vector")
            m_s = S.op("vector", lambda e, o_=o: e.tensor_tensor(
                hv(kh, o_, HL), hv(kh, o_, HL), hv(vb, o_, HL), ALU.mult),
                waits=[(cSC, m_sg)], inc=cVE)
            seq_start = (t % TPS) == 0
            S.drain("vector")
            for hg in range(NH):
                if i == 0:
                    init = 0.0 if seq_start else carry[:, hg:hg + 1]
                else:
                    init = cums[:, hg * T + HL - 1:hg * T + HL]
                S.op("vector", lambda e, g_=hg, ini=init, o_=o:
                     e.tensor_tensor_scan(
                    cums[:, g_ * T + o_:g_ * T + o_ + HL], onesT[:, 0:HL],
                    kh[:, g_ * T + o_:g_ * T + o_ + HL], ini,
                    ALU.mult, ALU.add),
                    waits=[(cVE, m_s)] if hg == 0 else [])
            S.drain("vector")
            if i == 1:
                cv = cums.rearrange("p (h tt) -> p h tt", h=NH)[:, :, T - 1:T]
                S.op("vector", lambda e: e.tensor_copy(
                    carry[:, 0:NH][:, :, None], cv))
                S.drain("vector")
            m_y[i] = S.op("vector", lambda e, o_=o: e.tensor_tensor(
                hv(qh, o_, HL), hv(qh, o_, HL), hv(cums, o_, HL), ALU.mult),
                inc=cVE)
            info = emit_cproj_mms(t, tile0, o, m_y[i], cp_state)
            if i == 0:
                cp_state["pending"] = info
            else:
                emit_cproj_uv(t, info, cp_state)
        return cp_state

    def emit_cproj_mms(t, tile0, o, y_mark, st):
        HL = T // 2
        s = fill("cw1", 0, 0, 4096, tile0)
        hA = st["hh"]
        st["hh"] ^= 1
        waits = [(cF[s], bk["slot_fill"][s]), (cVE, y_mark)] + half_wait(hA)
        b23 = bk.get("bank23_free")
        if b23:
            waits.append(b23)
            bk["bank23_free"] = None
        m_c1 = None
        for j in range(NH):
            for ml in range(2):
                m_c1 = mm(psH[hA][:, ml * T:ml * T + HL],
                          slot[s][:, j * 256 + ml * 128:
                                  j * 256 + (ml + 1) * 128],
                          qh[:, j * T + o:j * T + o + HL],
                          start=(j == 0), stop=(j == NH - 1), waits=waits,
                          inc=(cPE if (j == NH - 1 and ml == 1) else None))
                waits = []
        bk["slot_free"][s] = m_c1
        m_rt = S.op("scalar", lambda e, h_=hA, o_=o: e.activation(
            rtile.rearrange("p (c tt) -> p c tt", c=2)[:, :, o_:o_ + HL],
            psH[h_][:, 0:2 * T].rearrange(
                "p (c tt) -> p c tt", c=2)[:, :, 0:HL],
            AF.Copy), waits=[(cPE, m_c1)], inc=cSC)
        bk["half_free"][hA] = (cSC, m_rt)
        hB = st["hh"]
        st["hh"] ^= 1
        info = {"o": o, "grp": [], "halves": (hA, hB)}
        for ci in range(4):
            s2 = fill("cw2", ci, 0, 1024, tile0)
            hh = (hB, hB, hA, hA)[ci]
            col = 256 * (ci % 2)
            waits = [(cF[s2], bk["slot_fill"][s2]), (cSC, m_rt)]
            if ci % 2 == 0:
                waits += half_wait(hh)
            m_g = None
            for j in range(RC):
                for ml in range(4):
                    m_g = mm(psH[hh][:, ml * T + col:ml * T + col + HL],
                             slot[s2][:, j * 512 + ml * 128:
                                     j * 512 + (ml + 1) * 128],
                             rtile[:, j * T + o:j * T + o + HL],
                             start=(j == 0), stop=(j == RC - 1), waits=waits,
                             inc=(cPE if (j == RC - 1 and ml == 3) else None))
                    waits = []
            bk["slot_free"][s2] = m_g
            info["grp"].append((hh, col, m_g))
        info["hB_done"] = info["grp"][1][2]
        info["all_done"] = info["grp"][-1][2]
        return info

    def emit_cproj_uv(t, info, st):
        HL = T // 2
        o = info["o"]
        x = x_res[t % 2]
        hA, hB = info["halves"]
        lastc = None
        uv_marks = []
        for ci in range(4):
            hh, col, m_g = info["grp"][ci]
            gview = psH[hh].rearrange("p (c tt) -> p c tt", c=4)[
                :, 1::2, col:col + HL]
            uview = psH[hh].rearrange("p (c tt) -> p c tt", c=4)[
                :, 0::2, col:col + HL]
            sqv = sqs.rearrange("p (c tt) -> p c tt", c=2)[:, :, 0:HL]
            fsv = fscr.rearrange("p (c tt) -> p c tt", c=2)[:, :, 0:HL]
            w_sl = []
            if ci == 0:
                w_sl.append((cPE, info["hB_done"]))
            elif ci == 2:
                w_sl.append((cPE, info["all_done"]))
            uvf = bk.get("uv_free", 0)
            if uvf > 0:
                w_sl.append((cVE, uvf))
            m_sl = S.op("scalar", lambda e, g_=gview: e.activation(
                sqv, g_, AF.Silu), waits=w_sl, inc=cSC)
            S.drain("vector")
            S.op("vector", lambda e, u_=uview: e.tensor_tensor(
                fsv, u_, sqv, ALU.mult),
                waits=[(cSC, m_sl)] + ([(cVE, uvf)] if uvf > 0 else []))
            S.drain("vector")
            xv = x.rearrange("p (c tt) -> p c tt", c=DC)[
                :, 2 * ci:2 * ci + 2, o:o + HL]
            lastc = S.op("vector", lambda e, x_=xv: e.tensor_tensor(
                x_, x_, fsv, ALU.add), inc=cVE)
            bk["uv_free"] = lastc
            uv_marks.append(lastc)
            if ci == 1:
                bk["half_free"][hB] = (cVE, lastc)
        bk["half_free"][hA] = (cVE, lastc)
        st["last"] = lastc
        st["uv_marks"] = uv_marks

    def emit_out(t, done_mark):
        xi = t % 2
        m = S.op("scalar", lambda e, ti=t, b=xi: e.dma_start(
            out=o_out[ti], in_=x_res[b][:, :]),
            waits=[(cVE, done_mark), (cO, S.cnt["cO"])], inc=cO, inc_by=16)
        bk["x_out"][xi] = m

    def emit_startup():
        S.op("gpsimd", lambda e: e.memset(ones_col[:, :], 1.0))
        S.op("gpsimd", lambda e: e.memset(ones_row[:, :], 1.0))
        S.op("gpsimd", lambda e: e.memset(onesT[:, :], 1.0))
        S.op("gpsimd", lambda e: e.memset(carry[:, :], 0.0))
        for idx, v in CONSTV.items():
            S.op("gpsimd", lambda e, i=idx, vv=float(v): e.memset(
                consts[:, i:i + 1], vv))
        S.op("gpsimd", lambda e: e.memset(Ct[32:64, :], 1.0))
        S.op("gpsimd", lambda e: e.memset(Ct[64:128, :], 1.0))
        S.op("gpsimd", lambda e: e.memset(St[32:64, :], 0.0))
        S.op("gpsimd", lambda e: e.memset(St[64:128, :], 0.0))
        S.op("gpsimd", lambda e: e.iota(cos32[0:32, :], [[1, 2048]],
                                        channel_multiplier=0,
                                        allow_small_or_imprecise_dtypes=True))
        m_po = S.op("gpsimd", lambda e: e.iota(
            sin32[0:32, 0:1], [[1, 1]], channel_multiplier=1,
            allow_small_or_imprecise_dtypes=True), inc=cPO)
        bk["tabs_po"] = m_po
        nload = 0
        for dst, src in [(scl["s1"], sc_in["s1c"]), (scl["s2"], sc_in["s2c"]),
                         (scl["sh1"], sc_in["sh1c"]),
                         (scl["sh2"], sc_in["sh2c"]), (scl3, sc_in["s3c"]),
                         (sel16, sel16_in), (colsel, colsel_in)]:
            S.op("sync", lambda e, d=dst, s_=src: e.dma_start(out=d[:, :],
                                                              in_=s_[:, :]),
                 inc=cL, inc_by=16)
            nload += 1
        m_ld = 16 * nload
        S.drain("vector")
        S.op("vector", lambda e: e.tensor_scalar(
            s3h[:, :], scl3[:, :], cc(C_HALF), None, ALU.mult),
            waits=[(cL, m_ld), (cPO, m_po)])
        m_fr = S.op("scalar", lambda e: e.activation(
            rowscr[0:32, 0:1], sin32[0:32, 0:1], AF.Exp,
            scale=cc(C_FRQ, 32)), waits=[(cPO, m_po)], inc=cSC)
        S.drain("vector")
        S.op("vector", lambda e: e.tensor_scalar(
            cos32[0:32, :], cos32[0:32, :], rowscr[0:32, 0:1], None, ALU.mult),
            waits=[(cSC, m_fr)])
        S.drain("vector")
        S.op("vector", lambda e: e.tensor_scalar(
            sin32[0:32, :], cos32[0:32, :], cc(C_PIH, 32), None, ALU.add))
        for buf in (sin32, cos32):
            S.drain("vector")
            S.op("vector", lambda e, b=buf: e.tensor_scalar(
                tabf, b[0:32, :], cc(C_I2PI, 32), cc(C_MAGIC, 32),
                ALU.mult, ALU.add))
            S.drain("vector")
            S.op("vector", lambda e: e.tensor_scalar(
                tabf, tabf, cc(C_MAGIC, 32), None, ALU.subtract))
            S.drain("vector")
            S.op("vector", lambda e, b=buf: e.scalar_tensor_tensor(
                b[0:32, :], tabf, cc(C_N2PI, 32), b[0:32, :],
                ALU.mult, ALU.add))
            S.drain("vector")
            S.op("vector", lambda e, b=buf: e.tensor_scalar(
                b[0:32, :], b[0:32, :], cc(C_PI, 32), None, ALU.min))
            S.drain("vector")
            S.op("vector", lambda e, b=buf: e.tensor_scalar(
                b[0:32, :], b[0:32, :], cc(C_NPI, 32), None, ALU.max))
        S.drain("vector")
        m_v = S.op("vector", lambda e: e.engine_nop(), inc=cVE)
        S.op("scalar", lambda e: e.activation(sin32[0:32, :], sin32[0:32, :],
                                              AF.Sin), waits=[(cVE, m_v)])
        S.drain("scalar")
        m_tab = S.op("scalar", lambda e: e.activation(
            cos32[0:32, :], cos32[0:32, :], AF.Sin), inc=cSC)
        bk["tabs_sc"] = m_tab
        order = ["w1", "w2", "h1", "h2", "w3", "qw1", "qw2", "cw1", "cw2"]
        for i, lay in enumerate(order):
            waits = [(cP, 16 * i)] if i > 0 else []
            m = S.op("gpsimd", lambda e, l=lay: e.dma_start(
                out=w_scr[l][:], in_=w_f32[l][:]),
                waits=waits, inc=cP, inc_by=16)
            bk["prep"][lay] = m

    emit_startup()
    for _e in ("vector", "scalar", "tensor"):
        S.op(_e, lambda e: e.nop(),
             waits=[(cPO, bk["tabs_po"]), (cL, 16 * 7)])
    emit_xload(0)
    for t in range(nt):
        tile0 = (t == 0)
        if t + 1 < nt:
            emit_xload(t + 1)
        m_ln1 = emit_ln(t, "ln")
        f1_marks = emit_ffn(t, tile0, m_ln1)
        m_ln2 = emit_ln(t, "ln2", xready=[(cVE, f1_marks[3]),
                                          (cVE, f1_marks[7])])
        m_cs = emit_cs_tiles(t)
        m_hq = emit_rms_qkv1(t, tile0, m_ln2)
        S.drain("vector")
        S.op("vector", lambda e, b=t % 2: e.tensor_tensor(
            x_res[b][:, :], x_res[b][:, :], lnb[:, :], ALU.add))
        m_qhat, m_khat = emit_attn_front(t, tile0, m_hq)
        cp_state_holder = emit_attn(t, tile0, m_qhat, m_khat, m_cs)
        uvm = cp_state_holder.get("uv_marks")
        m_ln3 = emit_ln(t, "ln", xready=[(cVE, uvm[1]), (cVE, uvm[3])])
        m_f2 = emit_ffn(t, tile0, m_ln3)
        emit_out(t, m_f2[-1])
    S.op("scalar", lambda e: e.nop(), waits=[(cO, 16 * nt)])
    S.emit()
    return nc


def _pack_x(x_full):
    cores = []
    for c in range(NCORE):
        toks = np.ascontiguousarray(x_full[2 * c:2 * c + 2]).reshape(
            2 * 2048, D)
        tiles = []
        for t in range(NT):
            blk = toks[t * T:(t + 1) * T]
            fm = np.ascontiguousarray(blk.T).reshape(DC, 128, T)
            fm = np.ascontiguousarray(fm.transpose(1, 0, 2)).reshape(
                128, DC * T)
            tiles.append(fm)
        cores.append(np.ascontiguousarray(np.stack(tiles)))
    return cores


def _unpack_out(tiles_by_core):
    out = np.empty((16, 2048, D), dtype=np.float32)
    for c in range(NCORE):
        toks = np.empty((2 * 2048, D), dtype=np.float32)
        for t in range(NT):
            fm = np.asarray(tiles_by_core[c][t]).reshape(
                128, DC, T).transpose(1, 0, 2)
            toks[t * T:(t + 1) * T] = fm.reshape(D, T).T
        out[2 * c:2 * c + 2] = toks.reshape(2, 2048, D)
    return out


def _block_w(w, MB, NQ):
    O, I = w.shape
    KC_, MC_ = I // 128, O // 128
    NG, KQ = MC_ // MB, KC_ // NQ
    lhsT = np.ascontiguousarray(w.T)
    a = lhsT.reshape(KQ, NQ, 128, NG, MB * 128)
    a = a.transpose(3, 0, 2, 1, 4)
    return np.ascontiguousarray(a.reshape(NG, KQ, 128, NQ * MB * 128))


def _scale_cols(s):
    n = s.shape[0] // 128
    return np.ascontiguousarray(s.reshape(n, 128).T)


def prep_inputs(L0_attn_qkv_w1, L0_attn_qkv_w2, L0_attn_cproj_w1,
                L0_attn_cproj_w2, L0_ffn_w1_w, L0_ffn_w1_s, L0_ffn_w2_w,
                L0_ffn_w2_s, L0_ffn_h1_w, L0_ffn_h1_s, L0_ffn_h2_w,
                L0_ffn_h2_s, L0_ffn_w3_w, L0_ffn_w3_s):
    qw2p = np.asarray(L0_attn_qkv_w2, dtype=np.float32)
    cw2 = np.asarray(L0_attn_cproj_w2, dtype=np.float32)
    perm2 = []
    for i in range(8):
        perm2 += [i, 8 + i]
    cw2p = cw2.reshape(16, 128, DR)[perm2].reshape(16 * 128, DR)

    base = {
        "w1_f32": _block_w(np.asarray(L0_ffn_w1_w, np.float32), 4, 8),
        "w2_f32": _block_w(np.asarray(L0_ffn_w2_w, np.float32), 4, 8),
        "h1_f32": _block_w(np.asarray(L0_ffn_h1_w, np.float32), 4, 8),
        "h2_f32": _block_w(np.asarray(L0_ffn_h2_w, np.float32), 4, 8),
        "w3_f32": _block_w(np.asarray(L0_ffn_w3_w, np.float32), 4, 8),
        "qw1_f32": _block_w(np.asarray(L0_attn_qkv_w1, np.float32), 2, 8),
        "qw2_f32": _block_w(qw2p, 2, 2),
        "cw1_f32": _block_w(np.asarray(L0_attn_cproj_w1, np.float32), 2, 16),
        "cw2_f32": _block_w(cw2p, 4, 2),
        "s1c": _scale_cols(np.asarray(L0_ffn_w1_s, np.float32)),
        "s2c": _scale_cols(np.asarray(L0_ffn_w2_s, np.float32)),
        "sh1c": _scale_cols(np.asarray(L0_ffn_h1_s, np.float32)),
        "sh2c": _scale_cols(np.asarray(L0_ffn_h2_s, np.float32)),
        "s3c": _scale_cols(np.asarray(L0_ffn_w3_s, np.float32)),
    }
    sel16 = np.repeat(np.eye(16, dtype=np.float32), 128, axis=1)
    colsel = np.zeros((128, 256), np.float32)
    for h in range(16):
        colsel[:, h * 16 + h] = 1.0
    base["sel16"] = sel16.astype(ml_dtypes.bfloat16)
    base["colsel"] = colsel.astype(ml_dtypes.bfloat16)
    return base


def kernel(L0_attn_x, L0_attn_qkv_w1, L0_attn_qkv_w2, L0_attn_cproj_w1,
           L0_attn_cproj_w2, L0_ffn_w1_w, L0_ffn_w1_s, L0_ffn_w2_w,
           L0_ffn_w2_s, L0_ffn_h1_w, L0_ffn_h1_s, L0_ffn_h2_w, L0_ffn_h2_s,
           L0_ffn_w3_w, L0_ffn_w3_s):
    x_cores = _pack_x(np.asarray(L0_attn_x, dtype=np.float32))
    base = prep_inputs(L0_attn_qkv_w1, L0_attn_qkv_w2, L0_attn_cproj_w1,
                       L0_attn_cproj_w2, L0_ffn_w1_w, L0_ffn_w1_s,
                       L0_ffn_w2_w, L0_ffn_w2_s, L0_ffn_h1_w, L0_ffn_h1_s,
                       L0_ffn_h2_w, L0_ffn_h2_s, L0_ffn_w3_w, L0_ffn_w3_s)
    nc = build_program(NT)
    in_maps = [dict(base, x_in=x_cores[c]) for c in range(NCORE)]
    res = run_bass_kernel_spmd(nc, in_maps, core_ids=list(range(NCORE)))
    return _unpack_out([res.results[c]["o_out"] for c in range(NCORE)])


# version 15
# speedup vs baseline: 1.1039x; 1.0010x over previous
import numpy as np
import ml_dtypes
import concourse.bass as bass
import concourse.mybir as mybir
from concourse.bass_utils import run_bass_kernel_spmd

F32 = mybir.dt.float32
BF16 = mybir.dt.bfloat16
AF = mybir.ActivationFunctionType
ALU = mybir.AluOpType

D = 1024
DC = 8
H = 4096
HC = 32
T = 512
NT = 8
TPS = 4
NH = 16
DR = 256
RC = 2
NCORE = 8

EPS_LN = 1e-5
EPS_RMS = float(np.finfo(np.float32).eps)
NSLOT = 4
SLOTE = 4096

ENGINES = ("sync", "gpsimd", "vector", "scalar", "tensor")

WSPEC = {
    "w1": (8, 1, 4096), "w2": (8, 1, 4096),
    "h1": (8, 4, 4096), "h2": (8, 4, 4096),
    "w3": (2, 4, 4096),
    "qw1": (1, 1, 2048), "qw2": (24, 1, 512),
    "cw1": (1, 1, 4096), "cw2": (4, 1, 1024),
}

C_HALF, C_NEG1, C_ZERO, C_FRQ, C_I2PI, C_MAGIC, C_N2PI, C_PIH, \
    C_EPSLN, C_NHALF, C_I1024, C_I128, C_EPSRMS, C_PI, C_2PI, C_NPI = \
    range(16)
CONSTV = {C_HALF: 0.5, C_NEG1: -1.0, C_ZERO: 0.0,
          C_FRQ: -float(np.log(1024.0)) / 31.0,
          C_I2PI: float(1.0 / (2 * np.pi)), C_MAGIC: 12582912.0,
          C_N2PI: float(-2 * np.pi), C_PIH: float(np.pi / 2),
          C_EPSLN: EPS_LN, C_NHALF: -0.5, C_I1024: 1.0 / 1024.0,
          C_I128: 1.0 / 128.0, C_EPSRMS: EPS_RMS,
          C_PI: float(np.pi), C_2PI: float(2 * np.pi),
          C_NPI: float(-np.pi)}


class Sched:
    def __init__(self, nc):
        self.nc = nc
        self.q = {e: [] for e in ENGINES}
        self.sems = {}
        self.cnt = {}

    def sem(self, name):
        self.sems[name] = self.nc.alloc_semaphore(name)
        self.cnt[name] = 0
        return name

    def op(self, eng, fn, waits=(), inc=None, inc_by=1):
        waits = [(s, v) for (s, v) in waits if v and v > 0]
        mark = None
        if inc is not None:
            self.cnt[inc] += inc_by
            mark = self.cnt[inc]
        self.q[eng].append((waits, fn, inc, inc_by))
        return mark

    def drain(self, eng):
        self.op(eng, lambda e: e.drain())

    def emit(self):
        nc = self.nc
        with nc.Block() as block:
            for ename in ENGINES:
                def make_body(name):
                    def body(eng):
                        for waits, fn, inc, n in self.q[name]:
                            for (s, v) in waits:
                                eng.wait_ge(self.sems[s], v)
                            ins = fn(eng)
                            if inc is not None:
                                ins.then_inc(self.sems[inc], n)
                    return body
                getattr(block, ename)(make_body(ename))


def build_program(nt=NT):
    nc = bass.Bass("TRN2", target_bir_lowering=False)

    x_in = nc.dram_tensor("x_in", [nt, 128, DC * T], F32, kind="ExternalInput")
    o_out = nc.dram_tensor("o_out", [nt, 128, DC * T], F32,
                           kind="ExternalOutput")
    w_f32 = {k: nc.dram_tensor(f"{k}_f32", [ng, kq, 128, ch], F32,
                               kind="ExternalInput")
             for k, (ng, kq, ch) in WSPEC.items()}
    w_scr = {k: nc.dram_tensor(f"{k}_scr", [ng, kq, 128, ch], BF16)
             for k, (ng, kq, ch) in WSPEC.items()}
    sc_in = {k: nc.dram_tensor(k, [128, n], F32, kind="ExternalInput")
             for k, n in (("s1c", 32), ("s2c", 32), ("sh1c", 32),
                          ("sh2c", 32), ("s3c", 8))}
    sel16_in = nc.dram_tensor("sel16", [16, 2048], BF16, kind="ExternalInput")
    colsel_in = nc.dram_tensor("colsel", [128, 256], BF16, kind="ExternalInput")

    SB = nc.alloc_sbuf_tensor
    x_res = [SB(f"x{i}", [128, DC * T], F32) for i in range(2)]
    ring = SB("ring", [128, NSLOT * SLOTE], BF16)
    slot = [ring[:, i * SLOTE:(i + 1) * SLOTE] for i in range(NSLOT)]
    A = SB("arnA", [128, HC * T], BF16)
    B = SB("arnB", [128, HC * T], BF16)
    C = SB("arnC", [128, NH * T], BF16)
    lnb = SB("lnb", [128, DC * T], BF16)
    tmp1 = SB("tmp1", [128, 2 * 4 * T], BF16)
    hq = SB("hq", [128, RC * T], BF16)
    rtile = SB("rtile", [128, RC * T], BF16)
    sqs = SB("sqs", [128, 2 * T], BF16)
    fscr = SB("fscr", [128, 2 * T], F32)
    rowscr = SB("rowscr", [128, 3 * T], F32)
    abrow = SB("abrow", [128, 2 * T], BF16)
    bcb = SB("bcb", [128, 2 * T], BF16)
    rstd16 = SB("rstd16", [16, T], BF16)
    rstd16k = SB("rstd16k", [16, T], BF16)
    cos32 = SB("cos32", [32, 2048], F32)
    sin32 = SB("sin32", [32, 2048], F32)
    Ct = SB("Ct", [128, T], BF16)
    St = SB("St", [128, T], BF16)
    onesT = SB("onesT", [128, T // 2], BF16)
    i32s = SB("i32s", [32, 256], mybir.dt.int32)
    ones_col = SB("ones_col", [128, 1], BF16)
    ones_row = SB("ones_row", [1, 128], BF16)
    sel16 = SB("sel16t", [16, 2048], BF16)
    colsel = SB("colselt", [128, 256], BF16)
    scl = {k: SB(f"scl_{k}", [128, 32], F32) for k in ("s1", "s2", "sh1", "sh2")}
    scl3 = SB("scl_s3", [128, 8], F32)
    s3h = SB("s3h", [128, 8], F32)
    carry = SB("carry", [128, NH], F32)
    consts = SB("consts", [128, 16], F32)

    qb = A[:, 0:NH * T]
    kb = A[:, NH * T:2 * NH * T]
    swp = A[:, 0:NH * T]
    vb = B[:, 0:NH * T]
    qh = B[:, NH * T:2 * NH * T]
    kh = C[:, 0:NH * T]
    cums = B[:, 0:NH * T]
    xb8 = C[:, 0:DC * T]
    hin = C[:, DC * T:2 * DC * T]
    tabf = None

    tabf = x_res[0][0:32, 0:2048]
    psH = [nc.alloc_psum_tensor("psA", [128, 4 * T], F32),
           nc.alloc_psum_tensor("psB", [128, 4 * T], F32)]

    S = Sched(nc)
    cF = [S.sem(f"cF{i}") for i in range(NSLOT)]
    cX = [S.sem(f"cX{i}") for i in range(2)]
    cL = S.sem("cL")
    cO = S.sem("cO")
    cP = S.sem("cP")
    cSW = S.sem("cSW")
    cPE = S.sem("cPE")
    cSC = S.sem("cSC")
    cVE = S.sem("cVE")
    cPO = S.sem("cPO")

    bk = {
        "slot_free": [0] * NSLOT,
        "slot_fill": [0] * NSLOT,
        "slot_uses": [0] * NSLOT,
        "half_free": [(None, 0), (None, 0)],
        "x_out": [0, 0],
        "x_fill": [0, 0],
        "prep": {},
        "ci": 0,
        "rope_done": 0,
    }

    def cc(idx, p=128):
        return consts[0:p, idx:idx + 1]

    def fill(layer, ng, kq, elems, tile0):
        s = bk["ci"] % NSLOT
        bk["ci"] += 1
        waits = []
        if bk["slot_free"][s] > 0:
            waits.append((cPE, bk["slot_free"][s]))
        if tile0:
            waits.append((cP, bk["prep"][layer]))
        src = w_scr[layer][ng, kq]
        dst = slot[s] if elems == SLOTE else slot[s][:, 0:elems]
        m = S.op("sync", lambda e, d=dst, sr=src: e.dma_start(out=d, in_=sr),
                 waits=waits, inc=cF[s], inc_by=16)
        bk["slot_fill"][s] = m
        return s

    def half_wait(h):
        sem, m = bk["half_free"][h]
        return [(sem, m)] if (sem is not None and m > 0) else []

    def mm(out_ap, lhsT_ap, rhs_ap, start, stop, waits=(), inc=None):
        return S.op("tensor",
                    lambda e, o=out_ap, l=lhsT_ap, r=rhs_ap, st=start, sp=stop:
                    e.matmul(o, l, r, start=st, stop=sp),
                    waits=waits, inc=inc)

    def emit_xload(t):
        xi = t % 2
        waits = [(cO, bk["x_out"][xi])] if bk["x_out"][xi] > 0 else []
        if t == 0:
            waits.append((cSC, bk["tabs_sc"]))
        m = S.op("gpsimd", lambda e, ti=t, b=xi: e.dma_start(
            out=x_res[b][:, :], in_=x_in[ti]),
            waits=waits, inc=cX[xi], inc_by=16)
        bk["x_fill"][xi] = m

    def emit_ln(t, mode, xready=None):
        x = x_res[t % 2]
        xw = t // 2 + 1
        hf = bk.get("hin_free", 0)
        if xready is None:
            xready = [(cX[t % 2], 16 * xw)] * 2
        m_cast = None
        m_sq = None
        for hhalf in range(2):
            r0 = hhalf * 4 * T
            S.drain("vector")
            m_cast = S.op("vector", lambda e, r_=r0: e.tensor_copy(
                xb8[:, r_:r_ + 4 * T], x[:, r_:r_ + 4 * T]),
                waits=[xready[hhalf]], inc=cVE)
            S.drain("vector")
            m_sq = S.op("vector", lambda e, r_=r0: e.tensor_tensor(
                hin[:, r_:r_ + 4 * T], xb8[:, r_:r_ + 4 * T],
                xb8[:, r_:r_ + 4 * T], ALU.mult),
                waits=([(cPE, hf)] if (hhalf == 0 and hf > 0) else []),
                inc=cVE)
        w0 = half_wait(0)
        sq_marks = []
        for k in range(DC):
            mm(psH[0][0:1, 0:T], ones_col[:, 0:1], xb8[:, k * T:(k + 1) * T],
               start=(k == 0), stop=(k == DC - 1),
               waits=(w0 + [(cVE, m_cast - 2)]) if k == 0 else
                     ([(cVE, m_cast)] if k == 4 else []))
            msq = mm(psH[0][0:1, T:2 * T], ones_col[:, 0:1],
                     hin[:, k * T:(k + 1) * T],
                     start=(k == 0), stop=(k == DC - 1),
                     waits=[(cVE, m_sq - 2)] if k == 0 else
                           ([(cVE, m_sq)] if k == 4 else []),
                     inc=(cPE if k == DC - 1 else None))
            sq_marks.append(msq)
        bk["hin_free"] = sq_marks[-1]
        r_m = rowscr[0:1, 0:T]
        r_v = rowscr[0:1, T:2 * T]
        r_r = rowscr[0:1, 2 * T:3 * T]
        S.drain("vector")
        S.op("vector", lambda e: e.tensor_scalar(
            r_m, psH[0][0:1, 0:T], 1.0 / D, None, ALU.mult),
            waits=[(cPE, bk["hin_free"])])
        S.drain("vector")
        S.op("vector", lambda e: e.tensor_tensor(r_v, r_m, r_m, ALU.mult))
        S.drain("vector")
        m_var = S.op("vector", lambda e: e.scalar_tensor_tensor(
            r_v, psH[0][0:1, T:2 * T], 1.0 / D, r_v, ALU.mult, ALU.subtract),
            inc=cVE)
        S.op("scalar", lambda e: e.activation(r_r, r_v, AF.Ln,
                                              bias=cc(C_EPSLN, 1)),
             waits=[(cVE, m_var)])
        S.drain("scalar")
        m_ex = S.op("scalar", lambda e: e.activation(
            r_r, r_r, AF.Exp, scale=cc(C_NHALF, 1)), inc=cSC)
        S.drain("vector")
        S.op("vector", lambda e: e.scalar_tensor_tensor(
            rowscr[0:1, T:2 * T], r_m, -1.0, r_r, ALU.mult, ALU.mult),
            waits=[(cSC, m_ex)])
        S.drain("vector")
        S.op("vector", lambda e: e.tensor_copy(abrow[0:1, 0:T], r_r))
        S.drain("vector")
        m_ab = S.op("vector", lambda e: e.tensor_copy(
            abrow[0:1, T:2 * T], rowscr[0:1, T:2 * T]), inc=cVE)
        w1h = half_wait(1)
        mm(psH[1][:, 0:T], ones_row[0:1, :], abrow[0:1, 0:T], True, True,
           waits=w1h + [(cVE, m_ab)])
        m_bc = mm(psH[1][:, T:2 * T], ones_row[0:1, :], abrow[0:1, T:2 * T],
                  True, True, inc=cPE)
        m_ev = S.op("scalar", lambda e: e.activation(
            bcb[:, 0:2 * T], psH[1][:, 0:2 * T], AF.Copy),
            waits=[(cPE, m_bc)], inc=cSC)
        bk["half_free"][1] = (cSC, m_ev)
        hmarks = []
        for hh2 in range(2):
            r0 = hh2 * 4
            xv = xb8.rearrange("p (c tt) -> p c tt", c=DC)[:, r0:r0 + 4, :]
            lv = lnb.rearrange("p (c tt) -> p c tt", c=DC)[:, r0:r0 + 4, :]
            av = bcb[:, 0:T][:, None, :].broadcast_to([128, 4, T])
            bv = bcb[:, T:2 * T][:, None, :].broadcast_to([128, 4, T])
            S.drain("vector")
            S.op("vector", lambda e, l_=lv, x_=xv, a_=av: e.tensor_tensor(
                l_, x_, a_, ALU.mult),
                waits=[(cSC, m_ev)] if hh2 == 0 else [])
            S.drain("vector")
            m = S.op("vector", lambda e, l_=lv, b_=bv: e.tensor_tensor(
                l_, l_, b_, ALU.add), inc=cVE)
            hmarks.append(m)
        last_m = hmarks[-1]
        bk["half_free"][0] = (cVE, m_var)
        bk["lnb_hmarks"] = hmarks
        return last_m

    def mm_group(lay, g, h, kq_n, ch, rhs_buf, rhs_sem, rhs_marks, tile0,
                 extra_waits=()):
        nq = 8
        mbw = ch // nq // 128
        grp_mark = None
        for kq in range(kq_n):
            s = fill(lay, g, kq, ch, tile0)
            waits = [(cF[s], bk["slot_fill"][s])] + list(extra_waits if kq == 0
                                                         else [])
            if kq == 0:
                waits += half_wait(h)
            waited = 0
            for j in range(nq):
                k = kq * nq + j
                if rhs_marks is not None:
                    mkk = rhs_marks[min(k, len(rhs_marks) - 1)]
                    if mkk > waited:
                        waits.append((rhs_sem, mkk))
                        waited = mkk
                rhs = rhs_buf[:, k * T:(k + 1) * T]
                for ml in range(mbw):
                    is_chunk_last = (j == nq - 1 and ml == mbw - 1)
                    lhsT = slot[s][:, j * mbw * 128 + ml * 128:
                                   j * mbw * 128 + (ml + 1) * 128]
                    mk = mm(psH[h][:, ml * T:(ml + 1) * T], lhsT, rhs,
                            start=(kq == 0 and j == 0),
                            stop=(kq == kq_n - 1 and j == nq - 1),
                            waits=waits,
                            inc=(cPE if is_chunk_last else None))
                    waits = []
            bk["slot_free"][s] = mk
            grp_mark = mk
        return grp_mark

    def ffn_w1w2(t, tile0, hm):
        lnmarks = [hm[0]] * 4 + [hm[1]] * 4
        out_marks = []
        silu_mark = {}
        for q0 in range(0, 8, 2):
            for g in (q0, q0 + 1):
                h = g % 2
                gm = mm_group("w1", g, h, 1, 4096, lnb, cVE, lnmarks,
                              tile0)
                lastc = None
                for ml in range(4):
                    mch = g * 4 + ml
                    w_t = []
                    if ml == 0:
                        w_t = [(cPE, gm)]
                        tf = bk.get(f"tmp1_free{h}", 0)
                        if tf > 0:
                            w_t.append((cVE, tf))
                    lastc = S.op("scalar", lambda e, h_=h, ml_=ml, m_=mch:
                                 e.activation(
                        tmp1[:, (h_ * 4 + ml_) * T:(h_ * 4 + ml_ + 1) * T],
                        psH[h_][:, ml_ * T:(ml_ + 1) * T], AF.Silu,
                        scale=scl["s1"][:, m_:m_ + 1]),
                        waits=w_t, inc=(cSC if ml == 3 else None))
                silu_mark[g] = lastc
                bk["half_free"][h] = (cSC, lastc)
            for g in (q0, q0 + 1):
                h = g % 2
                gm = mm_group("w2", g, h, 1, 4096, lnb, cVE, lnmarks,
                              tile0)
                lastc = None
                for ml in range(4):
                    mch = g * 4 + ml
                    if ml == 0:
                        S.drain("vector")
                    lastc = S.op("vector", lambda e, h_=h, ml_=ml, m_=mch:
                                 e.scalar_tensor_tensor(
                        A[:, m_ * T:(m_ + 1) * T],
                        psH[h_][:, ml_ * T:(ml_ + 1) * T],
                        scl["s2"][:, m_:m_ + 1],
                        tmp1[:, (h_ * 4 + ml_) * T:(h_ * 4 + ml_ + 1) * T],
                        ALU.mult, ALU.mult),
                        waits=([(cPE, gm), (cSC, silu_mark[g])]
                               if ml == 0 else []),
                        inc=(cVE if ml == 3 else None))
                out_marks += [lastc] * 4
                bk["half_free"][h] = (cVE, lastc)
                bk[f"tmp1_free{h}"] = lastc
        return out_marks

    def ffn_layer(t, layer, rhs_buf, out_kind, out_buf, scale_sb,
                  rhs_sem, rhs_marks, act, tile0):
        ng, kq_n, ch = WSPEC[layer]
        mbw = ch // 8 // 128
        out_marks = []
        for g in range(ng):
            h = g % 2
            grp_mark = mm_group(layer, g, h, kq_n, ch, rhs_buf, rhs_sem,
                                rhs_marks, tile0)
            if out_kind == "gelu":
                lastc = None
                for ml in range(mbw):
                    mch = g * mbw + ml
                    lastc = S.op("scalar", lambda e, h_=h, ml_=ml, m_=mch:
                                 e.activation(
                        out_buf[:, m_ * T:(m_ + 1) * T],
                        psH[h_][:, ml_ * T:(ml_ + 1) * T], act,
                        scale=scale_sb[:, m_:m_ + 1]),
                        waits=[(cPE, grp_mark)] if ml == 0 else [],
                        inc=(cSC if ml == mbw - 1 else None))
                out_marks += [lastc] * mbw
                bk["half_free"][h] = (cSC, lastc)
            elif out_kind == "resid":
                lastc = None
                for ml in range(mbw):
                    mch = g * mbw + ml
                    if ml == 0:
                        S.drain("vector")
                    lastc = S.op("vector", lambda e, h_=h, ml_=ml, m_=mch:
                                 e.scalar_tensor_tensor(
                        x_res[t % 2][:, m_ * T:(m_ + 1) * T],
                        psH[h_][:, ml_ * T:(ml_ + 1) * T],
                        s3h[:, m_:m_ + 1],
                        x_res[t % 2][:, m_ * T:(m_ + 1) * T],
                        ALU.mult, ALU.add),
                        waits=[(cPE, grp_mark)] if ml == 0 else [],
                        inc=(cVE if ml == mbw - 1 else None))
                    out_marks.append(lastc)
                bk["half_free"][h] = (cVE, lastc)
        return out_marks

    def emit_ffn(t, tile0, lnb_mark):
        hm = bk.get("lnb_hmarks", [lnb_mark, lnb_mark])
        a_marks = ffn_w1w2(t, tile0, hm)
        b_marks = ffn_layer(t, "h1", A, "gelu", B, scl["sh1"],
                            cVE, a_marks, AF.Gelu, tile0)
        a2_marks = ffn_layer(t, "h2", B, "gelu", A, scl["sh2"],
                             cSC, b_marks, AF.Gelu, tile0)
        r_marks = ffn_layer(t, "w3", A, "resid", None, None,
                            cSC, a2_marks, None, tile0)
        return r_marks

    def emit_rms_qkv1(t, tile0, xlnb_mark):
        w0 = half_wait(0)
        hf = bk.get("hin_free", 0)
        S.drain("vector")
        m_sq = S.op("vector", lambda e: e.tensor_tensor(
            hin[:, :], lnb[:, :], lnb[:, :], ALU.mult),
            waits=([(cVE, xlnb_mark)] + ([(cPE, hf)] if hf > 0 else [])),
            inc=cVE)
        marks = []
        for k in range(DC):
            msq = mm(psH[0][0:1, 0:T], ones_col[:, 0:1],
                     hin[:, k * T:(k + 1) * T],
                     start=(k == 0), stop=(k == DC - 1),
                     waits=(w0 + [(cVE, m_sq)]) if k == 0 else [],
                     inc=(cPE if k == DC - 1 else None))
            marks.append(msq)
        bk["hin_free"] = marks[-1]
        r_r = rowscr[0:1, 0:T]
        S.op("scalar", lambda e: e.activation(
            r_r, psH[0][0:1, 0:T], AF.Ln, scale=cc(C_I1024, 1),
            bias=cc(C_EPSRMS, 1)), waits=[(cPE, marks[-1])])
        S.drain("scalar")
        m_ex = S.op("scalar", lambda e: e.activation(
            r_r, r_r, AF.Exp, scale=cc(C_NHALF, 1)), inc=cSC)
        S.drain("vector")
        m_ab = S.op("vector", lambda e: e.tensor_copy(abrow[0:1, 0:T], r_r),
                    waits=[(cSC, m_ex)], inc=cVE)
        w1h = half_wait(1)
        m_bc = mm(psH[1][:, 0:T], ones_row[0:1, :], abrow[0:1, 0:T], True,
                  True, waits=w1h + [(cVE, m_ab)], inc=cPE)
        bk["half_free"][0] = (cSC, m_ex)
        m_evr = S.op("scalar", lambda e: e.activation(
            bcb[:, 0:T], psH[1][:, 0:T], AF.Copy),
            waits=[(cPE, m_bc)], inc=cSC)
        bk["half_free"][1] = (cSC, m_evr)
        rv_ = bcb[:, 0:T][:, None, :].broadcast_to([128, DC, T])
        S.drain("vector")
        m_hin = S.op("vector", lambda e: e.tensor_tensor(
            hin.rearrange("p (c tt) -> p c tt", c=DC),
            lnb.rearrange("p (c tt) -> p c tt", c=DC), rv_, ALU.mult),
            waits=[(cSC, m_evr), (cPE, bk["hin_free"])], inc=cVE)
        s = fill("qw1", 0, 0, 2048, tile0)
        waits = [(cF[s], bk["slot_fill"][s]), (cVE, m_hin)] + half_wait(1)
        m_q1 = None
        for j in range(DC):
            for ml in range(2):
                m_q1 = mm(psH[1][:, ml * T:(ml + 1) * T],
                          slot[s][:, j * 256 + ml * 128:
                                  j * 256 + (ml + 1) * 128],
                          hin[:, j * T:(j + 1) * T],
                          start=(j == 0), stop=(j == DC - 1),
                          waits=waits,
                          inc=(cPE if (j == DC - 1 and ml == 1) else None))
                waits = []
        bk["slot_free"][s] = m_q1
        bk["hin_free"] = m_q1
        m_r = S.op("scalar", lambda e: e.activation(
            rtile[:, 0:2 * T], psH[1][:, 0:2 * T], AF.Relu),
            waits=[(cPE, m_q1)], inc=cSC)
        bk["half_free"][1] = (cSC, m_r)
        S.drain("vector")
        m_hq = S.op("vector", lambda e: e.tensor_tensor(
            hq[:, :], rtile[:, 0:2 * T], rtile[:, 0:2 * T], ALU.mult),
            waits=[(cSC, m_r)], inc=cVE)
        return m_hq

    def emit_attn_front(t, tile0, hq_mark):
        fsb = lnb[:, 0:4 * T]
        arena = {0: qb, 1: kb, 2: vb}
        ev_marks = []
        st_busy = {"q": 0, "k": 0}
        res = {}

        def stats_batch(which, j, src_, ev_need):
            sth = 0 if which == "q" else 1
            S.drain("vector")
            mq = S.op("vector", lambda e, j_=j, sb=src_: e.tensor_tensor(
                fsb, sb[:, j_ * 4 * T:(j_ + 1) * 4 * T],
                sb[:, j_ * 4 * T:(j_ + 1) * 4 * T], ALU.mult),
                waits=[(cSC, ev_need)] +
                      ([(cPE, st_busy[which])] if st_busy[which] else []),
                inc=cVE)
            m_st = None
            for i in range(4):
                hg = 4 * j + i
                m_st = mm(psH[sth][0:16, 2 * T:3 * T],
                          colsel[:, hg * 16:(hg + 1) * 16],
                          fsb[:, i * T:(i + 1) * T],
                          start=(hg == 0), stop=(hg == NH - 1),
                          waits=([(cVE, mq)] if i == 0 else []) +
                                (half_wait(sth) if hg == 0 else []),
                          inc=(cPE if i == 3 else None))
            st_busy[which] = m_st
            return m_st

        def rows_emit(which):
            sth = 0 if which == "q" else 1
            m_st = st_busy[which]
            S.op("scalar", lambda e, h_=sth: e.activation(
                rowscr[0:16, 0:T], psH[h_][0:16, 2 * T:3 * T], AF.Ln,
                scale=cc(C_I128, 16), bias=cc(C_EPSRMS, 16)),
                waits=[(cPE, m_st)])
            S.drain("scalar")
            m_ex = S.op("scalar", lambda e: e.activation(
                rowscr[0:16, 0:T], rowscr[0:16, 0:T], AF.Exp,
                scale=cc(C_NHALF, 16)), inc=cSC)
            S.drain("vector")
            rdst = rstd16[0:16, 0:T] if which == "q" else rstd16k[0:16, 0:T]
            m_r16 = S.op("vector", lambda e, r_=rdst: e.tensor_copy(
                r_, rowscr[0:16, 0:T]), waits=[(cSC, m_ex)], inc=cVE)
            return m_r16

        mult_state = {"bank": 0, "mark": {}, "last": {}}

        def mult_pair(which, hg, src_, dst, m_r16):
            sth = 0 if which == "q" else 1
            bnk = 2 + mult_state["bank"] % 2
            mult_state["bank"] += 1
            wts = [(cVE, m_r16)]
            pm = mult_state["mark"].get((sth, bnk))
            if pm:
                wts.append((cVE, pm))
            rsrc = rstd16[0:16, 0:T] if which == "q" else rstd16k[0:16, 0:T]
            m_bc = mm(psH[sth][:, bnk * T:(bnk + 1) * T],
                      sel16[0:16, hg * 128:(hg + 1) * 128],
                      rsrc, True, True, waits=wts, inc=cPE)
            S.drain("vector")
            m_mu = S.op("vector", lambda e, g_=hg, sb=src_, db=dst, h_=sth,
                        b_=bnk: e.tensor_tensor(
                db[:, g_ * T:(g_ + 1) * T], sb[:, g_ * T:(g_ + 1) * T],
                psH[h_][:, b_ * T:(b_ + 1) * T], ALU.mult),
                waits=[(cPE, m_bc)], inc=cVE)
            mult_state["mark"][(sth, bnk)] = m_mu
            mult_state["last"][which] = m_mu
            return m_mu

        gm = [0, 0]
        pending = []
        for g in range(24):
            hh = g % 2
            s = fill("qw2", g, 0, 512, tile0)
            waits = [(cF[s], bk["slot_fill"][s])]
            if g == 0:
                waits.append((cVE, hq_mark))
            if g in (0, 1):
                waits += half_wait(hh)
            else:
                waits += [(cSC, ev_marks[g - 2])]
            m_g = None
            for j in range(RC):
                for ml in range(2):
                    m_g = mm(psH[hh][:, ml * T:(ml + 1) * T],
                             slot[s][:, j * 256 + ml * 128:
                                     j * 256 + (ml + 1) * 128],
                             hq[:, j * T:(j + 1) * T],
                             start=(j == 0), stop=(j == RC - 1),
                             waits=waits,
                             inc=(cPE if (j == RC - 1 and ml == 1) else None))
                    waits = []
            bk["slot_free"][s] = m_g
            gm[hh] = m_g
            part = arena[g // 8]
            off = (g % 8) * 2 * T
            m_ev = S.op("scalar", lambda e, h_=hh, p_=part, o_=off:
                        e.activation(p_[:, o_:o_ + 2 * T],
                                     psH[h_][:, 0:2 * T], AF.Copy),
                        waits=[(cPE, m_g)], inc=cSC)
            ev_marks.append(m_ev)
            if g in (1, 3, 5, 7):
                stats_batch("q", g // 2, qb, m_ev)
            if g in (9, 11, 13, 15):
                stats_batch("k", (g - 8) // 2, kb, m_ev)
            if g == 9:
                m_r16q = rows_emit("q")
                pending += [("q", h_, qb, qh, m_r16q) for h_ in range(NH)]
            if g == 17:
                m_r16k = rows_emit("k")
                pending += [("k", h_, kb, kh, m_r16k) for h_ in range(NH)]
            for _ in range(2):
                if pending:
                    mult_pair(*pending.pop(0))
        while pending:
            mult_pair(*pending.pop(0))
        res["mq_hat"] = mult_state["last"]["q"]
        res["mk_hat"] = mult_state["last"]["k"]
        bk["half_free"][0] = (cSC, ev_marks[-2] if len(ev_marks) >= 2 else 0)
        bk["half_free"][1] = (cSC, ev_marks[-1])
        bk["bank23_free"] = (cVE, res["mk_hat"])
        return res["mq_hat"], res["mk_hat"]

    def emit_cs_tiles(t):
        cosv, sinv = sin32, cos32
        p0 = (t % TPS) * T
        S.drain("vector")
        S.op("vector", lambda e: e.tensor_copy(Ct[0:32, :],
                                               cosv[0:32, p0:p0 + T]),
             waits=[(cPO, bk["tabs_po"]), (cSC, bk["tabs_sc"])])
        S.op("vector", lambda e: e.tensor_copy(Ct[64:96, :],
                                               cosv[0:32, p0:p0 + T]))
        S.op("vector", lambda e: e.tensor_copy(St[0:32, :],
                                               sinv[0:32, p0:p0 + T]))
        m3 = S.op("vector", lambda e: e.tensor_scalar(
            St[64:96, :], sinv[0:32, p0:p0 + T], -1.0, None, ALU.mult),
            inc=cVE)
        return m3

    def hv(buf, o, L):
        return buf.rearrange("p (h tt) -> p h tt", h=NH)[:, :, o:o + L]

    def emit_attn(t, tile0, mq_hat, mk_hat, cs_mark):
        HL = T // 2

        def rope(buf, o, mark_in, sw_wait):
            bv3 = buf.rearrange("p (h tt) -> p h tt", h=NH)
            sv3 = swp.rearrange("p (h tt) -> p h tt", h=NH)
            S.op("scalar", lambda e: e.dma_start(
                out=sv3[0:32, :, o:o + HL], in_=bv3[64:96, :, o:o + HL]),
                waits=[(cVE, mark_in), (cSW, S.cnt["cSW"])] + sw_wait,
                inc=cSW, inc_by=16)
            m_sw = S.op("scalar", lambda e: e.dma_start(
                out=sv3[64:96, :, o:o + HL], in_=bv3[0:32, :, o:o + HL]),
                inc=cSW, inc_by=16)
            bv = bv3[:, :, o:o + HL]
            sv = sv3[:, :, o:o + HL]
            cvw = Ct[:, o:o + HL][:, None, :].broadcast_to([128, NH, HL])
            svw = St[:, o:o + HL][:, None, :].broadcast_to([128, NH, HL])
            S.drain("vector")
            S.op("vector", lambda e: e.tensor_tensor(bv, bv, cvw, ALU.mult),
                 waits=[(cVE, cs_mark), (cSW, m_sw)])
            S.drain("vector")
            S.op("vector", lambda e: e.tensor_tensor(sv, sv, svw, ALU.mult))
            S.drain("vector")
            m = S.op("vector", lambda e: e.tensor_tensor(bv, bv, sv, ALU.add),
                     inc=cVE)
            return m

        m_y = [None, None]
        cp_state = {"hh": 0}
        sv3f = swp.rearrange("p (h tt) -> p h tt", h=NH)
        bv3kf = kh.rearrange("p (h tt) -> p h tt", h=NH)
        swl = bk.get("swp_last", 0)
        S.op("scalar", lambda e: e.dma_start(
            out=swp[0:32, :], in_=kh[64:96, :]),
            waits=[(cVE, mk_hat), (cSW, S.cnt["cSW"])] +
                  ([(cVE, swl)] if swl else []),
            inc=cSW, inc_by=16)
        m_kswap = S.op("scalar", lambda e: e.dma_start(
            out=swp[64:96, :], in_=kh[0:32, :]), inc=cSW, inc_by=16)
        for o, i in ((0, 0), (HL, 1)):
            bv3q = qh.rearrange("p (h tt) -> p h tt", h=NH)
            bv3k = kh.rearrange("p (h tt) -> p h tt", h=NH)
            sv3 = swp.rearrange("p (h tt) -> p h tt", h=NH)
            cvw = Ct[:, o:o + HL][:, None, :].broadcast_to([128, NH, HL])
            svw = St[:, o:o + HL][:, None, :].broadcast_to([128, NH, HL])

            def swap_dma(bv3, dep):
                d1 = sv3[0:32, :, o:o + HL]
                s1 = bv3[64:96, :, o:o + HL]
                d2 = sv3[64:96, :, o:o + HL]
                s2 = bv3[0:32, :, o:o + HL]
                S.op("scalar", lambda e, d_=d1, s_=s1: e.dma_start(
                    out=d_, in_=s_),
                    waits=[(cVE, dep), (cSW, S.cnt["cSW"])],
                    inc=cSW, inc_by=16)
                return S.op("scalar", lambda e, d_=d2, s_=s2: e.dma_start(
                    out=d_, in_=s_), inc=cSW, inc_by=16)

            def rope3(bv3, m_sw, extra=()):
                bv = bv3[:, :, o:o + HL]
                sv = sv3[:, :, o:o + HL]
                S.drain("vector")
                S.op("vector", lambda e, b=bv, c_=cvw: e.tensor_tensor(
                    b, b, c_, ALU.mult),
                    waits=[(cVE, cs_mark), (cSW, m_sw)] + list(extra))
                S.drain("vector")
                S.op("vector", lambda e, s_=sv, w_=svw: e.tensor_tensor(
                    s_, s_, w_, ALU.mult))
                S.drain("vector")
                return S.op("vector", lambda e, b=bv, s_=sv: e.tensor_tensor(
                    b, b, s_, ALU.add), inc=cVE)

            m_k = rope3(bv3k, m_kswap)
            m_sg = S.op("scalar", lambda e, o_=o: e.activation(
                hv(kh, o_, HL), hv(kh, o_, HL), AF.Sigmoid),
                waits=[(cVE, m_k)], inc=cSC)
            m_swq = swap_dma(bv3q, max(m_k, mq_hat))
            m_rq = rope3(bv3q, m_swq)
            S.drain("vector")
            m_ropeq = S.op("vector", lambda e, o_=o: e.tensor_scalar(
                hv(qh, o_, HL), hv(qh, o_, HL), cc(C_ZERO), None, ALU.max),
                inc=cVE)
            if i == 1:
                bk["swp_last"] = m_rq
            if i == 1 and cp_state.get("pending") is not None:
                emit_cproj_uv(t, cp_state.pop("pending"), cp_state)
            S.drain("vector")
            m_s = S.op("vector", lambda e, o_=o: e.tensor_tensor(
                hv(kh, o_, HL), hv(kh, o_, HL), hv(vb, o_, HL), ALU.mult),
                waits=[(cSC, m_sg)], inc=cVE)
            seq_start = (t % TPS) == 0
            S.drain("vector")
            for hg in range(NH):
                if i == 0:
                    init = 0.0 if seq_start else carry[:, hg:hg + 1]
                else:
                    init = cums[:, hg * T + HL - 1:hg * T + HL]
                S.op("vector", lambda e, g_=hg, ini=init, o_=o:
                     e.tensor_tensor_scan(
                    cums[:, g_ * T + o_:g_ * T + o_ + HL], onesT[:, :],
                    kh[:, g_ * T + o_:g_ * T + o_ + HL], ini,
                    ALU.mult, ALU.add),
                    waits=[(cVE, m_s)] if hg == 0 else [])
            S.drain("vector")
            if i == 1:
                cv = cums.rearrange("p (h tt) -> p h tt", h=NH)[:, :, T - 1:T]
                S.op("vector", lambda e: e.tensor_copy(
                    carry[:, 0:NH][:, :, None], cv))
                S.drain("vector")
            m_y[i] = S.op("vector", lambda e, o_=o: e.tensor_tensor(
                hv(qh, o_, HL), hv(qh, o_, HL), hv(cums, o_, HL), ALU.mult),
                inc=cVE)
            info = emit_cproj_mms(t, tile0, o, m_y[i], cp_state)
            if i == 0:
                cp_state["pending"] = info
            else:
                emit_cproj_uv(t, info, cp_state)
        return cp_state

    def emit_cproj_mms(t, tile0, o, y_mark, st):
        HL = T // 2
        s = fill("cw1", 0, 0, 4096, tile0)
        hA = st["hh"]
        st["hh"] ^= 1
        waits = [(cF[s], bk["slot_fill"][s]), (cVE, y_mark)] + half_wait(hA)
        b23 = bk.get("bank23_free")
        if b23:
            waits.append(b23)
            bk["bank23_free"] = None
        m_c1 = None
        for j in range(NH):
            for ml in range(2):
                m_c1 = mm(psH[hA][:, ml * T:ml * T + HL],
                          slot[s][:, j * 256 + ml * 128:
                                  j * 256 + (ml + 1) * 128],
                          qh[:, j * T + o:j * T + o + HL],
                          start=(j == 0), stop=(j == NH - 1), waits=waits,
                          inc=(cPE if (j == NH - 1 and ml == 1) else None))
                waits = []
        bk["slot_free"][s] = m_c1
        m_rt = S.op("scalar", lambda e, h_=hA, o_=o: e.activation(
            rtile.rearrange("p (c tt) -> p c tt", c=2)[:, :, o_:o_ + HL],
            psH[h_][:, 0:2 * T].rearrange(
                "p (c tt) -> p c tt", c=2)[:, :, 0:HL],
            AF.Copy), waits=[(cPE, m_c1)], inc=cSC)
        bk["half_free"][hA] = (cSC, m_rt)
        hB = st["hh"]
        st["hh"] ^= 1
        info = {"o": o, "grp": [], "halves": (hA, hB)}
        for ci in range(4):
            s2 = fill("cw2", ci, 0, 1024, tile0)
            hh = (hB, hB, hA, hA)[ci]
            col = 256 * (ci % 2)
            waits = [(cF[s2], bk["slot_fill"][s2]), (cSC, m_rt)]
            if ci % 2 == 0:
                waits += half_wait(hh)
            m_g = None
            for j in range(RC):
                for ml in range(4):
                    m_g = mm(psH[hh][:, ml * T + col:ml * T + col + HL],
                             slot[s2][:, j * 512 + ml * 128:
                                     j * 512 + (ml + 1) * 128],
                             rtile[:, j * T + o:j * T + o + HL],
                             start=(j == 0), stop=(j == RC - 1), waits=waits,
                             inc=(cPE if (j == RC - 1 and ml == 3) else None))
                    waits = []
            bk["slot_free"][s2] = m_g
            info["grp"].append((hh, col, m_g))
        info["hB_done"] = info["grp"][1][2]
        info["all_done"] = info["grp"][-1][2]
        return info

    def emit_cproj_uv(t, info, st):
        HL = T // 2
        o = info["o"]
        x = x_res[t % 2]
        hA, hB = info["halves"]
        lastc = None
        uv_marks = []
        for ci in range(4):
            hh, col, m_g = info["grp"][ci]
            gview = psH[hh].rearrange("p (c tt) -> p c tt", c=4)[
                :, 1::2, col:col + HL]
            uview = psH[hh].rearrange("p (c tt) -> p c tt", c=4)[
                :, 0::2, col:col + HL]
            sqv = sqs.rearrange("p (c tt) -> p c tt", c=2)[:, :, 0:HL]
            fsv = fscr.rearrange("p (c tt) -> p c tt", c=2)[:, :, 0:HL]
            w_sl = []
            if ci == 0:
                w_sl.append((cPE, info["hB_done"]))
            elif ci == 2:
                w_sl.append((cPE, info["all_done"]))
            uvf = bk.get("uv_free", 0)
            if uvf > 0:
                w_sl.append((cVE, uvf))
            m_sl = S.op("scalar", lambda e, g_=gview: e.activation(
                sqv, g_, AF.Silu), waits=w_sl, inc=cSC)
            S.drain("vector")
            S.op("vector", lambda e, u_=uview: e.tensor_tensor(
                fsv, u_, sqv, ALU.mult),
                waits=[(cSC, m_sl)] + ([(cVE, uvf)] if uvf > 0 else []))
            S.drain("vector")
            xv = x.rearrange("p (c tt) -> p c tt", c=DC)[
                :, 2 * ci:2 * ci + 2, o:o + HL]
            lastc = S.op("vector", lambda e, x_=xv: e.tensor_tensor(
                x_, x_, fsv, ALU.add), inc=cVE)
            bk["uv_free"] = lastc
            uv_marks.append(lastc)
            if ci == 1:
                bk["half_free"][hB] = (cVE, lastc)
        bk["half_free"][hA] = (cVE, lastc)
        st["last"] = lastc
        st["uv_marks"] = uv_marks

    def emit_out(t, done_mark):
        xi = t % 2
        m = S.op("scalar", lambda e, ti=t, b=xi: e.dma_start(
            out=o_out[ti], in_=x_res[b][:, :]),
            waits=[(cVE, done_mark), (cO, S.cnt["cO"])], inc=cO, inc_by=16)
        bk["x_out"][xi] = m

    def emit_startup():
        S.op("gpsimd", lambda e: e.memset(ones_col[:, :], 1.0))
        S.op("gpsimd", lambda e: e.memset(ones_row[:, :], 1.0))
        S.op("gpsimd", lambda e: e.memset(onesT[:, :], 1.0))
        S.op("gpsimd", lambda e: e.memset(carry[:, :], 0.0))
        for idx, v in CONSTV.items():
            S.op("gpsimd", lambda e, i=idx, vv=float(v): e.memset(
                consts[:, i:i + 1], vv))
        S.op("gpsimd", lambda e: e.memset(Ct[32:64, :], 1.0))
        S.op("gpsimd", lambda e: e.memset(Ct[64:128, :], 1.0))
        S.op("gpsimd", lambda e: e.memset(St[32:64, :], 0.0))
        S.op("gpsimd", lambda e: e.memset(St[64:128, :], 0.0))
        S.op("gpsimd", lambda e: e.iota(cos32[0:32, :], [[1, 2048]],
                                        channel_multiplier=0,
                                        allow_small_or_imprecise_dtypes=True))
        m_po = S.op("gpsimd", lambda e: e.iota(
            sin32[0:32, 0:1], [[1, 1]], channel_multiplier=1,
            allow_small_or_imprecise_dtypes=True), inc=cPO)
        bk["tabs_po"] = m_po
        nload = 0
        for dst, src in [(scl["s1"], sc_in["s1c"]), (scl["s2"], sc_in["s2c"]),
                         (scl["sh1"], sc_in["sh1c"]),
                         (scl["sh2"], sc_in["sh2c"]), (scl3, sc_in["s3c"]),
                         (sel16, sel16_in), (colsel, colsel_in)]:
            S.op("sync", lambda e, d=dst, s_=src: e.dma_start(out=d[:, :],
                                                              in_=s_[:, :]),
                 inc=cL, inc_by=16)
            nload += 1
        m_ld = 16 * nload
        S.drain("vector")
        S.op("vector", lambda e: e.tensor_scalar(
            s3h[:, :], scl3[:, :], cc(C_HALF), None, ALU.mult),
            waits=[(cL, m_ld), (cPO, m_po)])
        m_fr = S.op("scalar", lambda e: e.activation(
            rowscr[0:32, 0:1], sin32[0:32, 0:1], AF.Exp,
            scale=cc(C_FRQ, 32)), waits=[(cPO, m_po)], inc=cSC)
        S.drain("vector")
        S.op("vector", lambda e: e.tensor_scalar(
            cos32[0:32, :], cos32[0:32, :], rowscr[0:32, 0:1], None, ALU.mult),
            waits=[(cSC, m_fr)])
        S.drain("vector")
        S.op("vector", lambda e: e.tensor_scalar(
            sin32[0:32, :], cos32[0:32, :], cc(C_PIH, 32), None, ALU.add))
        xs = tabf[0:32, 0:256]
        ns = x_res[0][0:32, 2048:2304]
        for buf in (sin32, cos32):
            for pc in range(8):
                c0, c1 = pc * 256, (pc + 1) * 256
                S.drain("vector")
                S.op("vector", lambda e, b=buf, a0=c0, a1=c1: e.tensor_scalar(
                    xs, b[0:32, a0:a1], cc(C_I2PI, 32), cc(C_HALF, 32),
                    ALU.mult, ALU.add))
                S.drain("vector")
                S.op("vector", lambda e: e.tensor_copy(i32s[:, :], xs))
                S.drain("vector")
                S.op("vector", lambda e: e.tensor_copy(ns, i32s[:, :]))
                S.drain("vector")
                S.op("vector", lambda e: e.tensor_tensor(xs, ns, xs, ALU.is_gt))
                S.drain("vector")
                S.op("vector", lambda e: e.tensor_tensor(ns, ns, xs,
                                                         ALU.subtract))
                S.drain("vector")
                S.op("vector", lambda e, b=buf, a0=c0, a1=c1:
                     e.scalar_tensor_tensor(
                    b[0:32, a0:a1], ns, cc(C_N2PI, 32),
                    b[0:32, a0:a1], ALU.mult, ALU.add))
            S.drain("vector")
            S.op("vector", lambda e, b=buf: e.tensor_scalar(
                b[0:32, :], b[0:32, :], cc(C_PI, 32), None, ALU.min))
            S.drain("vector")
            S.op("vector", lambda e, b=buf: e.tensor_scalar(
                b[0:32, :], b[0:32, :], cc(C_NPI, 32), None, ALU.max))
        S.drain("vector")
        m_v = S.op("vector", lambda e: e.engine_nop(), inc=cVE)
        S.op("scalar", lambda e: e.activation(sin32[0:32, :], sin32[0:32, :],
                                              AF.Sin), waits=[(cVE, m_v)])
        S.drain("scalar")
        m_tab = S.op("scalar", lambda e: e.activation(
            cos32[0:32, :], cos32[0:32, :], AF.Sin), inc=cSC)
        bk["tabs_sc"] = m_tab
        order = ["w1", "w2", "h1", "h2", "w3", "qw1", "qw2", "cw1", "cw2"]
        for i, lay in enumerate(order):
            waits = [(cP, 16 * i)] if i > 0 else []
            m = S.op("gpsimd", lambda e, l=lay: e.dma_start(
                out=w_scr[l][:], in_=w_f32[l][:]),
                waits=waits, inc=cP, inc_by=16)
            bk["prep"][lay] = m

    emit_startup()
    for _e in ("vector", "scalar", "tensor"):
        S.op(_e, lambda e: e.nop(),
             waits=[(cPO, bk["tabs_po"]), (cL, 16 * 7)])
    emit_xload(0)
    for t in range(nt):
        tile0 = (t == 0)
        if t + 1 < nt:
            emit_xload(t + 1)
        m_ln1 = emit_ln(t, "ln")
        f1_marks = emit_ffn(t, tile0, m_ln1)
        m_ln2 = emit_ln(t, "ln2", xready=[(cVE, f1_marks[3]),
                                          (cVE, f1_marks[7])])
        m_cs = emit_cs_tiles(t)
        m_hq = emit_rms_qkv1(t, tile0, m_ln2)
        S.drain("vector")
        S.op("vector", lambda e, b=t % 2: e.tensor_tensor(
            x_res[b][:, :], x_res[b][:, :], lnb[:, :], ALU.add))
        m_qhat, m_khat = emit_attn_front(t, tile0, m_hq)
        cp_state_holder = emit_attn(t, tile0, m_qhat, m_khat, m_cs)
        uvm = cp_state_holder.get("uv_marks")
        m_ln3 = emit_ln(t, "ln", xready=[(cVE, uvm[1]), (cVE, uvm[3])])
        m_f2 = emit_ffn(t, tile0, m_ln3)
        emit_out(t, m_f2[-1])
    S.op("scalar", lambda e: e.nop(), waits=[(cO, 16 * nt)])
    S.emit()
    return nc


def _pack_x(x_full):
    cores = []
    for c in range(NCORE):
        toks = np.ascontiguousarray(x_full[2 * c:2 * c + 2]).reshape(
            2 * 2048, D)
        tiles = []
        for t in range(NT):
            blk = toks[t * T:(t + 1) * T]
            fm = np.ascontiguousarray(blk.T).reshape(DC, 128, T)
            fm = np.ascontiguousarray(fm.transpose(1, 0, 2)).reshape(
                128, DC * T)
            tiles.append(fm)
        cores.append(np.ascontiguousarray(np.stack(tiles)))
    return cores


def _unpack_out(tiles_by_core):
    out = np.empty((16, 2048, D), dtype=np.float32)
    for c in range(NCORE):
        toks = np.empty((2 * 2048, D), dtype=np.float32)
        for t in range(NT):
            fm = np.asarray(tiles_by_core[c][t]).reshape(
                128, DC, T).transpose(1, 0, 2)
            toks[t * T:(t + 1) * T] = fm.reshape(D, T).T
        out[2 * c:2 * c + 2] = toks.reshape(2, 2048, D)
    return out


def _block_w(w, MB, NQ):
    O, I = w.shape
    KC_, MC_ = I // 128, O // 128
    NG, KQ = MC_ // MB, KC_ // NQ
    lhsT = np.ascontiguousarray(w.T)
    a = lhsT.reshape(KQ, NQ, 128, NG, MB * 128)
    a = a.transpose(3, 0, 2, 1, 4)
    return np.ascontiguousarray(a.reshape(NG, KQ, 128, NQ * MB * 128))


def _scale_cols(s):
    n = s.shape[0] // 128
    return np.ascontiguousarray(s.reshape(n, 128).T)


def prep_inputs(L0_attn_qkv_w1, L0_attn_qkv_w2, L0_attn_cproj_w1,
                L0_attn_cproj_w2, L0_ffn_w1_w, L0_ffn_w1_s, L0_ffn_w2_w,
                L0_ffn_w2_s, L0_ffn_h1_w, L0_ffn_h1_s, L0_ffn_h2_w,
                L0_ffn_h2_s, L0_ffn_w3_w, L0_ffn_w3_s):
    qw2p = np.asarray(L0_attn_qkv_w2, dtype=np.float32)
    cw2 = np.asarray(L0_attn_cproj_w2, dtype=np.float32)
    perm2 = []
    for i in range(8):
        perm2 += [i, 8 + i]
    cw2p = cw2.reshape(16, 128, DR)[perm2].reshape(16 * 128, DR)

    base = {
        "w1_f32": _block_w(np.asarray(L0_ffn_w1_w, np.float32), 4, 8),
        "w2_f32": _block_w(np.asarray(L0_ffn_w2_w, np.float32), 4, 8),
        "h1_f32": _block_w(np.asarray(L0_ffn_h1_w, np.float32), 4, 8),
        "h2_f32": _block_w(np.asarray(L0_ffn_h2_w, np.float32), 4, 8),
        "w3_f32": _block_w(np.asarray(L0_ffn_w3_w, np.float32), 4, 8),
        "qw1_f32": _block_w(np.asarray(L0_attn_qkv_w1, np.float32), 2, 8),
        "qw2_f32": _block_w(qw2p, 2, 2),
        "cw1_f32": _block_w(np.asarray(L0_attn_cproj_w1, np.float32), 2, 16),
        "cw2_f32": _block_w(cw2p, 4, 2),
        "s1c": _scale_cols(np.asarray(L0_ffn_w1_s, np.float32)),
        "s2c": _scale_cols(np.asarray(L0_ffn_w2_s, np.float32)),
        "sh1c": _scale_cols(np.asarray(L0_ffn_h1_s, np.float32)),
        "sh2c": _scale_cols(np.asarray(L0_ffn_h2_s, np.float32)),
        "s3c": _scale_cols(np.asarray(L0_ffn_w3_s, np.float32)),
    }
    sel16 = np.repeat(np.eye(16, dtype=np.float32), 128, axis=1)
    colsel = np.zeros((128, 256), np.float32)
    for h in range(16):
        colsel[:, h * 16 + h] = 1.0
    base["sel16"] = sel16.astype(ml_dtypes.bfloat16)
    base["colsel"] = colsel.astype(ml_dtypes.bfloat16)
    return base


def kernel(L0_attn_x, L0_attn_qkv_w1, L0_attn_qkv_w2, L0_attn_cproj_w1,
           L0_attn_cproj_w2, L0_ffn_w1_w, L0_ffn_w1_s, L0_ffn_w2_w,
           L0_ffn_w2_s, L0_ffn_h1_w, L0_ffn_h1_s, L0_ffn_h2_w, L0_ffn_h2_s,
           L0_ffn_w3_w, L0_ffn_w3_s):
    x_cores = _pack_x(np.asarray(L0_attn_x, dtype=np.float32))
    base = prep_inputs(L0_attn_qkv_w1, L0_attn_qkv_w2, L0_attn_cproj_w1,
                       L0_attn_cproj_w2, L0_ffn_w1_w, L0_ffn_w1_s,
                       L0_ffn_w2_w, L0_ffn_w2_s, L0_ffn_h1_w, L0_ffn_h1_s,
                       L0_ffn_h2_w, L0_ffn_h2_s, L0_ffn_w3_w, L0_ffn_w3_s)
    nc = build_program(NT)
    in_maps = [dict(base, x_in=x_cores[c]) for c in range(NCORE)]
    res = run_bass_kernel_spmd(nc, in_maps, core_ids=list(range(NCORE)))
    return _unpack_out([res.results[c]["o_out"] for c in range(NCORE)])
